# version 08
# baseline (speedup 1.0000x reference)
import numpy as np
import ml_dtypes
from contextlib import ExitStack

import os as _os
import concourse.bass as bass
import concourse.mybir as mybir
from concourse.bass_utils import run_bass_kernel_spmd

F32 = mybir.dt.float32
BF16 = mybir.dt.bfloat16
ALU = mybir.AluOpType
ACT = mybir.ActivationFunctionType
AXX = mybir.AxisListType.X

L = 4096
D = 1024
NT = L // 128
NF = D // 128
H = 16
HD = 64
MLP = 4096
DIN = 2048
NCC = DIN // 128
DST = 16
DTR = 64
TC = 512
NCH = L // TC

PHASES = "A"
DEBUG = False


class Prog:

    ENG = ("sync", "tensor", "vector", "scalar", "gpsimd")
    NDMASEM = 20

    def __init__(self, nc, ctx):
        self.nc = nc
        self.sems = {e: ctx.enter_context(nc.semaphore("sem_" + e))
                     for e in self.ENG}
        for q in ("sq", "gq"):
            for i in range(self.NDMASEM):
                self.sems[f"{q}:{i}"] = ctx.enter_context(
                    nc.semaphore(f"sem_{q}_{i}"))
        self.plan = {e: [] for e in self.ENG}
        self.count = {k: 0 for k in self.sems}
        self.last_w = {}
        self.readers = {}
        self.watermark = {e: {} for e in self.ENG}
        self.region_sem = {"sq": {}, "gq": {}}
        self.sem_rr = {"sq": 0, "gq": 0}

    def _dma_sem(self, queue, region):
        m = self.region_sem[queue]
        if region not in m:
            m[region] = f"{queue}:{self.sem_rr[queue] % self.NDMASEM}"
            self.sem_rr[queue] += 1
        return m[region]

    def _waits_for(self, eng, reads, writes):
        waits = {}

        def need(src, val):
            if src == eng == "tensor":
                return
            if val <= 0:
                return
            wm = self.watermark[eng]
            if val > wm.get(src, 0) and val > waits.get(src, 0):
                waits[src] = val

        for r in reads:
            w = self.last_w.get(r)
            if w:
                need(*w)
        for wr in writes:
            w = self.last_w.get(wr)
            if w:
                need(*w)
            for e2, v in self.readers.get(wr, {}).items():
                need(e2, v)
        for sk, v in waits.items():
            self.watermark[eng][sk] = v
        return waits

    def emit(self, eng, fn, reads=(), writes=(), tag=""):
        waits = self._waits_for(eng, reads, writes)
        self.count[eng] += 1
        self.plan[eng].append((waits, fn, (eng, 1), tag))
        for r in reads:
            self.readers.setdefault(r, {})[eng] = self.count[eng]
        for wr in writes:
            self.last_w[wr] = (eng, self.count[eng])
            self.readers[wr] = {}

    def dma(self, queue, out_ap, in_ap, reads=(), writes=(), tag="",
            transpose=False, slow=False):
        eng = "sync" if queue == "sq" else "gpsimd"
        waits = self._waits_for(eng, reads, writes)
        primary = (list(writes) + list(reads) + [f"_anon_{queue}"])[0]
        sk = self._dma_sem(queue, primary)
        self.count[sk] += 16
        if transpose:
            def fn(e, o=out_ap, i=in_ap):
                return e.dma_start_transpose(o, i)
        elif slow:
            def fn(e, o=out_ap, i=in_ap):
                return e.dma_start(out=o, in_=i, allow_slow_non_contiguous=True)
        else:
            def fn(e, o=out_ap, i=in_ap):
                return e.dma_start(out=o, in_=i)
        self.plan[eng].append((waits, fn, (sk, 16), tag))
        for r in reads:
            self.readers.setdefault(r, {})[sk] = self.count[sk]
        for wr in writes:
            self.last_w[wr] = (sk, self.count[sk])
            self.readers[wr] = {}

    def fence(self):
        snap = dict(self.count)
        for eng in self.ENG:
            waits = {}
            for sk, v in snap.items():
                if sk == eng:
                    continue
                if v > self.watermark[eng].get(sk, 0):
                    waits[sk] = v
                    self.watermark[eng][sk] = v
            self.plan[eng].append((waits, None, None, "fence"))
        self.region_sem = {"sq": {}, "gq": {}}

    def truncate_at_fence(self, k):
        for eng in self.ENG:
            n = 0
            for idx, (w, fn, inc, tag) in enumerate(self.plan[eng]):
                if tag == "fence":
                    n += 1
                    if n == k:
                        self.plan[eng] = self.plan[eng][:idx + 1]
                        break

    def finalize(self, block):
        for eng in self.ENG:
            steps = self.plan[eng]

            def make_body(eng=eng, steps=steps):
                def body(e):
                    for waits, fn, inc, tag in steps:
                        for src, val in waits.items():
                            e.wait_ge(self.sems[src], val)
                        if fn is None:
                            continue
                        inst = fn(e)
                        inst.then_inc(self.sems[inc[0]], inc[1])
                return body
            getattr(block, eng)(make_body())


def bcast(ap, reps):
    return ap.rearrange("p n -> p n ()").broadcast_to(
        (ap.shape[0], ap.shape[1], reps))


def build(phases=None, debug=None, cut=None):
    phases = PHASES if phases is None else phases
    debug = DEBUG if debug is None else debug
    nc = bass.Bass("TRN2", target_bir_lowering=False)
    ctx = ExitStack()
    p = Prog(nc, ctx)
    skind = "ExternalOutput" if debug else "Internal"

    def din(name, shape, dt=F32):
        return nc.dram_tensor(name, shape, dt, kind="ExternalInput").ap()

    W = {}
    for n, sh in [
        ("x", [L, D]),
        ("Wq", [D, D]), ("bq", [D]), ("gq", [D]), ("betaq", [D]),
        ("Wk", [D, D]), ("bk", [D]), ("gk", [D]), ("betak", [D]),
        ("Wv", [D, D]), ("bv", [D]), ("gv", [D]), ("betav", [D]),
        ("Wfq", [D, D]), ("bfq", [D]), ("Wfk", [D, D]), ("bfk", [D]),
        ("ga", [D]), ("ba", [D]),
        ("W1", [D, MLP]), ("b1", [MLP]), ("W2", [MLP, D]), ("b2", [D]),
        ("norm_w", [D]), ("in_proj_w", [2 * DIN, D]),
        ("conv_w", [DIN, 1, 4]), ("conv_b", [DIN]),
        ("x_proj_w", [DTR + 2 * DST, DIN]), ("dt_proj_w", [DIN, DTR]),
        ("dt_proj_b", [DIN]), ("A_log", [DIN, DST]), ("Dv", [DIN]),
        ("out_proj_w", [D, DIN]),
        ("fw", [D, D]), ("fgamma", [D]), ("fbeta", [D]),
    ]:
        W[n] = din(n, sh)
    c_eye_f = din("c_eye_f", [128, 128])
    c_eye_b = din("c_eye_b", [128, 128], BF16)
    c_ones_col_b = din("c_ones_col_b", [128, 1], BF16)
    c_ones_row_b = din("c_ones_row_b", [1, 128], BF16)

    out_d = nc.dram_tensor("out", [L, D], F32, kind="ExternalOutput").ap()

    def dscr(name, shape, dt):
        return nc.dram_tensor(name, shape, dt, kind=skind).ap()

    xres = dscr("xres", [L, D], F32)
    zfm = dscr("zfm", [NF, 128, L], BF16)
    qfm = dscr("qfm", [NF, 128, L], BF16)
    kfm = dscr("kfm", [NF, 128, L], BF16)
    vtm = dscr("vtm", [NT, 128, D], BF16)
    pqfm = dscr("pqfm", [NF, 128, L], BF16)
    pkwtm = dscr("pkwtm", [NT, 128, D], BF16)
    afm = dscr("afm", [NF, 128, L], BF16)
    g1fm = dscr("g1fm", [MLP // 128, 128, L], BF16)
    rowb = nc.dram_tensor("rowb", [8, 4096], F32, kind=skind).ap()
    xnb = nc.dram_tensor("xnb", [L, D], F32, kind=skind).ap()
    ipT_d = nc.dram_tensor("ipT_d", [NF, 128, 2 * DIN], BF16, kind=skind).ap()
    opT_d = nc.dram_tensor("opT_d", [NCC, 128, D], BF16, kind=skind).ap()
    bc_d = nc.dram_tensor("bc_d", [2 * DST, TC], F32, kind=skind).ap()
    wqT_d = nc.dram_tensor("wqT_d", [NF, 128, 1024], BF16, kind=skind).ap()
    cc_in = nc.dram_tensor("cc_in", [1, 8], F32, kind="Internal").ap()
    cc_out = nc.dram_tensor("cc_out", [1, 8], F32, kind="Internal").ap()

    sb = lambda n, sh, dt=F32: ctx.enter_context(nc.sbuf_tensor(n, sh, dt))
    ones_col_b = sb("ones_col_b", [128, 1], BF16)
    ones_row_b = sb("ones_row_b", [1, 128], BF16)
    eye_f = sb("eye_f", [128, 128])
    eye_b = sb("eye_b", [128, 128], BF16)
    p.dma("sq", ones_col_b[:, :], c_ones_col_b[:, :], writes=["ones_col_b"])
    p.dma("sq", ones_row_b[:, :], c_ones_row_b[:, :], writes=["ones_row_b"])
    p.dma("sq", eye_f[:, :], c_eye_f[:, :], writes=["eye_f"])
    p.dma("sq", eye_b[:, :], c_eye_b[:, :], writes=["eye_b"])

    P0 = ctx.enter_context(nc.psum_tensor("P0", [128, 1024], F32))
    P1 = ctx.enter_context(nc.psum_tensor("P1", [128, 1024], F32))
    P2 = ctx.enter_context(nc.psum_tensor("P2", [128, 512], F32))
    P3 = ctx.enter_context(nc.psum_tensor("P3", [128, 512], F32))
    PR = ctx.enter_context(nc.psum_tensor("PR", [1, 1024], F32))

    st_sum = sb("st_sum", [128, NT])
    st_ssq = sb("st_ssq", [128, NT])
    st_rstd = sb("st_rstd", [128, NT])
    st_nmr = sb("st_nmr", [128, NT])
    st_tmp = sb("st_tmp", [128, NT])
    sq_scr = sb("sq_scr", [128, 1024])

    def act(out_ap, in_ap, func, reads, writes, bias=0.0, scale=1.0, accum=None,
            accw=()):
        def fn(e, o=out_ap, i=in_ap, b=bias, s=scale, a=accum, f=func):
            return e.activation(o, i, f, bias=b, scale=s, accum_out=a)
        p.emit("scalar", fn, reads=reads, writes=list(writes) + list(accw))

    def vtt(op, out_ap, a_ap, b_ap, reads, writes):
        def fn(e, o=out_ap, a=a_ap, b=b_ap, op=op):
            return e.tensor_tensor(o, a, b, op=op)
        p.emit("vector", fn, reads=reads, writes=writes)

    def vts(out_ap, in_ap, s1, s2, op0, op1, reads, writes):
        def fn(e, o=out_ap, i=in_ap, s1=s1, s2=s2, op0=op0, op1=op1):
            return e.tensor_scalar(o, i, s1, s2, op0=op0, op1=op1)
        p.emit("vector", fn, reads=reads, writes=writes)

    def vstt(out_ap, in0_ap, scalar, in1_ap, op0, op1, reads, writes):
        def fn(e, o=out_ap, i0=in0_ap, s=scalar, i1=in1_ap, op0=op0, op1=op1):
            return e.scalar_tensor_tensor(o, i0, s, i1, op0=op0, op1=op1)
        p.emit("vector", fn, reads=reads, writes=writes)

    def vcopy(out_ap, in_ap, reads, writes):
        def fn(e, o=out_ap, i=in_ap):
            return e.tensor_copy(o, i)
        p.emit("vector", fn, reads=reads, writes=writes)

    def vred(out_ap, in_ap, op, reads, writes, absval=None):
        def fn(e, o=out_ap, i=in_ap, op=op, av=absval):
            return e.tensor_reduce(o, i, axis=AXX, op=op, apply_absolute_value=av)
        p.emit("vector", fn, reads=reads, writes=writes)

    def vrecip(out_ap, in_ap, reads, writes):
        def fn(e, o=out_ap, i=in_ap):
            return e.reciprocal(o, i)
        p.emit("vector", fn, reads=reads, writes=writes)

    def vmemset(ap, val, writes):
        def fn(e, a=ap, v=val):
            return e.memset(a, v)
        p.emit("vector", fn, writes=writes)

    def vscan(out_ap, a_ap, b_ap, init, reads, writes):
        def fn(e, o=out_ap, a=a_ap, b=b_ap, i=init):
            return e.tensor_tensor_scan(o, a, b, i, op0=ALU.mult, op1=ALU.add)
        p.emit("vector", fn, reads=reads, writes=writes)

    def mm(out_ap, lhsT_ap, rhs_ap, start, stop, reads, writes, tile_position=None):
        def fn(e, o=out_ap, l=lhsT_ap, r=rhs_ap, st=start, sp=stop,
               tp=tile_position):
            return e.matmul(o, l, r, start=st, stop=sp, tile_position=tp)
        p.emit("tensor", fn, reads=reads, writes=writes)

    def transp(out_ap, in_ap, ident, reads, writes):
        def fn(e, o=out_ap, i=in_ap, id_=ident):
            return e.transpose(o, i, id_)
        p.emit("tensor", fn, reads=reads, writes=writes)

    def rstd_batch(n, sl, eps=1e-5):
        par = (sl.start or 0) % 2
        kS, kQ = f"st_sum{par}", f"st_ssq{par}"
        kR, kN, kT = f"st_rstd{par}", f"st_nmr{par}", f"st_tmp{par}"
        vts(st_tmp[:, sl], st_sum[:, sl], 1.0 / n, None, ALU.mult, ALU.bypass,
            [kS], [kT])
        act(st_nmr[:, sl], st_tmp[:, sl], ACT.Square, [kT], [kN])
        vstt(st_ssq[:, sl], st_ssq[:, sl], 1.0 / n, st_nmr[:, sl], ALU.mult,
             ALU.subtract, [kQ, kN], [kQ])
        vts(st_ssq[:, sl], st_ssq[:, sl], eps, None, ALU.add, ALU.bypass,
            [kQ], [kQ])
        vrecip(st_rstd[:, sl], st_ssq[:, sl], [kQ], [kR])
        act(st_rstd[:, sl], st_rstd[:, sl], ACT.Sqrt, [kR], [kR])
        act(st_nmr[:, sl], st_rstd[:, sl], ACT.Square, [kR], [kN])
        vtt(ALU.mult, st_nmr[:, sl], st_nmr[:, sl], st_ssq[:, sl],
            [kN, kQ], [kN])
        vts(st_nmr[:, sl], st_nmr[:, sl], -0.5, 1.5, ALU.mult, ALU.add,
            [kN], [kN])
        vtt(ALU.mult, st_rstd[:, sl], st_rstd[:, sl], st_nmr[:, sl],
            [kR, kN], [kR])
        vtt(ALU.mult, st_nmr[:, sl], st_tmp[:, sl], st_rstd[:, sl],
            [kT, kR], [kN])
        vts(st_nmr[:, sl], st_nmr[:, sl], -1.0, None, ALU.mult, ALU.bypass,
            [kN], [kN])

    def load_rep(actx, name, dram_vec):
        n = dram_vec.shape[0]
        t = actx.enter_context(nc.sbuf_tensor("rep_" + name, [128, n], F32))
        p.dma("sq", t[:, :],
              dram_vec.rearrange("(o n) -> o n", o=1).partition_broadcast(128),
              writes=["rep_" + name])
        return t

    def load_w_bf16(wctx, name, dram_w, rows, cols, stage, stage_key):
        tiles = []
        scw = stage.shape[1]
        for k in range(rows // 128):
            t = wctx.enter_context(
                nc.sbuf_tensor(f"wb_{name}_{k}", [128, cols], BF16))
            tiles.append(t)
            for c0 in range(0, cols, scw):
                c1 = min(cols, c0 + scw)
                p.dma("sq", stage[:, :c1 - c0],
                      dram_w[k * 128:(k + 1) * 128, c0:c1],
                      writes=[stage_key], tag="wload")
                vcopy(t[:, c0:c1], stage[:, :c1 - c0], [stage_key],
                      [f"wb_{name}_{k}"])
        return tiles

    def transpose_tm_tile(src_bf_ap, src_key, dst_dram, i, ev, ev_key):
        p2b = P2[:, :512].bitcast(BF16)
        for c in range(NF):
            transp(p2b[:, c * 128:(c + 1) * 128],
                   src_bf_ap[:, c * 128:(c + 1) * 128], eye_b[:, :],
                   [src_key, "eye_b"], ["P2"])
        act(ev[:, :1024], p2b[:, :1024], ACT.Copy, ["P2"], [ev_key])
        p.dma("gq", dst_dram[:, :, i * 128:(i + 1) * 128]
              .rearrange("c p t -> p c t"),
              ev[:, :1024].rearrange("p (c t) -> p c t", c=NF),
              reads=[ev_key], tag="fmst")

    taps = {}

    if "A" in phases:
        actx = ctx.enter_context(ExitStack())
        asb = lambda n, sh, dt=F32: actx.enter_context(nc.sbuf_tensor(n, sh, dt))

        xt = [asb(f"a0x{b}", [128, D]) for b in range(2)]
        zt = [asb(f"a0z{b}", [128, D]) for b in range(2)]
        ztb = asb("a0ztb", [128, D], BF16)
        ev = asb("a0ev", [128, 1024], BF16)
        for i in range(NT):
            b = i % 2
            kx, kz = f"a0x{b}", f"a0z{b}"
            p.dma("sq", xt[b][:, :], W["x"][i * 128:(i + 1) * 128, :], writes=[kx])
            act(sq_scr[:, :D], xt[b][:, :], ACT.Square, [kx], ["sq_scr"],
                accum=st_ssq[:, i:i + 1], accw=[f"st_ssq{i % 2}"])
            act(zt[b][:, :], xt[b][:, :], ACT.Copy, [kx], [kz],
                accum=st_sum[:, i:i + 1], accw=[f"st_sum{i % 2}"])
            rstd_batch(D, slice(i, i + 1))
            act(zt[b][:, :], xt[b][:, :], ACT.Identity,
                [kx, f"st_rstd{i % 2}", f"st_nmr{i % 2}"],
                [kz], bias=st_nmr[:, i:i + 1], scale=st_rstd[:, i:i + 1])
            vtt(ALU.add, xt[b][:, :], xt[b][:, :], zt[b][:, :], [kx, kz], [kx])
            p.dma("sq", xres[i * 128:(i + 1) * 128, :], xt[b][:, :], reads=[kx])
            vcopy(ztb[:, :], zt[b][:, :], [kz], ["a0ztb"])
            transpose_tm_tile(ztb[:, :], "a0ztb", zfm, i, ev, "a0ev")
        p.fence()

        fmincat = [asb(f"fmincat{b}", [128, NF * TC], BF16) for b in range(2)]
        fmin = [[fmincat[b][:, c * TC:(c + 1) * TC] for c in range(NF)]
                for b in range(2)]

        def load_fm_chunk(src_dram, j, b):
            p.dma("sq", fmincat[b][:, :].rearrange("p (c t) -> p c t", c=NF),
                  src_dram[:, :, j * TC:(j + 1) * TC].rearrange("c p t -> p c t"),
                  writes=[f"fmin{b}"], tag="fmld")

        wstage = asb("wstage", [128, 1024])
        brow_f = asb("brow_f", [1, 1024])
        brow_b = asb("brow_b", [1, 1024], BF16)
        ev_t = asb("ev_t", [128, 1024])
        ev_s = asb("ev_s", [128, 1024])
        ev_b = asb("ev_b", [128, 1024], BF16)
        evT = asb("evT", [128, 1024], BF16)
        qprobe = asb("qprobe", [1, 1024])
        qpb_f = asb("qpb_f", [128, 16])
        qpb = [asb(f"qpb{c}", [128, 16], BF16) for c in range(NF)]
        logit = asb("logit", [16, L])
        smax = asb("smax", [16, 1])
        srec = asb("srec", [16, 1])
        ssum = asb("ssum", [16, 1])
        sc0 = asb("sc0", [16, 1])
        negm = asb("negm", [16, 1])
        sc_tm = asb("sc_tm", [128, NT * 16])
        bfq_sb = asb("bfq_sb", [128, NF])
        pq_f = asb("pq_f", [128, TC])
        pq_b = asb("pq_b", [128, TC], BF16)
        pk_f = asb("pk_f", [128, 1024])
        pkw_t = asb("pkw_t", [128, 1024], BF16)
        ksum_row = asb("ksum_row", [1, 1024])
        ksum_fm_f = asb("ksum_fm_f", [128, NF])
        ksum_fm = asb("ksum_fm", [128, 64], BF16)
        sc0_fm = asb("sc0_fm", [128, NF])
        bfk_fm = asb("bfk_fm", [128, NF])
        pk0_fm = asb("pk0_fm", [128, NF])
        kv_sb = asb("kv_sb", [128, 1024], BF16)
        kvin = [asb(f"kvin{b}", [128, 1024], BF16) for b in range(2)]
        vvin = [asb(f"vvin{b}", [128, 1024], BF16) for b in range(2)]
        pqin = [asb(f"pqin{b}", [128, 1024], BF16) for b in range(2)]
        den_t = asb("den_t", [128, 16])
        att_t = asb("att_t", [128, 1024])
        a_t = asb("a_t", [128, 1024])
        a_tb = asb("a_tb", [128, 1024], BF16)
        x2 = [asb(f"x2{b}", [128, D]) for b in range(2)]

        def proj_tm(dst_dram, Wname, bias_name, g_name, bt_name,
                    probe_psum=False, do_transpose=True):
            with ExitStack() as wctx:
                wt = load_w_bf16(wctx, Wname, W[Wname], D, 1024, wstage, "wstage")
                g_rep = load_rep(wctx, "g" + Wname, W[g_name])
                bt_rep = load_rep(wctx, "bt" + Wname, W[bt_name])
                p.dma("sq", brow_f[:, :],
                      W[bias_name].rearrange("(o n) -> o n", o=1), writes=["brow_f"])
                vcopy(brow_b[:, :], brow_f[:, :], ["brow_f"], ["brow_b"])

                def mm_part(i):
                    j, tb = i // 4, i % 4
                    if tb == 0:
                        load_fm_chunk(zfm, j, j % 2)
                    bj = j % 2
                    ps, psk = (P0, "P0") if i % 2 == 0 else (P1, "P1")
                    t0 = tb * 128
                    for half in range(2):
                        h0 = half * 512
                        for c in range(NF):
                            mm(ps[:, h0:h0 + 512], fmin[bj][c][:, t0:t0 + 128],
                               wt[c][:, h0:h0 + 512], start=(c == 0), stop=False,
                               reads=[f"fmin{bj}", f"wb_{Wname}_{c}"],
                               writes=[psk])
                        mm(ps[:, h0:h0 + 512], ones_row_b[:, :],
                           brow_b[:, h0:h0 + 512], start=False, stop=True,
                           reads=["ones_row_b", "brow_b"], writes=[psk])

                def epi_part(i):
                    ps, psk = (P0, "P0") if i % 2 == 0 else (P1, "P1")
                    act(ev_t[:, :], ps[:, :1024], ACT.Copy, [psk], ["ev_t"],
                        accum=st_sum[:, i:i + 1], accw=[f"st_sum{i % 2}"])
                    act(ev_s[:, :], ev_t[:, :], ACT.Square, ["ev_t"],
                        ["ev_s"], accum=st_ssq[:, i:i + 1], accw=[f"st_ssq{i % 2}"])
                    rstd_batch(D, slice(i, i + 1))
                    act(ev_t[:, :], ev_t[:, :], ACT.Identity,
                        ["ev_t", f"st_rstd{i % 2}", f"st_nmr{i % 2}"], ["ev_t"],
                        bias=st_nmr[:, i:i + 1], scale=st_rstd[:, i:i + 1])
                    vtt(ALU.mult, ev_s[:, :], ev_t[:, :], g_rep[:, :],
                        ["ev_t", "rep_g" + Wname], ["ev_s"])
                    vtt(ALU.add, ev_b[:, :], ev_s[:, :], bt_rep[:, :],
                        ["ev_s", "rep_bt" + Wname], ["ev_b"])
                    if probe_psum:
                        for half in range(2):
                            h0 = half * 512
                            mm(PR[:1, h0:h0 + 512], ones_col_b[:, :],
                               ev_b[:, h0:h0 + 512],
                               start=(i == 0), stop=(i == NT - 1),
                               reads=["ones_col_b", "ev_b"], writes=["PR"])
                    if do_transpose:
                        transpose_tm_tile(ev_b[:, :], "ev_b", dst_dram, i,
                                          evT, "evT")
                    else:
                        vcopy(evT[:, :], ev_b[:, :], ["ev_b"], ["evT"])
                        p.dma("gq", dst_dram[i, :, :], evT[:, :],
                              reads=["evT"], tag="vtm_st")

                for i in range(NT):
                    mm_part(i)
                    if i > 0:
                        epi_part(i - 1)
                epi_part(NT - 1)
                p.fence()

        proj_tm(qfm, "Wq", "bq", "gq", "betaq", probe_psum=True)
        act(qprobe[:, :], PR[:1, :1024], ACT.Copy, ["PR"], ["qprobe"],
            scale=1.0 / (L * 8.0))
        proj_tm(kfm, "Wk", "bk", "gk", "betak")
        proj_tm(vtm, "Wv", "bv", "gv", "betav", do_transpose=False)

        p.dma("gq", rowb[0:1, 0:1024], qprobe[:, :], reads=["qprobe"],
              writes=["rb0"])
        for c in range(NF):
            vmemset(qpb_f[:, :], 0.0, ["qpb_f"])
            for hh in range(2):
                h = 2 * c + hh
                p.dma("sq", qpb_f[hh * 64:(hh + 1) * 64, h:h + 1],
                      rowb[0:1, c * 128 + hh * 64:c * 128 + (hh + 1) * 64]
                      .rearrange("o n -> (o n) ()"),
                      reads=["rb0"], writes=["qpb_f"], slow=True)
            vcopy(qpb[c][:, :], qpb_f[:, :], ["qpb_f"], [f"qpb{c}"])
        for j in range(NCH):
            bj = j % 2
            load_fm_chunk(kfm, j, bj)
            for c in range(NF):
                mm(P2[:16, :512], qpb[c][:, :], fmin[bj][c][:, :],
                   start=(c == 0), stop=(c == NF - 1),
                   reads=[f"fmin{bj}", f"qpb{c}"], writes=["P2"])
            vcopy(logit[:, j * TC:(j + 1) * TC], P2[:16, :512], ["P2"], ["logit"])
        vred(smax[:, :], logit[:, :], ALU.max, ["logit"], ["smax"])
        vts(smax[:, :], smax[:, :], 0.0, None, ALU.max, ALU.bypass, ["smax"],
            ["smax"])
        vts(negm[:, :], smax[:, :], -1.0, None, ALU.mult, ALU.bypass, ["smax"],
            ["negm"])
        act(logit[:, :], logit[:, :], ACT.Exp, ["logit", "negm"], ["logit"],
            bias=negm[:, 0:1], accum=ssum[:, 0:1], accw=["ssum"])
        act(sc0[:, :], negm[:, :], ACT.Exp, ["negm"], ["sc0"])
        vtt(ALU.add, ssum[:, :], ssum[:, :], sc0[:, :], ["ssum", "sc0"], ["ssum"])
        vrecip(srec[:, :], ssum[:, :], ["ssum"], ["srec"])

        def _scmul(e):
            return e.tensor_scalar(logit[:, :], logit[:, :], srec[:, 0:1], None,
                                   op0=ALU.mult)
        p.emit("vector", _scmul, reads=["logit", "srec"], writes=["logit"])
        vtt(ALU.mult, sc0[:, :], sc0[:, :], srec[:, :], ["sc0", "srec"], ["sc0"])

        for i in range(NT):
            transp(P3[:, :16], logit[:, i * 128:(i + 1) * 128], eye_f[:16, :16],
                   ["logit", "eye_f"], ["P3"])
            vcopy(sc_tm[:, i * 16:(i + 1) * 16], P3[:128, :16], ["P3"], ["sc_tm"])
        p.fence()

        with ExitStack() as wctx:
            wsb = lambda n, sh, dt=F32: wctx.enter_context(nc.sbuf_tensor(n, sh, dt))
            wfq = load_w_bf16(wctx, "Wfq", W["Wfq"], D, 1024, wstage, "wstage")
            for c in range(NF):
                p.dma("sq", bfq_sb[:, c:c + 1],
                      W["bfq"][c * 128:(c + 1) * 128].rearrange("(p o) -> p o", o=1),
                      writes=["bfq_sb"], slow=True)

            def mm_part(k):
                j, m = k // NF, k % NF
                if m == 0:
                    load_fm_chunk(qfm, j, j % 2)
                bj = j % 2
                ps, psk = (P0, "P0") if k % 2 == 0 else (P1, "P1")
                for c in range(NF):
                    mm(ps[:, :TC], wfq[c][:, m * 128:(m + 1) * 128],
                       fmin[bj][c][:, :], start=(c == 0), stop=(c == NF - 1),
                       reads=[f"fmin{bj}", f"wb_Wfq_{c}"], writes=[psk])

            def epi_part(k):
                j, m = k // NF, k % NF
                ps, psk = (P0, "P0") if k % 2 == 0 else (P1, "P1")
                act(pq_f[:, :], ps[:, :TC], ACT.Tanh, [psk, "bfq_sb"],
                    ["pq_f"], bias=bfq_sb[:, m:m + 1])
                vts(pq_b[:, :], pq_f[:, :], 1.0, None, ALU.add, ALU.bypass,
                    ["pq_f"], ["pq_b"])
                p.dma("gq", pqfm[m, :, j * TC:(j + 1) * TC], pq_b[:, :],
                      reads=["pq_b"], tag="pq_st")

            for k in range(NCH * NF):
                mm_part(k)
                if k > 0:
                    epi_part(k - 1)
            epi_part(NCH * NF - 1)
            p.fence()

        with ExitStack() as wctx:
            wsb = lambda n, sh, dt=F32: wctx.enter_context(nc.sbuf_tensor(n, sh, dt))
            wfk = load_w_bf16(wctx, "Wfk", W["Wfk"], D, 1024, wstage, "wstage")
            p.dma("sq", brow_f[:, :], W["bfk"].rearrange("(o n) -> o n", o=1),
                  writes=["brow_f"])
            vcopy(brow_b[:, :], brow_f[:, :], ["brow_f"], ["brow_b"])

            def mm_part(i):
                j, tb = i // 4, i % 4
                if tb == 0:
                    load_fm_chunk(kfm, j, j % 2)
                bj = j % 2
                ps, psk = (P0, "P0") if i % 2 == 0 else (P1, "P1")
                t0 = tb * 128
                for half in range(2):
                    h0 = half * 512
                    for c in range(NF):
                        mm(ps[:, h0:h0 + 512], fmin[bj][c][:, t0:t0 + 128],
                           wfk[c][:, h0:h0 + 512], start=(c == 0), stop=False,
                           reads=[f"fmin{bj}", f"wb_Wfk_{c}"], writes=[psk])
                    mm(ps[:, h0:h0 + 512], ones_row_b[:, :],
                       brow_b[:, h0:h0 + 512], start=False, stop=True,
                       reads=["ones_row_b", "brow_b"], writes=[psk])

            def epi_part(i):
                ps, psk = (P0, "P0") if i % 2 == 0 else (P1, "P1")
                act(pk_f[:, :], ps[:, :1024], ACT.Tanh, [psk], ["pk_f"])
                vstt(pkw_t[:, :].rearrange("p (h e) -> p h e", h=16),
                     pk_f[:, :].rearrange("p (h e) -> p h e", h=16), 1.0,
                     bcast(sc_tm[:, i * 16:(i + 1) * 16], 64),
                     ALU.add, ALU.mult, ["pk_f", "sc_tm"], ["pkw_t"])
                for half in range(2):
                    h0 = half * 512
                    mm(PR[:1, h0:h0 + 512], ones_col_b[:, :],
                       pkw_t[:, h0:h0 + 512],
                       start=(i == 0), stop=(i == NT - 1),
                       reads=["ones_col_b", "pkw_t"], writes=["PR"])
                p.dma("gq", pkwtm[i, :, :], pkw_t[:, :], reads=["pkw_t"],
                      tag="pkw_st")

            for i in range(NT):
                mm_part(i)
                if i > 0:
                    epi_part(i - 1)
            epi_part(NT - 1)
            act(ksum_row[:, :], PR[:1, :1024], ACT.Copy, ["PR"], ["ksum_row"])
            p.dma("gq", rowb[2:3, 0:1024], ksum_row[:, :], reads=["ksum_row"],
                  writes=["rb2"])
            for c in range(NF):
                p.dma("sq", ksum_fm_f[:, c:c + 1],
                      rowb[2:3, c * 128:(c + 1) * 128].rearrange("o (p q) -> (o p) q", q=1),
                      reads=["rb2"], writes=["ksum_fm_f"], slow=True)
            p.dma("gq", rowb[1:2, 0:16], sc0[:, :], reads=["sc0"], writes=["rb1"],
                  slow=True)
            for c in range(NF):
                for hh in range(2):
                    h = 2 * c + hh
                    p.dma("sq", sc0_fm[hh * 64:(hh + 1) * 64, c:c + 1],
                          rowb[1:2, h:h + 1].partition_broadcast(64),
                          reads=["rb1"], writes=["sc0_fm"], slow=True)
            for c in range(NF):
                p.dma("sq", bfk_fm[:, c:c + 1],
                      W["bfk"][c * 128:(c + 1) * 128].rearrange("(p o) -> p o", o=1),
                      writes=["bfk_fm"], slow=True)
            act(pk0_fm[:, :], bfk_fm[:, :], ACT.Tanh, ["bfk_fm"], ["pk0_fm"])
            vts(pk0_fm[:, :], pk0_fm[:, :], 1.0, None, ALU.add, ALU.bypass,
                ["pk0_fm"], ["pk0_fm"])
            vtt(ALU.mult, pk0_fm[:, :], pk0_fm[:, :], sc0_fm[:, :],
                ["pk0_fm", "sc0_fm"], ["pk0_fm"])
            vtt(ALU.add, ksum_fm_f[:, :], ksum_fm_f[:, :], pk0_fm[:, :],
                ["ksum_fm_f", "pk0_fm"], ["ksum_fm_f"])
            vmemset(ksum_fm[:, :], 0.0, ["ksum_fm"])
            for h in range(H):
                c, hh = h // 2, h % 2
                vcopy(ksum_fm[hh * 64:(hh + 1) * 64, 8 * c + hh:8 * c + hh + 1],
                      ksum_fm_f[hh * 64:(hh + 1) * 64, c:c + 1],
                      ["ksum_fm_f"], ["ksum_fm"])
            p.fence()

        kv_acc = asb("kv_acc", [128, 512])
        vmemset(kv_acc[:, :], 0.0, ["kv_acc"])
        for i in range(NT):
            b = i % 2
            p.dma("sq", kvin[b][:, :], pkwtm[i, :, :], writes=[f"kvin{b}"])
            p.dma("sq", vvin[b][:, :], vtm[i, :, :], writes=[f"vvin{b}"])
            for h in range(H):
                c, hh = h // 2, h % 2
                mm(P0[hh * 64:(hh + 1) * 64, c * 64:(c + 1) * 64],
                   kvin[b][:, h * 64:(h + 1) * 64],
                   vvin[b][:, h * 64:(h + 1) * 64],
                   start=True, stop=True,
                   reads=[f"kvin{b}", f"vvin{b}"], writes=["P0"],
                   tile_position=(0, hh * 64))
            vtt(ALU.add, kv_acc[:, :], kv_acc[:, :], P0[:, :512],
                ["kv_acc", "P0"], ["kv_acc"])
        vmemset(kv_sb[:, :], 0.0, ["kv_sb"])
        for h in range(H):
            c, hh = h // 2, h % 2
            vcopy(kv_sb[hh * 64:(hh + 1) * 64,
                        c * 128 + hh * 64:c * 128 + hh * 64 + 64],
                  kv_acc[hh * 64:(hh + 1) * 64, c * 64:(c + 1) * 64],
                  ["kv_acc"], ["kv_sb"])
        p.fence()

        with ExitStack() as wctx:
            ga_rep = load_rep(wctx, "ga", W["ga"])
            ba_rep = load_rep(wctx, "ba", W["ba"])

            def mm_part(i):
                b = i % 2
                p.dma("sq", pqin[b][:, :].rearrange("p (c t) -> p c t", c=NF),
                      pqfm[:, :, i * 128:(i + 1) * 128].rearrange("c p t -> p c t"),
                      writes=[f"pqin{b}"])
                p.dma("sq", x2[b][:, :], xres[i * 128:(i + 1) * 128, :],
                      writes=[f"x2{b}"])
                ps, psk = (P0, "P0") if i % 2 == 0 else (P1, "P1")
                for c in range(NF):
                    lhsT = pqin[b][:, c * 128:(c + 1) * 128]
                    mm(ps[:, c * 128:(c + 1) * 128], lhsT,
                       kv_sb[:, c * 128:(c + 1) * 128],
                       start=True, stop=True, reads=[f"pqin{b}", "kv_sb"],
                       writes=[psk])
                    mm(P2[:, 4 * c:4 * c + 2], lhsT,
                       ksum_fm[:, 8 * c:8 * c + 2],
                       start=True, stop=True, reads=[f"pqin{b}", "ksum_fm"],
                       writes=["P2"])
                vcopy(den_t[:, :],
                      P2[:, 0:32].rearrange("p (c two) -> p c two", two=4)[:, :, 0:2],
                      ["P2"], ["den_t"])
                vts(den_t[:, :], den_t[:, :], 1e-6, None, ALU.add, ALU.bypass,
                    ["den_t"], ["den_t"])
                vrecip(den_t[:, :], den_t[:, :], ["den_t"], ["den_t"])
                vtt(ALU.mult, att_t[:, :].rearrange("p (h e) -> p h e", h=16),
                    ps[:, :1024].rearrange("p (h e) -> p h e", h=16),
                    bcast(den_t[:, :], 64), [psk, "den_t"], ["att_t"])

            def epi_part(i):
                b = i % 2
                vred(st_sum[:, i:i + 1], att_t[:, :], ALU.add, ["att_t"],
                     [f"st_sum{i % 2}"])
                act(sq_scr[:, :D], att_t[:, :], ACT.Square, ["att_t"],
                    ["sq_scr"], accum=st_ssq[:, i:i + 1], accw=[f"st_ssq{i % 2}"])
                rstd_batch(D, slice(i, i + 1))
                act(a_t[:, :], att_t[:, :], ACT.Identity,
                    ["att_t", f"st_rstd{i % 2}", f"st_nmr{i % 2}"], ["a_t"],
                    bias=st_nmr[:, i:i + 1], scale=st_rstd[:, i:i + 1])
                vtt(ALU.mult, a_t[:, :], a_t[:, :], ga_rep[:, :],
                    ["a_t", "rep_ga"], ["a_t"])
                vtt(ALU.add, a_t[:, :], a_t[:, :], ba_rep[:, :],
                    ["a_t", "rep_ba"], ["a_t"])
                vtt(ALU.add, x2[b][:, :], x2[b][:, :], a_t[:, :],
                    [f"x2{b}", "a_t"], [f"x2{b}"])
                p.dma("sq", xres[i * 128:(i + 1) * 128, :], x2[b][:, :],
                      reads=[f"x2{b}"])
                vcopy(a_tb[:, :], a_t[:, :], ["a_t"], ["a_tb"])
                transpose_tm_tile(a_tb[:, :], "a_tb", afm, i, evT, "evT")

            for i in range(NT):
                mm_part(i)
                epi_part(i)
            p.fence()

        actx.close()
        actx = ctx.enter_context(ExitStack())
        asb = lambda n, sh, dt=F32: actx.enter_context(nc.sbuf_tensor(n, sh, dt))
        wstage = asb("wstage2", [128, 1024])
        fmin = [[asb(f"fmin2_{b}_{c}", [128, TC], BF16) for c in range(NF)]
                for b in range(2)]

        def load_fm_chunk(src_dram, j, b):
            for c in range(NF):
                p.dma("sq", fmin[b][c][:, :], src_dram[c, :, j * TC:(j + 1) * TC],
                      writes=[f"fmin{b}"], tag="fmld")

        with ExitStack() as wctx:
            w1 = load_w_bf16(wctx, "W1", W["W1"], D, MLP, wstage, "wstage")
            b1_sb = wctx.enter_context(nc.sbuf_tensor("b1_sb", [128, MLP // 128], F32))
            g1_b = wctx.enter_context(nc.sbuf_tensor("g1_b", [128, TC], BF16))
            for c in range(MLP // 128):
                p.dma("sq", b1_sb[:, c:c + 1],
                      W["b1"][c * 128:(c + 1) * 128].rearrange("(p o) -> p o", o=1),
                      writes=["b1_sb"], slow=True)

            def mm_part(k):
                j, m = k // (MLP // 128), k % (MLP // 128)
                if m == 0:
                    load_fm_chunk(afm, j, j % 2)
                bj = j % 2
                ps, psk = (P0, "P0") if k % 2 == 0 else (P1, "P1")
                for c in range(NF):
                    mm(ps[:, :TC], w1[c][:, m * 128:(m + 1) * 128],
                       fmin[bj][c][:, :], start=(c == 0), stop=(c == NF - 1),
                       reads=[f"fmin{bj}", f"wb_W1_{c}"], writes=[psk])

            def epi_part(k):
                j, m = k // (MLP // 128), k % (MLP // 128)
                ps, psk = (P0, "P0") if k % 2 == 0 else (P1, "P1")
                act(g1_b[:, :], ps[:, :TC], ACT.Gelu, [psk, "b1_sb"],
                    ["g1_b"], bias=b1_sb[:, m:m + 1])
                p.dma("gq", g1fm[m, :, j * TC:(j + 1) * TC], g1_b[:, :],
                      reads=["g1_b"], tag="g1_st")

            NK = NCH * (MLP // 128)
            for k in range(NK):
                mm_part(k)
                if k > 0:
                    epi_part(k - 1)
            epi_part(NK - 1)
            p.fence()

        with ExitStack() as wctx:
            w2 = load_w_bf16(wctx, "W2", W["W2"], MLP, 1024, wstage, "wstage")
            b2_rep = load_rep(wctx, "b2", W["b2"])
            wsb = lambda n, sh, dt=F32: wctx.enter_context(nc.sbuf_tensor(n, sh, dt))
            g1cat = [wsb(f"g1cat{b}", [128, MLP], BF16) for b in range(2)]
            g1in = [[g1cat[b][:, c * 128:(c + 1) * 128]
                     for c in range(MLP // 128)] for b in range(2)]
            x3 = [wsb(f"x3{b}", [128, D]) for b in range(2)]
            mo_t = [wsb(f"mo_t{b}", [128, 1024]) for b in range(2)]

            def mm_part(i):
                b = i % 2
                p.dma("sq", g1cat[b][:, :].rearrange("p (c t) -> p c t", c=MLP // 128),
                      g1fm[:, :, i * 128:(i + 1) * 128].rearrange("c p t -> p c t"),
                      writes=[f"g1in{b}"], tag="g1ld")
                p.dma("sq", x3[b][:, :], xres[i * 128:(i + 1) * 128, :],
                      writes=[f"x3{b}"])
                ps, psk = (P0, "P0") if i % 2 == 0 else (P1, "P1")
                for half in range(2):
                    h0 = half * 512
                    for c in range(MLP // 128):
                        mm(ps[:, h0:h0 + 512], g1in[b][c][:, :],
                           w2[c][:, h0:h0 + 512],
                           start=(c == 0), stop=(c == MLP // 128 - 1),
                           reads=[f"g1in{b}", f"wb_W2_{c}"], writes=[psk])

            def epi_part(i):
                b = i % 2
                ps, psk = (P0, "P0") if i % 2 == 0 else (P1, "P1")
                vtt(ALU.add, mo_t[b][:, :], ps[:, :1024], b2_rep[:, :],
                    [psk, "rep_b2"], [f"mo_t{b}"])
                vtt(ALU.add, x3[b][:, :], x3[b][:, :], mo_t[b][:, :],
                    [f"x3{b}", f"mo_t{b}"], [f"x3{b}"])
                dst = xres if len(phases) > 1 else out_d
                p.dma("sq", dst[i * 128:(i + 1) * 128, :], x3[b][:, :],
                      reads=[f"x3{b}"], tag="xout")

            for i in range(NT):
                mm_part(i)
                if i > 0:
                    epi_part(i - 1)
            epi_part(NT - 1)
            p.fence()
        actx.close()

    fctx = None
    alpha_sb = None
    if "B" in phases or "D" in phases:
        fctx = ctx.enter_context(ExitStack())
        alpha_sb = fctx.enter_context(nc.sbuf_tensor("alpha_sb", [1, 16], F32))
        fpctx = ExitStack()
        fsb = lambda n, sh, dt=F32: fpctx.enter_context(nc.sbuf_tensor(n, sh, dt))
        wqT = [fsb(f"wqTp_{c}", [128, 1024], BF16) for c in range(NF)]
        fstage = fsb("fstage", [128, 1024])
        fabs = fsb("fabs", [128, 1024])
        fmask = fsb("fmask", [128, 1024])
        fsgn = fsb("fsgn", [128, 1024])
        fsums = fsb("fsums", [128, 8])
        fred = fsb("fred", [128, 16])
        frow = fsb("frow", [1, 16])
        ones_col_f = fsb("ones_col_f", [128, 16])
        vmemset(ones_col_f[:, :], 1.0, ["ones_col_f"])

        fwsm = fsb("fwsm", [128, 1024], BF16)
        evF = fsb("evF", [128, 1024], BF16)
        dstag = fsb("dstag", [128, 16])
        vmemset(fred[:, :], 0.0, ["fred"])
        for t in range(NF):
            p.dma("sq", fstage[:, :], W["fw"][t * 128:(t + 1) * 128, :],
                  writes=["fstage"])
            act(fabs[:, :], fstage[:, :], ACT.Abs, ["fstage"], ["fabs"],
                accum=fsums[:, t:t + 1], accw=["fsums"])
        vred(fred[:, 0:1], fsums[:, :], ALU.add, ["fsums"], ["fred"])
        def mmf(out_ap, lhsT_ap, rhs_ap, reads, writes):
            mm(out_ap, lhsT_ap, rhs_ap, start=True, stop=True, reads=reads,
               writes=writes)
        mmf(PR[:1, 0:1], ones_col_f[:, 0:1], fred[:, 0:1],
            ["fred", "ones_col_f"], ["PR"])
        act(frow[:, 0:1], PR[:1, 0:1], ACT.Copy, ["PR"], ["frow"])
        vts(frow[:, 0:1], frow[:, 0:1], 0.7 / (1024.0 * 1024.0), None,
            ALU.mult, ALU.bypass, ["frow"], ["frow"])
        p.dma("gq", rowb[3:4, 0:1], frow[:, 0:1], reads=["frow"], writes=["rb3"],
              slow=True)
        p.dma("sq", dstag[:, 0:1], rowb[3:4, 0:1].partition_broadcast(128),
              reads=["rb3"], writes=["dstag"], slow=True)
        for t in range(NF):
            p.dma("sq", fstage[:, :], W["fw"][t * 128:(t + 1) * 128, :],
                  writes=["fstage"])
            act(fabs[:, :], fstage[:, :], ACT.Abs, ["fstage"], ["fabs"])
            def _mk(e, t=t):
                return e.tensor_single_scalar(fmask[:, :], fabs[:, :],
                                              dstag[:, 0:1], op=ALU.is_gt)
            p.emit("vector", _mk, reads=["fabs", "dstag"], writes=["fmask"])
            vtt(ALU.mult, sq_scr[:, :1024], fabs[:, :], fmask[:, :],
                ["fabs", "fmask"], ["sq_scr"])
            vred(fsums[:, t:t + 1], sq_scr[:, :1024], ALU.add, ["sq_scr"],
                 ["fsums"])
            vred(fred[:, t:t + 1], fmask[:, :], ALU.add, ["fmask"], ["fred"])
            act(fsgn[:, :], fstage[:, :], ACT.Sign, ["fstage"], ["fsgn"])
            vtt(ALU.mult, fwsm[:, :], fsgn[:, :], fmask[:, :],
                ["fsgn", "fmask"], ["fwsm"])
            p2b = P2[:, :512].bitcast(BF16)
            for c in range(NF):
                transp(p2b[:, c * 128:(c + 1) * 128],
                       fwsm[:, c * 128:(c + 1) * 128], eye_b[:, :],
                       ["fwsm", "eye_b"], ["P2"])
            act(evF[:, :1024], p2b[:, :1024], ACT.Copy, ["P2"], ["evF"])
            for c in range(NF):
                vcopy(wqT[c][:, t * 128:(t + 1) * 128],
                      evF[:, c * 128:(c + 1) * 128], ["evF"], [f"wqTp_{c}"])
        vred(fred[:, 8:9], fsums[:, :], ALU.add, ["fsums"], ["fred"])
        mmf(PR[:1, 0:16], ones_col_f[:, 0:1], fred[:, 0:16],
            ["fred", "ones_col_f"], ["PR"])
        act(frow[:, :], PR[:1, 0:16], ACT.Copy, ["PR"], ["frow"])
        vred(alpha_sb[:, 0:1], frow[:, 0:8], ALU.add, ["frow"], ["alpha_sb"])
        vts(alpha_sb[:, 0:1], alpha_sb[:, 0:1], 1.0, None, ALU.max, ALU.bypass,
            ["alpha_sb"], ["alpha_sb"])
        vrecip(alpha_sb[:, 1:2], alpha_sb[:, 0:1], ["alpha_sb"], ["alpha_sb"])
        def _amul(e):
            return e.tensor_tensor(alpha_sb[:, 2:3], alpha_sb[:, 1:2],
                                   frow[:, 8:9], op=ALU.mult)
        p.emit("vector", _amul, reads=["alpha_sb", "frow"], writes=["alpha_sb"])
        for c in range(NF):
            p.dma("gq", wqT_d[c, :, :], wqT[c][:, :], reads=[f"wqTp_{c}"],
                  tag="wqT_st")
        p.fence()
        fpctx.close()

    def ffn_phase(tag, dst_final):
        with ExitStack() as qctx:
            qsb = lambda n, sh, dt=F32: qctx.enter_context(
                nc.sbuf_tensor("f" + tag + n, sh, dt))
            gam_rep = load_rep(qctx, "fg" + tag, W["fgamma"])
            bet_rep = load_rep(qctx, "fb" + tag, W["fbeta"])
            xt = [qsb(f"x{b}", [128, D]) for b in range(2)]
            zt = [qsb(f"z{b}", [128, D]) for b in range(2)]
            amax_st = qsb("amax", [128, NT])
            gmax = qsb("gmax", [1, 8])
            scl = qsb("scl", [1, 4])
            inv_s = qsb("invs", [128, 1])
            osc = qsb("osc", [128, 1])
            cbuf = qsb("cbuf", [128, D])
            bbuf = qsb("bbuf", [128, D])
            qb = qsb("qb", [128, D], BF16)
            evQ = qsb("evQ", [128, 1024], BF16)
            qcat = [qctx.enter_context(
                nc.sbuf_tensor(f"f{tag}qcat{b}", [128, NF * TC], BF16))
                for b in range(2)]
            qin = [[qcat[b][:, c * TC:(c + 1) * TC] for c in range(NF)]
                   for b in range(2)]
            mo = qsb("mo", [128, D])
            wqTcat = qctx.enter_context(
                nc.sbuf_tensor(f"f{tag}wqTcat", [128, NF * 1024], BF16))
            wqT = [wqTcat[:, c * 1024:(c + 1) * 1024] for c in range(NF)]
            p.dma("sq", wqTcat[:, :].rearrange("p (c t) -> p c t", c=NF),
                  wqT_d[:, :, :].rearrange("c p t -> p c t"),
                  writes=[f"f{tag}wqT"])
            for i in range(NT):
                b = i % 2
                kx, kz = f"f{tag}x{b}", f"f{tag}z{b}"
                p.dma("sq", xt[b][:, :], xres[i * 128:(i + 1) * 128, :],
                      writes=[kx])
                act(sq_scr[:, :D], xt[b][:, :], ACT.Square, [kx], ["sq_scr"],
                    accum=st_ssq[:, i:i + 1], accw=[f"st_ssq{i % 2}"])
                act(zt[b][:, :], xt[b][:, :], ACT.Copy, [kx], [kz],
                    accum=st_sum[:, i:i + 1], accw=[f"st_sum{i % 2}"])
                rstd_batch(D, slice(i, i + 1))
                act(zt[b][:, :], xt[b][:, :], ACT.Identity,
                    [kx, f"st_rstd{i % 2}", f"st_nmr{i % 2}"], [kz],
                    bias=st_nmr[:, i:i + 1], scale=st_rstd[:, i:i + 1])
                vtt(ALU.mult, zt[b][:, :], zt[b][:, :], gam_rep[:, :],
                    [kz, "rep_fg" + tag], [kz])
                vtt(ALU.add, zt[b][:, :], zt[b][:, :], bet_rep[:, :],
                    [kz, "rep_fb" + tag], [kz])
                vred(amax_st[:, i:i + 1], zt[b][:, :], ALU.max, [kz],
                     ["f" + tag + "amax"], absval=True)
                p.dma("sq", xnb[i * 128:(i + 1) * 128, :], zt[b][:, :],
                      reads=[kz], tag="xnb_st")
            vred(amax_st[:, 0:1], amax_st[:, :], ALU.max, ["f" + tag + "amax"],
                 ["f" + tag + "amax"])
            transp(P3[0:1, 0:128], amax_st[:, 0:1], eye_f[:, :],
                   ["f" + tag + "amax", "eye_f"], ["P3"])
            vred(scl[:, 0:1], P3[:1, 0:128], ALU.max, ["P3"], ["fscl" + tag])
            def _brd(e):
                return e.tensor_copy(gmax[:, :], scl[:, 0:1].broadcast_to((1, 8)))
            p.emit("vector", _brd, reads=["fscl" + tag], writes=["fgmax" + tag])
            p.dma("gq", cc_in[:, :], gmax[:, :], reads=["fgmax" + tag],
                  writes=["fccin" + tag])
            def _cc(e):
                return e.collective_compute(
                    "AllReduce", ALU.max, replica_groups=[list(range(8))],
                    ins=[cc_in.opt()], outs=[cc_out.opt()])
            p.emit("gpsimd", _cc, reads=["fccin" + tag], writes=["fccout" + tag],
                   tag="cc")
            p.dma("sq", gmax[:, :], cc_out[:, :], reads=["fccout" + tag],
                  writes=["fgmax" + tag])
            vts(scl[:, 1:2], gmax[:, 0:1], 1e-8, None, ALU.max, ALU.bypass,
                ["fgmax" + tag], ["fscl" + tag])
            vrecip(scl[:, 2:3], scl[:, 1:2], ["fscl" + tag], ["fscl" + tag])
            vts(scl[:, 2:3], scl[:, 2:3], 127.0, None, ALU.mult, ALU.bypass,
                ["fscl" + tag], ["fscl" + tag])
            vts(scl[:, 1:2], scl[:, 1:2], 1.0 / 127.0, None, ALU.mult,
                ALU.bypass, ["fscl" + tag], ["fscl" + tag])
            def _osc(e):
                return e.tensor_tensor(scl[:, 3:4], scl[:, 1:2],
                                       alpha_sb[:, 2:3], op=ALU.mult)
            p.emit("vector", _osc, reads=["fscl" + tag, "alpha_sb"],
                   writes=["fscl" + tag])
            p.dma("gq", rowb[4:5, 0:4], scl[:, :], reads=["fscl" + tag],
                  writes=["rb4" + tag], slow=True)
            p.dma("sq", inv_s[:, :], rowb[4:5, 2:3].partition_broadcast(128),
                  reads=["rb4" + tag], writes=["finvs" + tag], slow=True)
            p.dma("sq", osc[:, :], rowb[4:5, 3:4].partition_broadcast(128),
                  reads=["rb4" + tag], writes=["fosc" + tag], slow=True)
            def quant_tile(i):
                b = i % 2
                kz = f"f{tag}z{b}"
                j, tb = i // 4, i % 4
                bj = j % 2
                p.dma("sq", zt[b][:, :], xnb[i * 128:(i + 1) * 128, :],
                      writes=[kz])
                act(cbuf[:, :], zt[b][:, :], ACT.Identity,
                    [kz, "finvs" + tag], ["fcb"], scale=inv_s[:, 0:1])
                vts(cbuf[:, :], cbuf[:, :], -127.0, 127.0, ALU.max, ALU.min,
                    ["fcb"], ["fcb"])
                vcopy(bbuf[:, :], cbuf[:, :], ["fcb"], ["fbb"])
                for k in [128.0, 64.0, 32.0, 16.0, 8.0, 4.0, 2.0, 1.0]:
                    vts(sq_scr[:, :D], bbuf[:, :], k - 128.5, None, ALU.is_ge,
                        ALU.bypass, ["fbb"], ["sq_scr"])
                    vstt(bbuf[:, :], sq_scr[:, :D], -k, bbuf[:, :], ALU.mult,
                         ALU.add, ["sq_scr", "fbb"], ["fbb"])
                vtt(ALU.subtract, cbuf[:, :], cbuf[:, :], bbuf[:, :],
                    ["fcb", "fbb"], ["fcb"])
                vts(qb[:, :], cbuf[:, :], -128.0, None, ALU.add, ALU.bypass,
                    ["fcb"], ["fqb"])
                p2b = P2[:, :512].bitcast(BF16)
                for c in range(NF):
                    transp(p2b[:, c * 128:(c + 1) * 128],
                           qb[:, c * 128:(c + 1) * 128], eye_b[:, :],
                           ["fqb", "eye_b"], ["P2"])
                act(qcat[bj][:, :].rearrange("p (c t) -> p c t", c=NF)
                    [:, :, tb * 128:tb * 128 + 128]
                    if False else
                    qcat[bj][:, tb * 128:tb * 128 + 128]
                    .rearrange("p (o t) -> p o t", o=1)
                    .broadcast_to((128, 1, 128)) if False else
                    qcat[bj][:, :]
                    .rearrange("p (c t) -> p c t", c=NF)[:, :, tb * 128:(tb + 1) * 128],
                    p2b[:, :1024].rearrange("p (c t) -> p c t", c=NF),
                    ACT.Copy, ["P2"], [f"f{tag}qin{bj}"])

            def mm_chunk(j):
                bj = j % 2
                for tb in range(4):
                    i = j * 4 + tb
                    b = i % 2
                    kx = f"f{tag}x{b}"
                    p.dma("sq", xt[b][:, :], xres[i * 128:(i + 1) * 128, :],
                          writes=[kx])
                    ps, psk = (P0, "P0") if i % 2 == 0 else (P1, "P1")
                    t0 = tb * 128
                    for half in range(2):
                        h0 = half * 512
                        for c in range(NF):
                            mm(ps[:, h0:h0 + 512], qin[bj][c][:, t0:t0 + 128],
                               wqT[c][:, h0:h0 + 512], start=(c == 0),
                               stop=(c == NF - 1),
                               reads=[f"f{tag}qin{bj}", f"f{tag}wqT"],
                               writes=[psk])
                    act(mo[:, :], ps[:, :1024], ACT.Identity,
                        [psk, "fosc" + tag], ["fmo"], scale=osc[:, 0:1])
                    vtt(ALU.add, xt[b][:, :], xt[b][:, :], mo[:, :],
                        [kx, "fmo"], [kx])
                    p.dma("sq", dst_final[i * 128:(i + 1) * 128, :], xt[b][:, :],
                          reads=[kx], tag="xout")

            for j in range(NCH):
                for tb in range(4):
                    quant_tile(j * 4 + tb)
                if j > 0:
                    mm_chunk(j - 1)
            mm_chunk(NCH - 1)
            p.fence()

    if "B" in phases:
        ffn_phase("B", xres if ("C" in phases or "D" in phases) else out_d)

    if "C" in phases:
        cctx = ctx.enter_context(ExitStack())
        csb = lambda n, sh, dt=F32: cctx.enter_context(nc.sbuf_tensor(n, sh, dt))
        dtT = csb("dtT", [64, DIN], BF16)
        xpwT = [csb(f"xpwT_{t}", [128, 96], BF16) for t in range(NCC)]
        A_sb = csb("A_sb", [128, NCC * DST])
        conv_sb = csb("conv_sb", [128, NCC * 4])
        cvb_sb = csb("cvb_sb", [128, NCC])
        dtb_sb = csb("dtb_sb", [128, NCC])
        Dv_sb = csb("Dv_sb", [128, NCC])
        cpctx = ExitStack()
        csb = lambda n, sh, dt=F32: cpctx.enter_context(nc.sbuf_tensor(n, sh, dt))
        cstage = csb("cstage", [128, 1024])
        cstb = csb("cstb", [128, 1024], BF16)
        evC = csb("evC", [128, 1024], BF16)
        for o in range(2 * DIN // 128):
            p.dma("sq", cstage[:, :], W["in_proj_w"][o * 128:(o + 1) * 128, :],
                  writes=["cstage"])
            vcopy(cstb[:, :], cstage[:, :], ["cstage"], ["cstb"])
            p2b = P2[:, :512].bitcast(BF16)
            for c in range(NF):
                transp(p2b[:, c * 128:(c + 1) * 128],
                       cstb[:, c * 128:(c + 1) * 128], eye_b[:, :],
                       ["cstb", "eye_b"], ["P2"])
            act(evC[:, :1024], p2b[:, :1024], ACT.Copy, ["P2"], ["evC"])
            for c in range(NF):
                p.dma("gq", ipT_d[c, :, o * 128:(o + 1) * 128],
                      evC[:, c * 128:(c + 1) * 128], reads=["evC"], tag="ipT_st")
        for o in range(D // 128):
            p.dma("sq", cstage[:, :1024],
                  W["out_proj_w"][o * 128:(o + 1) * 128, 0:1024],
                  writes=["cstage"])
            p.dma("sq", sq_scr[:, :1024],
                  W["out_proj_w"][o * 128:(o + 1) * 128, 1024:2048],
                  writes=["sq_scr"])
            vcopy(cstb[:, :], cstage[:, :1024], ["cstage"], ["cstb"])
            p2b = P2[:, :512].bitcast(BF16)
            for c in range(8):
                transp(p2b[:, c * 128:(c + 1) * 128],
                       cstb[:, c * 128:(c + 1) * 128], eye_b[:, :],
                       ["cstb", "eye_b"], ["P2"])
            act(evC[:, :1024], p2b[:, :1024], ACT.Copy, ["P2"], ["evC"])
            for c in range(8):
                p.dma("gq", opT_d[c, :, o * 128:(o + 1) * 128],
                      evC[:, c * 128:(c + 1) * 128], reads=["evC"], tag="opT_st")
            vcopy(cstb[:, :], sq_scr[:, :1024], ["sq_scr"], ["cstb"])
            for c in range(8):
                transp(p2b[:, c * 128:(c + 1) * 128],
                       cstb[:, c * 128:(c + 1) * 128], eye_b[:, :],
                       ["cstb", "eye_b"], ["P2"])
            act(evC[:, :1024], p2b[:, :1024], ACT.Copy, ["P2"], ["evC"])
            for c in range(8):
                p.dma("gq", opT_d[8 + c, :, o * 128:(o + 1) * 128],
                      evC[:, c * 128:(c + 1) * 128], reads=["evC"], tag="opT_st")
        for t in range(NCC):
            p.dma("sq", cstage[:, :64], W["dt_proj_w"][t * 128:(t + 1) * 128, :],
                  writes=["cstage"])
            vcopy(cstb[:, :64], cstage[:, :64], ["cstage"], ["cstb"])
            transp(P2[:, 0:64].bitcast(BF16)[0:64, :],
                   cstb[:, :64], eye_b[:, :], ["cstb", "eye_b"], ["P2"])
            vcopy(dtT[:, t * 128:(t + 1) * 128],
                  P2[:, 0:64].bitcast(BF16)[0:64, :], ["P2"], ["dtT"])
        xpw_sb = csb("xpw_sb", [96, DIN])
        xpw_sbb = csb("xpw_sbb", [96, DIN], BF16)
        p.dma("sq", xpw_sb[:, :], W["x_proj_w"][:, :], writes=["xpw_sb"])
        vcopy(xpw_sbb[:, :], xpw_sb[:, :], ["xpw_sb"], ["xpw_sbb"])
        for t in range(NCC):
            transp(P2[:, 0:48].bitcast(BF16)[:, 0:96],
                   xpw_sbb[:, t * 128:(t + 1) * 128], eye_b[:96, :96],
                   ["xpw_sbb", "eye_b"], ["P2"])
            vcopy(xpwT[t][:, :], P2[:, 0:48].bitcast(BF16)[:, 0:96], ["P2"],
                  [f"xpwT_{t}"])
        for t in range(NCC):
            p.dma("sq", cstage[:, :DST], W["A_log"][t * 128:(t + 1) * 128, :],
                  writes=["cstage"])
            act(A_sb[:, t * DST:(t + 1) * DST], cstage[:, :DST], ACT.Exp,
                ["cstage"], ["A_sb"])
        vts(A_sb[:, :], A_sb[:, :], -1.0, None, ALU.mult, ALU.bypass,
            ["A_sb"], ["A_sb"])
        for t in range(NCC):
            p.dma("sq", conv_sb[:, t * 4:(t + 1) * 4],
                  W["conv_w"].rearrange("c o k -> (c o) k")[t * 128:(t + 1) * 128, :],
                  writes=["conv_sb"])
        for t in range(NCC):
            p.dma("sq", cvb_sb[:, t:t + 1],
                  W["conv_b"][t * 128:(t + 1) * 128].rearrange("(p o) -> p o", o=1),
                  writes=["cvb_sb"], slow=True)
            p.dma("sq", dtb_sb[:, t:t + 1],
                  W["dt_proj_b"][t * 128:(t + 1) * 128].rearrange("(p o) -> p o", o=1),
                  writes=["dtb_sb"], slow=True)
            p.dma("sq", Dv_sb[:, t:t + 1],
                  W["Dv"][t * 128:(t + 1) * 128].rearrange("(p o) -> p o", o=1),
                  writes=["Dv_sb"], slow=True)
        p.fence()
        cpctx.close()
        csb = lambda n, sh, dt=F32: cctx.enter_context(nc.sbuf_tensor(n, sh, dt))

        with ExitStack() as wctx:
            nw_rep = load_rep(wctx, "nw", W["norm_w"])
            xt = [wctx.enter_context(nc.sbuf_tensor(f"c0x{b}", [128, D], F32))
                  for b in range(2)]
            zt = [wctx.enter_context(nc.sbuf_tensor(f"c0z{b}", [128, D], F32))
                  for b in range(2)]
            xrb = wctx.enter_context(nc.sbuf_tensor("c0xrb", [128, D], BF16))
            evC0 = wctx.enter_context(nc.sbuf_tensor("c0ev", [128, 1024], BF16))
            for i in range(NT):
                b = i % 2
                kx, kz = f"c0x{b}", f"c0z{b}"
                p.dma("sq", xt[b][:, :], xres[i * 128:(i + 1) * 128, :],
                      writes=[kx])
                act(sq_scr[:, :D], xt[b][:, :], ACT.Square, [kx], ["sq_scr"],
                    accum=st_ssq[:, i:i + 1], accw=[f"st_ssq{i % 2}"])
                act(zt[b][:, :], xt[b][:, :], ACT.Copy, [kx], [kz],
                    accum=st_sum[:, i:i + 1], accw=[f"st_sum{i % 2}"])
                rstd_batch(D, slice(i, i + 1))
                act(zt[b][:, :], xt[b][:, :], ACT.Identity,
                    [kx, f"st_rstd{i % 2}", f"st_nmr{i % 2}"], [kz],
                    bias=st_nmr[:, i:i + 1], scale=st_rstd[:, i:i + 1])
                vtt(ALU.add, xt[b][:, :], xt[b][:, :], zt[b][:, :], [kx, kz],
                    [kx])
                p.dma("sq", xres[i * 128:(i + 1) * 128, :], xt[b][:, :],
                      reads=[kx])
                act(sq_scr[:, :D], zt[b][:, :], ACT.Square, [kz], ["sq_scr"],
                    accum=st_ssq[:, i:i + 1], accw=[f"st_ssq{i % 2}"])
                vts(st_tmp[:, i:i + 1], st_ssq[:, i:i + 1], 1.0 / D, 1e-5,
                    ALU.mult, ALU.add, [f"st_ssq{i % 2}"], [f"st_tmp{i % 2}"])
                vrecip(st_rstd[:, i:i + 1], st_tmp[:, i:i + 1],
                       [f"st_tmp{i % 2}"], [f"st_rstd{i % 2}"])
                act(st_rstd[:, i:i + 1], st_rstd[:, i:i + 1], ACT.Sqrt,
                    [f"st_rstd{i % 2}"], [f"st_rstd{i % 2}"])
                act(st_nmr[:, i:i + 1], st_rstd[:, i:i + 1], ACT.Square,
                    [f"st_rstd{i % 2}"], [f"st_nmr{i % 2}"])
                vtt(ALU.mult, st_nmr[:, i:i + 1], st_nmr[:, i:i + 1],
                    st_tmp[:, i:i + 1], [f"st_nmr{i % 2}", f"st_tmp{i % 2}"],
                    [f"st_nmr{i % 2}"])
                vts(st_nmr[:, i:i + 1], st_nmr[:, i:i + 1], -0.5, 1.5,
                    ALU.mult, ALU.add, [f"st_nmr{i % 2}"], [f"st_nmr{i % 2}"])
                vtt(ALU.mult, st_rstd[:, i:i + 1], st_rstd[:, i:i + 1],
                    st_nmr[:, i:i + 1], [f"st_rstd{i % 2}", f"st_nmr{i % 2}"],
                    [f"st_rstd{i % 2}"])
                act(zt[b][:, :], zt[b][:, :], ACT.Identity,
                    [kz, f"st_rstd{i % 2}"], [kz], scale=st_rstd[:, i:i + 1])
                vtt(ALU.mult, xrb[:, :], zt[b][:, :], nw_rep[:, :],
                    [kz, "rep_nw"], ["c0xrb"])
                transpose_tm_tile(xrb[:, :], "c0xrb", qfm, i, evC0, "c0ev")
            p.fence()

        TCp = TC + 3
        xp = csb("xp", [128, NCC * TCp], BF16)
        u_cm = csb("u_cm", [128, NCC * TC])
        zgs = csb("zgs", [128, NCC * TC], BF16)
        bc_rep = csb("bc_rep", [128, 2 * DST * TC], BF16)
        b_half = csb("b_half", [128, 4 * TC], BF16)
        h_buf = csb("h_buf", [128, DST * TC], BF16)
        h_init = csb("h_init", [128, NCC * DST])
        ybf = csb("ybf", [128, NCC * TC], BF16)
        delta_t = [csb(f"delta_t{q}", [128, TC]) for q in range(2)]
        du3 = [csb(f"du3_{q}", [128, TC]) for q in range(3)]
        dA_q = csb("dA_q", [128, 5 * TC], BF16)
        g8 = csb("g8", [128, 8 * TC])
        xdbl_b = csb("xdbl_b", [64, TC], BF16)
        xincat = csb("xincat", [128, NF * TC], BF16)
        xin = [[xincat[:, c * TC:(c + 1) * TC] for c in range(NF)]]
        ipw = [csb(f"ipw{b}", [128, 8 * 128], BF16) for b in range(1)]
        opw = [csb(f"opw{b}", [128, 512], BF16) for b in range(2)]
        xt2 = [csb(f"c1x{b}", [128, D]) for b in range(4)]
        vmemset(h_init[:, :], 0.0, ["h_init"])
        vmemset(xp[:, :], 0.0, ["xp"])
        dstC = xres if "D" in phases else out_d

        def gtt(out_ap, a_ap, b_ap, reads, writes):
            def fn(e, o=out_ap, a=a_ap, b=b_ap):
                return e.tensor_tensor(o, a, b, op=ALU.add)
            p.emit("gpsimd", fn, reads=reads, writes=writes)

        def emit_xz_group(j, m):
            p.dma("sq", ipw[0][:, :].rearrange("p (c t) -> p c t", c=NF),
                  ipT_d[:, :, m * 128:(m + 1) * 128].rearrange("c p t -> p c t"),
                  writes=["ipw0"], tag="ipwld")
            for c in range(NF):
                mm(P2[:, :TC], ipw[0][:, c * 128:(c + 1) * 128],
                   xin[0][c][:, :], start=(c == 0), stop=(c == NF - 1),
                   reads=["ipw0", "cxin0"], writes=["P2"])
            if m < NCC:
                act(xp[:, m * TCp + 3:(m + 1) * TCp], P2[:, :TC],
                    ACT.Copy, ["P2"], ["xp"])
            else:
                cb = m - NCC
                act(zgs[:, cb * TC:(cb + 1) * TC], P2[:, :TC],
                    ACT.Silu, ["P2"], ["zgs"])

        def emit_xin(j):
            p.dma("sq", xincat[:, :].rearrange("p (c t) -> p c t", c=NF),
                  qfm[:, :, j * TC:(j + 1) * TC].rearrange("c p t -> p c t"),
                  writes=["cxin0"], tag="xnld")

        def emit_pre(j):
            for cb in range(NCC):
                x0 = cb * TCp
                q = cb % 3
                dq, kq = du3[q], f"du3_{q}"
                def _c0(e, x0=x0, cb=cb, dq=dq):
                    return e.tensor_scalar(dq[:, :], xp[:, x0:x0 + TC],
                                           conv_sb[:, 4 * cb:4 * cb + 1], None,
                                           op0=ALU.mult)
                p.emit("vector", _c0, reads=["xp", "conv_sb"], writes=[kq])
                for jj in range(1, 4):
                    vstt(dq[:, :], xp[:, x0 + jj:x0 + jj + TC],
                         conv_sb[:, 4 * cb + jj:4 * cb + jj + 1], dq[:, :],
                         ALU.mult, ALU.add, ["xp", "conv_sb", kq], [kq])
                act(u_cm[:, cb * TC:(cb + 1) * TC], dq[:, :], ACT.Silu,
                    [kq, "cvb_sb"], ["u_cm"], bias=cvb_sb[:, cb:cb + 1])
            for cb in range(NCC):
                def _tl(e, cb=cb):
                    return e.tensor_copy(xp[:, cb * TCp:cb * TCp + 3],
                                         xp[:, cb * TCp + TC:cb * TCp + TC + 3])
                p.emit("vector", _tl, reads=["xp"], writes=["xp"])
            for cb in range(NCC):
                vcopy(b_half[:, 0:TC], u_cm[:, cb * TC:(cb + 1) * TC], ["u_cm"],
                      ["b_half"])
                mm(P2[0:96, :TC], xpwT[cb][:, :], b_half[:, 0:TC],
                   start=(cb == 0), stop=(cb == NCC - 1),
                   reads=[f"xpwT_{cb}", "b_half"], writes=["P2"])
            xdbl = g8[0:96, 7 * TC:8 * TC]
            act(xdbl[:, :], P2[0:96, :TC], ACT.Copy, ["P2"], ["g8"])
            vcopy(xdbl_b[:, :], xdbl[0:64, :], ["g8"], ["xdbl_b"])
            p.dma("gq", bc_d[:, :], xdbl[64:96, :], reads=["g8"],
                  writes=["bcd"])
            for n in range(2 * DST):
                p.dma("sq", delta_t[n % 2][:, :],
                      bc_d[n:n + 1, :].partition_broadcast(128),
                      reads=["bcd"], writes=[f"delta_t{n % 2}"])
                vcopy(bc_rep[:, n * TC:(n + 1) * TC], delta_t[n % 2][:, :],
                      [f"delta_t{n % 2}"], ["bc_rep"])

        def delta_chain(cb):
            qd = cb % 2
            q3 = cb % 3
            dl, kdl = delta_t[qd], f"delta_t{qd}"
            du, kdu = du3[q3], f"du3_{q3}"
            mm(P3[:, :TC], dtT[:, cb * 128:(cb + 1) * 128], xdbl_b[:, :],
               start=True, stop=True, reads=["dtT", "xdbl_b"], writes=["P3"])
            act(dl[:, :], P3[:, :TC], ACT.Abs, ["P3", "dtb_sb"], [kdl],
                bias=dtb_sb[:, cb:cb + 1])
            act(du[:, :], dl[:, :], ACT.Exp, [kdl], [kdu], scale=-1.0)
            act(dl[:, :], du[:, :], ACT.Ln, [kdu], [kdl], bias=1.0)
            act(du[:, :], P3[:, :TC], ACT.Relu, ["P3", "dtb_sb"], [kdu],
                bias=dtb_sb[:, cb:cb + 1])
            vtt(ALU.add, dl[:, :], dl[:, :], du[:, :], [kdl, kdu], [kdl])
            vtt(ALU.mult, du[:, :], dl[:, :],
                u_cm[:, cb * TC:(cb + 1) * TC], [kdl, "u_cm"], [kdu])

        def emit_ypost(cb):
            q3 = cb % 3
            du, kdu = du3[q3], f"du3_{q3}"
            vstt(du[:, :], u_cm[:, cb * TC:(cb + 1) * TC],
                 Dv_sb[:, cb:cb + 1], du[:, :], ALU.mult, ALU.add,
                 ["u_cm", "Dv_sb", kdu], [kdu])
            vtt(ALU.mult, ybf[:, cb * TC:(cb + 1) * TC], du[:, :],
                zgs[:, cb * TC:(cb + 1) * TC], [kdu, "zgs"], ["ybf"])

        def emit_scan(j, interleave_next):
            delta_chain(0)
            for cb in range(NCC):
                qd = cb % 2
                q3 = cb % 3
                dl, kdl = delta_t[qd], f"delta_t{qd}"
                du, kdu = du3[q3], f"du3_{q3}"
                if cb + 1 < NCC:
                    delta_chain(cb + 1)
                if interleave_next:
                    if cb == 0:
                        emit_xin(j + 1)
                    emit_xz_group(j + 1, 2 * cb)
                    emit_xz_group(j + 1, 2 * cb + 1)
                for hf in range(4):
                    vtt(ALU.mult,
                        b_half[:, :].rearrange("p (n t) -> p n t", n=4),
                        bcast(du[:, :], 4).rearrange("p t n -> p n t"),
                        bc_rep[:, hf * 4 * TC:(hf + 1) * 4 * TC]
                        .rearrange("p (n t) -> p n t", n=4),
                        [kdu, "bc_rep"], ["b_half"])
                    for nn in range(4):
                        n = hf * 4 + nn
                        sl = n % 5
                        act(dA_q[:, sl * TC:(sl + 1) * TC], dl[:, :], ACT.Exp,
                            [kdl, "A_sb"], [f"dA{sl}"],
                            scale=A_sb[:, cb * DST + n:cb * DST + n + 1])
                        vscan(h_buf[:, n * TC:(n + 1) * TC],
                              dA_q[:, sl * TC:(sl + 1) * TC],
                              b_half[:, nn * TC:(nn + 1) * TC],
                              h_init[:, cb * DST + n:cb * DST + n + 1],
                              [f"dA{sl}", "b_half", "h_init"], ["h_buf"])
                if cb > 0:
                    emit_ypost(cb - 1)
                def _sv(e, cb=cb):
                    return e.tensor_copy(
                        h_init[:, cb * DST:(cb + 1) * DST],
                        h_buf[:, TC - 1::TC])
                p.emit("vector", _sv, reads=["h_buf"], writes=["h_init"])
                vtt(ALU.mult, h_buf[:, :], h_buf[:, :],
                    bc_rep[:, DST * TC:2 * DST * TC], ["h_buf", "bc_rep"],
                    ["h_buf"])
                vtt(ALU.add, g8[:, :], h_buf[:, 0:8 * TC],
                    h_buf[:, 8 * TC:16 * TC], ["h_buf"], ["g8"])
                for lvl in (4, 2):
                    gtt(g8[:, 0:lvl * TC], g8[:, 0:lvl * TC],
                        g8[:, lvl * TC:2 * lvl * TC], ["g8"], ["g8"])
                gtt(du[:, :], g8[:, 0:TC], g8[:, TC:2 * TC], ["g8"], [kdu])
            emit_ypost(NCC - 1)

        def emit_out(j):
            for tb in range(4):
                i = j * 4 + tb
                p.dma("sq", xt2[tb][:, :], xres[i * 128:(i + 1) * 128, :],
                      writes=[f"c1x{tb}"])
            for half in range(2):
                h0 = half * 512
                for c in range(NCC):
                    bm = c % 2
                    p.dma("sq", opw[bm][:, :], opT_d[c, :, h0:h0 + 512],
                          writes=[f"opw{bm}"], tag="opwld")
                    for tb in range(4):
                        ps, psk = (P0, "P0") if tb < 2 else (P1, "P1")
                        o0 = (tb % 2) * 512
                        mm(ps[:, o0:o0 + 512],
                           ybf[:, c * TC + tb * 128:c * TC + tb * 128 + 128],
                           opw[bm][:, :], start=(c == 0), stop=(c == NCC - 1),
                           reads=[f"opw{bm}", "ybf"], writes=[psk])
                for tb in range(4):
                    ps, psk = (P0, "P0") if tb < 2 else (P1, "P1")
                    o0 = (tb % 2) * 512
                    vtt(ALU.add, xt2[tb][:, h0:h0 + 512],
                        xt2[tb][:, h0:h0 + 512], ps[:, o0:o0 + 512],
                        [f"c1x{tb}", psk], [f"c1x{tb}"])
                    if half == 1:
                        i = j * 4 + tb
                        p.dma("sq", dstC[i * 128:(i + 1) * 128, :],
                              xt2[tb][:, :], reads=[f"c1x{tb}"], tag="xoutC")

        emit_xin(0)
        for m in range(2 * DIN // 128):
            emit_xz_group(0, m)
        emit_pre(0)
        for j in range(NCH):
            emit_scan(j, interleave_next=(j + 1 < NCH))
            if j + 1 < NCH:
                emit_pre(j + 1)
            emit_out(j)
        p.fence()
        cctx.close()

    if "D" in phases:
        ffn_phase("D", out_d)
    if fctx is not None:
        fctx.close()


    p.fence()
    if cut is not None:
        p.truncate_at_fence(cut)
    block = ctx.enter_context(nc.Block())
    p.finalize(block)
    ctx.close()
    return nc, taps


_BUILD_CACHE = {}


def _get_built(phases, debug):
    key = (phases, debug)
    if key not in _BUILD_CACHE:
        _BUILD_CACHE[key] = build(phases, debug)
    return _BUILD_CACHE[key]


def make_in_maps(inputs):
    pref = {
        "x": "L0_attn_a_x",
        "Wq": "L0_attn_a_Wq", "bq": "L0_attn_a_bq", "gq": "L0_attn_a_gq",
        "betaq": "L0_attn_a_betaq",
        "Wk": "L0_attn_a_Wk", "bk": "L0_attn_a_bk", "gk": "L0_attn_a_gk",
        "betak": "L0_attn_a_betak",
        "Wv": "L0_attn_a_Wv", "bv": "L0_attn_a_bv", "gv": "L0_attn_a_gv",
        "betav": "L0_attn_a_betav",
        "Wfq": "L0_attn_a_Wfq", "bfq": "L0_attn_a_bfq",
        "Wfk": "L0_attn_a_Wfk", "bfk": "L0_attn_a_bfk",
        "ga": "L0_attn_a_ga", "ba": "L0_attn_a_ba",
        "W1": "L0_attn_a_W1", "b1": "L0_attn_a_b1",
        "W2": "L0_attn_a_W2", "b2": "L0_attn_a_b2",
        "norm_w": "L0_attn_b_norm_w", "in_proj_w": "L0_attn_b_in_proj_w",
        "conv_w": "L0_attn_b_conv_w", "conv_b": "L0_attn_b_conv_b",
        "x_proj_w": "L0_attn_b_x_proj_w", "dt_proj_w": "L0_attn_b_dt_proj_w",
        "dt_proj_b": "L0_attn_b_dt_proj_b", "A_log": "L0_attn_b_A_log",
        "Dv": "L0_attn_b_D", "out_proj_w": "L0_attn_b_out_proj_w",
        "fw": "L0_ffn_weight", "fgamma": "L0_ffn_gamma", "fbeta": "L0_ffn_beta",
    }
    shared = {}
    for k, src in pref.items():
        if k == "x":
            continue
        shared[k] = np.ascontiguousarray(np.asarray(inputs[src], np.float32))
    shared["c_eye_f"] = np.eye(128, dtype=np.float32)
    shared["c_eye_b"] = np.eye(128, dtype=ml_dtypes.bfloat16)
    shared["c_ones_col_b"] = np.ones((128, 1), dtype=ml_dtypes.bfloat16)
    shared["c_ones_row_b"] = np.ones((1, 128), dtype=ml_dtypes.bfloat16)
    x = np.ascontiguousarray(np.asarray(inputs["L0_attn_a_x"], np.float32))
    return [dict(shared, x=np.ascontiguousarray(x[b])) for b in range(8)]


def kernel(**inputs):
    nc, _ = _get_built("ABCD", False)
    in_maps = make_in_maps(inputs)
    res = run_bass_kernel_spmd(nc, in_maps, core_ids=list(range(8)))
    return np.stack([np.asarray(res.results[b]["out"], np.float32)
                     for b in range(8)], axis=0)


# version 09
# speedup vs baseline: 1.0026x; 1.0026x over previous
import numpy as np
import ml_dtypes
from contextlib import ExitStack

import os as _os
import concourse.bass as bass
import concourse.mybir as mybir
from concourse.bass_utils import run_bass_kernel_spmd

F32 = mybir.dt.float32
BF16 = mybir.dt.bfloat16
ALU = mybir.AluOpType
ACT = mybir.ActivationFunctionType
AXX = mybir.AxisListType.X

L = 4096
D = 1024
NT = L // 128
NF = D // 128
H = 16
HD = 64
MLP = 4096
DIN = 2048
NCC = DIN // 128
DST = 16
DTR = 64
TC = 512
NCH = L // TC

PHASES = "A"
DEBUG = False


class Prog:

    ENG = ("sync", "tensor", "vector", "scalar", "gpsimd")
    NDMASEM = 20

    def __init__(self, nc, ctx):
        self.nc = nc
        self.sems = {e: ctx.enter_context(nc.semaphore("sem_" + e))
                     for e in self.ENG}
        for q in ("sq", "gq"):
            for i in range(self.NDMASEM):
                self.sems[f"{q}:{i}"] = ctx.enter_context(
                    nc.semaphore(f"sem_{q}_{i}"))
        self.plan = {e: [] for e in self.ENG}
        self.count = {k: 0 for k in self.sems}
        self.last_w = {}
        self.readers = {}
        self.watermark = {e: {} for e in self.ENG}
        self.region_sem = {"sq": {}, "gq": {}}
        self.sem_rr = {"sq": 0, "gq": 0}

    def _dma_sem(self, queue, region):
        m = self.region_sem[queue]
        if region not in m:
            m[region] = f"{queue}:{self.sem_rr[queue] % self.NDMASEM}"
            self.sem_rr[queue] += 1
        return m[region]

    def _waits_for(self, eng, reads, writes):
        waits = {}

        def need(src, val):
            if src == eng == "tensor":
                return
            if val <= 0:
                return
            wm = self.watermark[eng]
            if val > wm.get(src, 0) and val > waits.get(src, 0):
                waits[src] = val

        for r in reads:
            w = self.last_w.get(r)
            if w:
                need(*w)
        for wr in writes:
            w = self.last_w.get(wr)
            if w:
                need(*w)
            for e2, v in self.readers.get(wr, {}).items():
                need(e2, v)
        for sk, v in waits.items():
            self.watermark[eng][sk] = v
        return waits

    def emit(self, eng, fn, reads=(), writes=(), tag=""):
        waits = self._waits_for(eng, reads, writes)
        self.count[eng] += 1
        self.plan[eng].append((waits, fn, (eng, 1), tag))
        for r in reads:
            self.readers.setdefault(r, {})[eng] = self.count[eng]
        for wr in writes:
            self.last_w[wr] = (eng, self.count[eng])
            self.readers[wr] = {}

    def dma(self, queue, out_ap, in_ap, reads=(), writes=(), tag="",
            transpose=False, slow=False):
        eng = "sync" if queue == "sq" else "gpsimd"
        waits = self._waits_for(eng, reads, writes)
        primary = (list(writes) + list(reads) + [f"_anon_{queue}"])[0]
        sk = self._dma_sem(queue, primary)
        self.count[sk] += 16
        if transpose:
            def fn(e, o=out_ap, i=in_ap):
                return e.dma_start_transpose(o, i)
        elif slow:
            def fn(e, o=out_ap, i=in_ap):
                return e.dma_start(out=o, in_=i, allow_slow_non_contiguous=True)
        else:
            def fn(e, o=out_ap, i=in_ap):
                return e.dma_start(out=o, in_=i)
        self.plan[eng].append((waits, fn, (sk, 16), tag))
        for r in reads:
            self.readers.setdefault(r, {})[sk] = self.count[sk]
        for wr in writes:
            self.last_w[wr] = (sk, self.count[sk])
            self.readers[wr] = {}

    def fence(self):
        snap = dict(self.count)
        for eng in self.ENG:
            waits = {}
            for sk, v in snap.items():
                if sk == eng:
                    continue
                if v > self.watermark[eng].get(sk, 0):
                    waits[sk] = v
                    self.watermark[eng][sk] = v
            self.plan[eng].append((waits, None, None, "fence"))
        self.region_sem = {"sq": {}, "gq": {}}

    def truncate_at_fence(self, k):
        for eng in self.ENG:
            n = 0
            for idx, (w, fn, inc, tag) in enumerate(self.plan[eng]):
                if tag == "fence":
                    n += 1
                    if n == k:
                        self.plan[eng] = self.plan[eng][:idx + 1]
                        break

    def finalize(self, block):
        for eng in self.ENG:
            steps = self.plan[eng]

            def make_body(eng=eng, steps=steps):
                def body(e):
                    for waits, fn, inc, tag in steps:
                        for src, val in waits.items():
                            e.wait_ge(self.sems[src], val)
                        if fn is None:
                            continue
                        inst = fn(e)
                        inst.then_inc(self.sems[inc[0]], inc[1])
                return body
            getattr(block, eng)(make_body())


def bcast(ap, reps):
    return ap.rearrange("p n -> p n ()").broadcast_to(
        (ap.shape[0], ap.shape[1], reps))


def build(phases=None, debug=None, cut=None):
    phases = PHASES if phases is None else phases
    debug = DEBUG if debug is None else debug
    nc = bass.Bass("TRN2", target_bir_lowering=False)
    ctx = ExitStack()
    p = Prog(nc, ctx)
    skind = "ExternalOutput" if debug else "Internal"

    def din(name, shape, dt=F32):
        return nc.dram_tensor(name, shape, dt, kind="ExternalInput").ap()

    W = {}
    for n, sh in [
        ("x", [L, D]),
        ("Wq", [D, D]), ("bq", [D]), ("gq", [D]), ("betaq", [D]),
        ("Wk", [D, D]), ("bk", [D]), ("gk", [D]), ("betak", [D]),
        ("Wv", [D, D]), ("bv", [D]), ("gv", [D]), ("betav", [D]),
        ("Wfq", [D, D]), ("bfq", [D]), ("Wfk", [D, D]), ("bfk", [D]),
        ("ga", [D]), ("ba", [D]),
        ("W1", [D, MLP]), ("b1", [MLP]), ("W2", [MLP, D]), ("b2", [D]),
        ("norm_w", [D]), ("in_proj_w", [2 * DIN, D]),
        ("conv_w", [DIN, 1, 4]), ("conv_b", [DIN]),
        ("x_proj_w", [DTR + 2 * DST, DIN]), ("dt_proj_w", [DIN, DTR]),
        ("dt_proj_b", [DIN]), ("A_log", [DIN, DST]), ("Dv", [DIN]),
        ("out_proj_w", [D, DIN]),
        ("fw", [D, D]), ("fgamma", [D]), ("fbeta", [D]),
    ]:
        W[n] = din(n, sh)
    c_eye_f = din("c_eye_f", [128, 128])
    c_eye_b = din("c_eye_b", [128, 128], BF16)
    c_ones_col_b = din("c_ones_col_b", [128, 1], BF16)
    c_ones_row_b = din("c_ones_row_b", [1, 128], BF16)

    out_d = nc.dram_tensor("out", [L, D], F32, kind="ExternalOutput").ap()

    def dscr(name, shape, dt):
        return nc.dram_tensor(name, shape, dt, kind=skind).ap()

    xres = dscr("xres", [L, D], F32)
    zfm = dscr("zfm", [NF, 128, L], BF16)
    qfm = dscr("qfm", [NF, 128, L], BF16)
    kfm = dscr("kfm", [NF, 128, L], BF16)
    vtm = dscr("vtm", [NT, 128, D], BF16)
    pqfm = dscr("pqfm", [NF, 128, L], BF16)
    pkwtm = dscr("pkwtm", [NT, 128, D], BF16)
    afm = dscr("afm", [NF, 128, L], BF16)
    g1fm = dscr("g1fm", [MLP // 128, 128, L], BF16)
    rowb = nc.dram_tensor("rowb", [8, 4096], F32, kind=skind).ap()
    xnb = nc.dram_tensor("xnb", [L, D], F32, kind=skind).ap()
    ipT_d = nc.dram_tensor("ipT_d", [NF, 128, 2 * DIN], BF16, kind=skind).ap()
    opT_d = nc.dram_tensor("opT_d", [NCC, 128, D], BF16, kind=skind).ap()
    bc_d = nc.dram_tensor("bc_d", [2 * DST, TC], F32, kind=skind).ap()
    wqT_d = nc.dram_tensor("wqT_d", [NF, 128, 1024], BF16, kind=skind).ap()
    cc_in = nc.dram_tensor("cc_in", [1, 8], F32, kind="Internal").ap()
    cc_out = nc.dram_tensor("cc_out", [1, 8], F32, kind="Internal").ap()

    sb = lambda n, sh, dt=F32: ctx.enter_context(nc.sbuf_tensor(n, sh, dt))
    ones_col_b = sb("ones_col_b", [128, 1], BF16)
    ones_row_b = sb("ones_row_b", [1, 128], BF16)
    eye_f = sb("eye_f", [128, 128])
    eye_b = sb("eye_b", [128, 128], BF16)
    p.dma("sq", ones_col_b[:, :], c_ones_col_b[:, :], writes=["ones_col_b"])
    p.dma("sq", ones_row_b[:, :], c_ones_row_b[:, :], writes=["ones_row_b"])
    p.dma("sq", eye_f[:, :], c_eye_f[:, :], writes=["eye_f"])
    p.dma("sq", eye_b[:, :], c_eye_b[:, :], writes=["eye_b"])

    P0 = ctx.enter_context(nc.psum_tensor("P0", [128, 1024], F32))
    P1 = ctx.enter_context(nc.psum_tensor("P1", [128, 1024], F32))
    P2 = ctx.enter_context(nc.psum_tensor("P2", [128, 512], F32))
    P3 = ctx.enter_context(nc.psum_tensor("P3", [128, 512], F32))
    PR = ctx.enter_context(nc.psum_tensor("PR", [1, 1024], F32))

    st_sum = sb("st_sum", [128, NT])
    st_ssq = sb("st_ssq", [128, NT])
    st_rstd = sb("st_rstd", [128, NT])
    st_nmr = sb("st_nmr", [128, NT])
    st_tmp = sb("st_tmp", [128, NT])
    sq_scr = sb("sq_scr", [128, 1024])

    def act(out_ap, in_ap, func, reads, writes, bias=0.0, scale=1.0, accum=None,
            accw=()):
        def fn(e, o=out_ap, i=in_ap, b=bias, s=scale, a=accum, f=func):
            return e.activation(o, i, f, bias=b, scale=s, accum_out=a)
        p.emit("scalar", fn, reads=reads, writes=list(writes) + list(accw))

    def vtt(op, out_ap, a_ap, b_ap, reads, writes):
        def fn(e, o=out_ap, a=a_ap, b=b_ap, op=op):
            return e.tensor_tensor(o, a, b, op=op)
        p.emit("vector", fn, reads=reads, writes=writes)

    def vts(out_ap, in_ap, s1, s2, op0, op1, reads, writes):
        def fn(e, o=out_ap, i=in_ap, s1=s1, s2=s2, op0=op0, op1=op1):
            return e.tensor_scalar(o, i, s1, s2, op0=op0, op1=op1)
        p.emit("vector", fn, reads=reads, writes=writes)

    def vstt(out_ap, in0_ap, scalar, in1_ap, op0, op1, reads, writes):
        def fn(e, o=out_ap, i0=in0_ap, s=scalar, i1=in1_ap, op0=op0, op1=op1):
            return e.scalar_tensor_tensor(o, i0, s, i1, op0=op0, op1=op1)
        p.emit("vector", fn, reads=reads, writes=writes)

    def vcopy(out_ap, in_ap, reads, writes):
        def fn(e, o=out_ap, i=in_ap):
            return e.tensor_copy(o, i)
        p.emit("vector", fn, reads=reads, writes=writes)

    def vred(out_ap, in_ap, op, reads, writes, absval=None):
        def fn(e, o=out_ap, i=in_ap, op=op, av=absval):
            return e.tensor_reduce(o, i, axis=AXX, op=op, apply_absolute_value=av)
        p.emit("vector", fn, reads=reads, writes=writes)

    def vrecip(out_ap, in_ap, reads, writes):
        def fn(e, o=out_ap, i=in_ap):
            return e.reciprocal(o, i)
        p.emit("vector", fn, reads=reads, writes=writes)

    def vmemset(ap, val, writes):
        def fn(e, a=ap, v=val):
            return e.memset(a, v)
        p.emit("vector", fn, writes=writes)

    def vscan(out_ap, a_ap, b_ap, init, reads, writes):
        def fn(e, o=out_ap, a=a_ap, b=b_ap, i=init):
            return e.tensor_tensor_scan(o, a, b, i, op0=ALU.mult, op1=ALU.add)
        p.emit("vector", fn, reads=reads, writes=writes)

    def mm(out_ap, lhsT_ap, rhs_ap, start, stop, reads, writes, tile_position=None):
        def fn(e, o=out_ap, l=lhsT_ap, r=rhs_ap, st=start, sp=stop,
               tp=tile_position):
            return e.matmul(o, l, r, start=st, stop=sp, tile_position=tp)
        p.emit("tensor", fn, reads=reads, writes=writes)

    def transp(out_ap, in_ap, ident, reads, writes):
        def fn(e, o=out_ap, i=in_ap, id_=ident):
            return e.transpose(o, i, id_)
        p.emit("tensor", fn, reads=reads, writes=writes)

    def rstd_batch(n, sl, eps=1e-5):
        par = (sl.start or 0) % 2
        kS, kQ = f"st_sum{par}", f"st_ssq{par}"
        kR, kN, kT = f"st_rstd{par}", f"st_nmr{par}", f"st_tmp{par}"
        vts(st_tmp[:, sl], st_sum[:, sl], 1.0 / n, None, ALU.mult, ALU.bypass,
            [kS], [kT])
        act(st_nmr[:, sl], st_tmp[:, sl], ACT.Square, [kT], [kN])
        vstt(st_ssq[:, sl], st_ssq[:, sl], 1.0 / n, st_nmr[:, sl], ALU.mult,
             ALU.subtract, [kQ, kN], [kQ])
        vts(st_ssq[:, sl], st_ssq[:, sl], eps, None, ALU.add, ALU.bypass,
            [kQ], [kQ])
        vrecip(st_rstd[:, sl], st_ssq[:, sl], [kQ], [kR])
        act(st_rstd[:, sl], st_rstd[:, sl], ACT.Sqrt, [kR], [kR])
        act(st_nmr[:, sl], st_rstd[:, sl], ACT.Square, [kR], [kN])
        vtt(ALU.mult, st_nmr[:, sl], st_nmr[:, sl], st_ssq[:, sl],
            [kN, kQ], [kN])
        vts(st_nmr[:, sl], st_nmr[:, sl], -0.5, 1.5, ALU.mult, ALU.add,
            [kN], [kN])
        vtt(ALU.mult, st_rstd[:, sl], st_rstd[:, sl], st_nmr[:, sl],
            [kR, kN], [kR])
        vtt(ALU.mult, st_nmr[:, sl], st_tmp[:, sl], st_rstd[:, sl],
            [kT, kR], [kN])
        vts(st_nmr[:, sl], st_nmr[:, sl], -1.0, None, ALU.mult, ALU.bypass,
            [kN], [kN])

    def load_rep(actx, name, dram_vec):
        n = dram_vec.shape[0]
        t = actx.enter_context(nc.sbuf_tensor("rep_" + name, [128, n], F32))
        p.dma("sq", t[:, :],
              dram_vec.rearrange("(o n) -> o n", o=1).partition_broadcast(128),
              writes=["rep_" + name])
        return t

    def load_w_bf16(wctx, name, dram_w, rows, cols, stage, stage_key):
        tiles = []
        scw = stage.shape[1]
        for k in range(rows // 128):
            t = wctx.enter_context(
                nc.sbuf_tensor(f"wb_{name}_{k}", [128, cols], BF16))
            tiles.append(t)
            for c0 in range(0, cols, scw):
                c1 = min(cols, c0 + scw)
                p.dma("sq", stage[:, :c1 - c0],
                      dram_w[k * 128:(k + 1) * 128, c0:c1],
                      writes=[stage_key], tag="wload")
                vcopy(t[:, c0:c1], stage[:, :c1 - c0], [stage_key],
                      [f"wb_{name}_{k}"])
        return tiles

    def transpose_tm_tile(src_bf_ap, src_key, dst_dram, i, ev, ev_key):
        p2b = P2[:, :512].bitcast(BF16)
        for c in range(NF):
            transp(p2b[:, c * 128:(c + 1) * 128],
                   src_bf_ap[:, c * 128:(c + 1) * 128], eye_b[:, :],
                   [src_key, "eye_b"], ["P2"])
        act(ev[:, :1024], p2b[:, :1024], ACT.Copy, ["P2"], [ev_key])
        p.dma("gq", dst_dram[:, :, i * 128:(i + 1) * 128]
              .rearrange("c p t -> p c t"),
              ev[:, :1024].rearrange("p (c t) -> p c t", c=NF),
              reads=[ev_key], tag="fmst")

    taps = {}

    if "A" in phases:
        actx = ctx.enter_context(ExitStack())
        asb = lambda n, sh, dt=F32: actx.enter_context(nc.sbuf_tensor(n, sh, dt))

        xt = [asb(f"a0x{b}", [128, D]) for b in range(2)]
        zt = [asb(f"a0z{b}", [128, D]) for b in range(2)]
        ztb = asb("a0ztb", [128, D], BF16)
        ev = asb("a0ev", [128, 1024], BF16)
        def a0_s1(i):
            b = i % 2
            kx = f"a0x{b}"
            p.dma("sq", xt[b][:, :], W["x"][i * 128:(i + 1) * 128, :], writes=[kx])
            act(sq_scr[:, :D], xt[b][:, :], ACT.Square, [kx], ["sq_scr"],
                accum=st_ssq[:, i:i + 1], accw=[f"st_ssq{i % 2}"])
            act(zt[b][:, :], xt[b][:, :], ACT.Copy, [kx], [f"a0z{b}"],
                accum=st_sum[:, i:i + 1], accw=[f"st_sum{i % 2}"])

        def a0_s2(i):
            b = i % 2
            kx, kz = f"a0x{b}", f"a0z{b}"
            rstd_batch(D, slice(i, i + 1))
            act(zt[b][:, :], xt[b][:, :], ACT.Identity,
                [kx, f"st_rstd{i % 2}", f"st_nmr{i % 2}"],
                [kz], bias=st_nmr[:, i:i + 1], scale=st_rstd[:, i:i + 1])
            vtt(ALU.add, xt[b][:, :], xt[b][:, :], zt[b][:, :], [kx, kz], [kx])
            p.dma("sq", xres[i * 128:(i + 1) * 128, :], xt[b][:, :], reads=[kx])
            vcopy(ztb[:, :], zt[b][:, :], [kz], ["a0ztb"])
            transpose_tm_tile(ztb[:, :], "a0ztb", zfm, i, ev, "a0ev")

        a0_s1(0)
        for i in range(NT):
            if i + 1 < NT:
                a0_s1(i + 1)
            a0_s2(i)
        p.fence()

        fmincat = [asb(f"fmincat{b}", [128, NF * TC], BF16) for b in range(2)]
        fmin = [[fmincat[b][:, c * TC:(c + 1) * TC] for c in range(NF)]
                for b in range(2)]

        def load_fm_chunk(src_dram, j, b):
            p.dma("sq", fmincat[b][:, :].rearrange("p (c t) -> p c t", c=NF),
                  src_dram[:, :, j * TC:(j + 1) * TC].rearrange("c p t -> p c t"),
                  writes=[f"fmin{b}"], tag="fmld")

        wstage = asb("wstage", [128, 1024])
        brow_f = asb("brow_f", [1, 1024])
        brow_b = asb("brow_b", [1, 1024], BF16)
        ev_t = asb("ev_t", [128, 1024])
        ev_s = asb("ev_s", [128, 1024])
        ev_b = asb("ev_b", [128, 1024], BF16)
        evT = asb("evT", [128, 1024], BF16)
        qprobe = asb("qprobe", [1, 1024])
        qpb_f = asb("qpb_f", [128, 16])
        qpb = [asb(f"qpb{c}", [128, 16], BF16) for c in range(NF)]
        logit = asb("logit", [16, L])
        smax = asb("smax", [16, 1])
        srec = asb("srec", [16, 1])
        ssum = asb("ssum", [16, 1])
        sc0 = asb("sc0", [16, 1])
        negm = asb("negm", [16, 1])
        sc_tm = asb("sc_tm", [128, NT * 16])
        kin = asb("kin", [128, NF * TC], BF16)
        bfq_sb = asb("bfq_sb", [128, NF])
        pq_f = asb("pq_f", [128, TC])
        pq_b = asb("pq_b", [128, TC], BF16)
        pk_f = asb("pk_f", [128, 1024])
        pkw_t = asb("pkw_t", [128, 1024], BF16)
        ksum_row = asb("ksum_row", [1, 1024])
        ksum_fm_f = asb("ksum_fm_f", [128, NF])
        ksum_fm = asb("ksum_fm", [128, 64], BF16)
        sc0_fm = asb("sc0_fm", [128, NF])
        bfk_fm = asb("bfk_fm", [128, NF])
        pk0_fm = asb("pk0_fm", [128, NF])
        kv_sb = asb("kv_sb", [128, 1024], BF16)
        kvin = [asb(f"kvin{b}", [128, 1024], BF16) for b in range(2)]
        vvin = [asb(f"vvin{b}", [128, 1024], BF16) for b in range(2)]
        pqin = [asb(f"pqin{b}", [128, 1024], BF16) for b in range(2)]
        den_t = asb("den_t", [128, 16])
        att_t = asb("att_t", [128, 1024])
        a_t = asb("a_t", [128, 1024])
        a_tb = asb("a_tb", [128, 1024], BF16)
        x2 = [asb(f"x2{b}", [128, D]) for b in range(2)]

        def proj_tm(dst_dram, Wname, bias_name, g_name, bt_name,
                    probe_psum=False, do_transpose=True):
            with ExitStack() as wctx:
                wt = load_w_bf16(wctx, Wname, W[Wname], D, 1024, wstage, "wstage")
                g_rep = load_rep(wctx, "g" + Wname, W[g_name])
                bt_rep = load_rep(wctx, "bt" + Wname, W[bt_name])
                p.dma("sq", brow_f[:, :],
                      W[bias_name].rearrange("(o n) -> o n", o=1), writes=["brow_f"])
                vcopy(brow_b[:, :], brow_f[:, :], ["brow_f"], ["brow_b"])

                def mm_part(i):
                    j, tb = i // 4, i % 4
                    if tb == 0:
                        load_fm_chunk(zfm, j, j % 2)
                    bj = j % 2
                    ps, psk = (P0, "P0") if i % 2 == 0 else (P1, "P1")
                    t0 = tb * 128
                    for half in range(2):
                        h0 = half * 512
                        for c in range(NF):
                            mm(ps[:, h0:h0 + 512], fmin[bj][c][:, t0:t0 + 128],
                               wt[c][:, h0:h0 + 512], start=(c == 0), stop=False,
                               reads=[f"fmin{bj}", f"wb_{Wname}_{c}"],
                               writes=[psk])
                        mm(ps[:, h0:h0 + 512], ones_row_b[:, :],
                           brow_b[:, h0:h0 + 512], start=False, stop=True,
                           reads=["ones_row_b", "brow_b"], writes=[psk])

                def epi_part(i):
                    ps, psk = (P0, "P0") if i % 2 == 0 else (P1, "P1")
                    act(ev_t[:, :], ps[:, :1024], ACT.Copy, [psk], ["ev_t"],
                        accum=st_sum[:, i:i + 1], accw=[f"st_sum{i % 2}"])
                    act(ev_s[:, :], ev_t[:, :], ACT.Square, ["ev_t"],
                        ["ev_s"], accum=st_ssq[:, i:i + 1], accw=[f"st_ssq{i % 2}"])
                    rstd_batch(D, slice(i, i + 1))
                    act(ev_t[:, :], ev_t[:, :], ACT.Identity,
                        ["ev_t", f"st_rstd{i % 2}", f"st_nmr{i % 2}"], ["ev_t"],
                        bias=st_nmr[:, i:i + 1], scale=st_rstd[:, i:i + 1])
                    vtt(ALU.mult, ev_s[:, :], ev_t[:, :], g_rep[:, :],
                        ["ev_t", "rep_g" + Wname], ["ev_s"])
                    vtt(ALU.add, ev_b[:, :], ev_s[:, :], bt_rep[:, :],
                        ["ev_s", "rep_bt" + Wname], ["ev_b"])
                    if probe_psum:
                        for half in range(2):
                            h0 = half * 512
                            mm(PR[:1, h0:h0 + 512], ones_col_b[:, :],
                               ev_b[:, h0:h0 + 512],
                               start=(i == 0), stop=(i == NT - 1),
                               reads=["ones_col_b", "ev_b"], writes=["PR"])
                    if do_transpose:
                        transpose_tm_tile(ev_b[:, :], "ev_b", dst_dram, i,
                                          evT, "evT")
                    else:
                        vcopy(evT[:, :], ev_b[:, :], ["ev_b"], ["evT"])
                        p.dma("gq", dst_dram[i, :, :], evT[:, :],
                              reads=["evT"], tag="vtm_st")

                for i in range(NT):
                    mm_part(i)
                    if i > 0:
                        epi_part(i - 1)
                epi_part(NT - 1)
                p.fence()

        proj_tm(qfm, "Wq", "bq", "gq", "betaq", probe_psum=True)
        act(qprobe[:, :], PR[:1, :1024], ACT.Copy, ["PR"], ["qprobe"],
            scale=1.0 / (L * 8.0))
        proj_tm(kfm, "Wk", "bk", "gk", "betak")
        proj_tm(vtm, "Wv", "bv", "gv", "betav", do_transpose=False)

        p.dma("gq", rowb[0:1, 0:1024], qprobe[:, :], reads=["qprobe"],
              writes=["rb0"])
        for c in range(NF):
            vmemset(qpb_f[:, :], 0.0, ["qpb_f"])
            for hh in range(2):
                h = 2 * c + hh
                p.dma("sq", qpb_f[hh * 64:(hh + 1) * 64, h:h + 1],
                      rowb[0:1, c * 128 + hh * 64:c * 128 + (hh + 1) * 64]
                      .rearrange("o n -> (o n) ()"),
                      reads=["rb0"], writes=["qpb_f"], slow=True)
            vcopy(qpb[c][:, :], qpb_f[:, :], ["qpb_f"], [f"qpb{c}"])

        def emit_logit_chunk(j):
            p.dma("sq", kin[:, :].rearrange("p (c t) -> p c t", c=NF),
                  kfm[:, :, j * TC:(j + 1) * TC].rearrange("c p t -> p c t"),
                  writes=["kin"], tag="kld")
            for c in range(NF):
                mm(P2[:16, :512], qpb[c][:, :], kin[:, c * TC:(c + 1) * TC],
                   start=(c == 0), stop=(c == NF - 1),
                   reads=["kin", f"qpb{c}"], writes=["P2"])
            vcopy(logit[:, j * TC:(j + 1) * TC], P2[:16, :512], ["P2"],
                  ["logit"])

        def emit_softmax():
            vred(smax[:, :], logit[:, :], ALU.max, ["logit"], ["smax"])
            vts(smax[:, :], smax[:, :], 0.0, None, ALU.max, ALU.bypass, ["smax"],
                ["smax"])
            vts(negm[:, :], smax[:, :], -1.0, None, ALU.mult, ALU.bypass, ["smax"],
                ["negm"])
            act(logit[:, :], logit[:, :], ACT.Exp, ["logit", "negm"], ["logit"],
                bias=negm[:, 0:1], accum=ssum[:, 0:1], accw=["ssum"])
            act(sc0[:, :], negm[:, :], ACT.Exp, ["negm"], ["sc0"])
            vtt(ALU.add, ssum[:, :], ssum[:, :], sc0[:, :], ["ssum", "sc0"], ["ssum"])
            vrecip(srec[:, :], ssum[:, :], ["ssum"], ["srec"])

            def _scmul(e):
                return e.tensor_scalar(logit[:, :], logit[:, :], srec[:, 0:1], None,
                                       op0=ALU.mult)
            p.emit("vector", _scmul, reads=["logit", "srec"], writes=["logit"])
            vtt(ALU.mult, sc0[:, :], sc0[:, :], srec[:, :], ["sc0", "srec"], ["sc0"])
            for i in range(NT):
                transp(P3[:, :16], logit[:, i * 128:(i + 1) * 128], eye_f[:16, :16],
                       ["logit", "eye_f"], ["P3"])
                vcopy(sc_tm[:, i * 16:(i + 1) * 16], P3[:128, :16], ["P3"], ["sc_tm"])

        with ExitStack() as wctx:
            wsb = lambda n, sh, dt=F32: wctx.enter_context(nc.sbuf_tensor(n, sh, dt))
            wfq = load_w_bf16(wctx, "Wfq", W["Wfq"], D, 1024, wstage, "wstage")
            for c in range(NF):
                p.dma("sq", bfq_sb[:, c:c + 1],
                      W["bfq"][c * 128:(c + 1) * 128].rearrange("(p o) -> p o", o=1),
                      writes=["bfq_sb"], slow=True)

            def mm_part(k):
                j, m = k // NF, k % NF
                if m == 0:
                    load_fm_chunk(qfm, j, j % 2)
                bj = j % 2
                ps, psk = (P0, "P0") if k % 2 == 0 else (P1, "P1")
                for c in range(NF):
                    mm(ps[:, :TC], wfq[c][:, m * 128:(m + 1) * 128],
                       fmin[bj][c][:, :], start=(c == 0), stop=(c == NF - 1),
                       reads=[f"fmin{bj}", f"wb_Wfq_{c}"], writes=[psk])

            def epi_part(k):
                j, m = k // NF, k % NF
                ps, psk = (P0, "P0") if k % 2 == 0 else (P1, "P1")
                act(pq_f[:, :], ps[:, :TC], ACT.Tanh, [psk, "bfq_sb"],
                    ["pq_f"], bias=bfq_sb[:, m:m + 1])
                vts(pq_b[:, :], pq_f[:, :], 1.0, None, ALU.add, ALU.bypass,
                    ["pq_f"], ["pq_b"])
                p.dma("gq", pqfm[m, :, j * TC:(j + 1) * TC], pq_b[:, :],
                      reads=["pq_b"], tag="pq_st")

            for k in range(NCH * NF):
                mm_part(k)
                if k % NF == 1 and k // NF < NCH:
                    emit_logit_chunk(k // NF)
                if k > 0:
                    epi_part(k - 1)
            epi_part(NCH * NF - 1)
            emit_softmax()
            p.fence()

        with ExitStack() as wctx:
            wsb = lambda n, sh, dt=F32: wctx.enter_context(nc.sbuf_tensor(n, sh, dt))
            wfk = load_w_bf16(wctx, "Wfk", W["Wfk"], D, 1024, wstage, "wstage")
            p.dma("sq", brow_f[:, :], W["bfk"].rearrange("(o n) -> o n", o=1),
                  writes=["brow_f"])
            vcopy(brow_b[:, :], brow_f[:, :], ["brow_f"], ["brow_b"])

            def mm_part(i):
                j, tb = i // 4, i % 4
                if tb == 0:
                    load_fm_chunk(kfm, j, j % 2)
                bj = j % 2
                ps, psk = (P0, "P0") if i % 2 == 0 else (P1, "P1")
                t0 = tb * 128
                for half in range(2):
                    h0 = half * 512
                    for c in range(NF):
                        mm(ps[:, h0:h0 + 512], fmin[bj][c][:, t0:t0 + 128],
                           wfk[c][:, h0:h0 + 512], start=(c == 0), stop=False,
                           reads=[f"fmin{bj}", f"wb_Wfk_{c}"], writes=[psk])
                    mm(ps[:, h0:h0 + 512], ones_row_b[:, :],
                       brow_b[:, h0:h0 + 512], start=False, stop=True,
                       reads=["ones_row_b", "brow_b"], writes=[psk])

            def epi_part(i):
                ps, psk = (P0, "P0") if i % 2 == 0 else (P1, "P1")
                act(pk_f[:, :], ps[:, :1024], ACT.Tanh, [psk], ["pk_f"])
                vstt(pkw_t[:, :].rearrange("p (h e) -> p h e", h=16),
                     pk_f[:, :].rearrange("p (h e) -> p h e", h=16), 1.0,
                     bcast(sc_tm[:, i * 16:(i + 1) * 16], 64),
                     ALU.add, ALU.mult, ["pk_f", "sc_tm"], ["pkw_t"])
                for half in range(2):
                    h0 = half * 512
                    mm(PR[:1, h0:h0 + 512], ones_col_b[:, :],
                       pkw_t[:, h0:h0 + 512],
                       start=(i == 0), stop=(i == NT - 1),
                       reads=["ones_col_b", "pkw_t"], writes=["PR"])
                p.dma("gq", pkwtm[i, :, :], pkw_t[:, :], reads=["pkw_t"],
                      tag="pkw_st")

            for i in range(NT):
                mm_part(i)
                if i > 0:
                    epi_part(i - 1)
            epi_part(NT - 1)
            act(ksum_row[:, :], PR[:1, :1024], ACT.Copy, ["PR"], ["ksum_row"])
            p.dma("gq", rowb[2:3, 0:1024], ksum_row[:, :], reads=["ksum_row"],
                  writes=["rb2"])
            for c in range(NF):
                p.dma("sq", ksum_fm_f[:, c:c + 1],
                      rowb[2:3, c * 128:(c + 1) * 128].rearrange("o (p q) -> (o p) q", q=1),
                      reads=["rb2"], writes=["ksum_fm_f"], slow=True)
            p.dma("gq", rowb[1:2, 0:16], sc0[:, :], reads=["sc0"], writes=["rb1"],
                  slow=True)
            for c in range(NF):
                for hh in range(2):
                    h = 2 * c + hh
                    p.dma("sq", sc0_fm[hh * 64:(hh + 1) * 64, c:c + 1],
                          rowb[1:2, h:h + 1].partition_broadcast(64),
                          reads=["rb1"], writes=["sc0_fm"], slow=True)
            for c in range(NF):
                p.dma("sq", bfk_fm[:, c:c + 1],
                      W["bfk"][c * 128:(c + 1) * 128].rearrange("(p o) -> p o", o=1),
                      writes=["bfk_fm"], slow=True)
            act(pk0_fm[:, :], bfk_fm[:, :], ACT.Tanh, ["bfk_fm"], ["pk0_fm"])
            vts(pk0_fm[:, :], pk0_fm[:, :], 1.0, None, ALU.add, ALU.bypass,
                ["pk0_fm"], ["pk0_fm"])
            vtt(ALU.mult, pk0_fm[:, :], pk0_fm[:, :], sc0_fm[:, :],
                ["pk0_fm", "sc0_fm"], ["pk0_fm"])
            vtt(ALU.add, ksum_fm_f[:, :], ksum_fm_f[:, :], pk0_fm[:, :],
                ["ksum_fm_f", "pk0_fm"], ["ksum_fm_f"])
            vmemset(ksum_fm[:, :], 0.0, ["ksum_fm"])
            for h in range(H):
                c, hh = h // 2, h % 2
                vcopy(ksum_fm[hh * 64:(hh + 1) * 64, 8 * c + hh:8 * c + hh + 1],
                      ksum_fm_f[hh * 64:(hh + 1) * 64, c:c + 1],
                      ["ksum_fm_f"], ["ksum_fm"])
            p.fence()

        kv_acc = asb("kv_acc", [128, 512])
        vmemset(kv_acc[:, :], 0.0, ["kv_acc"])
        for i in range(NT):
            b = i % 2
            ps, psk = (P0, "P0") if i % 2 == 0 else (P1, "P1")
            p.dma("sq", kvin[b][:, :], pkwtm[i, :, :], writes=[f"kvin{b}"])
            p.dma("sq", vvin[b][:, :], vtm[i, :, :], writes=[f"vvin{b}"])
            for h in range(H):
                c, hh = h // 2, h % 2
                mm(ps[hh * 64:(hh + 1) * 64, c * 64:(c + 1) * 64],
                   kvin[b][:, h * 64:(h + 1) * 64],
                   vvin[b][:, h * 64:(h + 1) * 64],
                   start=True, stop=True,
                   reads=[f"kvin{b}", f"vvin{b}"], writes=[psk],
                   tile_position=(0, hh * 64))
            vtt(ALU.add, kv_acc[:, :], kv_acc[:, :], ps[:, :512],
                ["kv_acc", psk], ["kv_acc"])
        vmemset(kv_sb[:, :], 0.0, ["kv_sb"])
        for h in range(H):
            c, hh = h // 2, h % 2
            vcopy(kv_sb[hh * 64:(hh + 1) * 64,
                        c * 128 + hh * 64:c * 128 + hh * 64 + 64],
                  kv_acc[hh * 64:(hh + 1) * 64, c * 64:(c + 1) * 64],
                  ["kv_acc"], ["kv_sb"])
        p.fence()

        with ExitStack() as wctx:
            ga_rep = load_rep(wctx, "ga", W["ga"])
            ba_rep = load_rep(wctx, "ba", W["ba"])

            def mm_part(i):
                b = i % 2
                p.dma("sq", pqin[b][:, :].rearrange("p (c t) -> p c t", c=NF),
                      pqfm[:, :, i * 128:(i + 1) * 128].rearrange("c p t -> p c t"),
                      writes=[f"pqin{b}"])
                p.dma("sq", x2[b][:, :], xres[i * 128:(i + 1) * 128, :],
                      writes=[f"x2{b}"])
                ps, psk = (P0, "P0") if i % 2 == 0 else (P1, "P1")
                for c in range(NF):
                    lhsT = pqin[b][:, c * 128:(c + 1) * 128]
                    mm(ps[:, c * 128:(c + 1) * 128], lhsT,
                       kv_sb[:, c * 128:(c + 1) * 128],
                       start=True, stop=True, reads=[f"pqin{b}", "kv_sb"],
                       writes=[psk])
                    mm(P2[:, 4 * c:4 * c + 2], lhsT,
                       ksum_fm[:, 8 * c:8 * c + 2],
                       start=True, stop=True, reads=[f"pqin{b}", "ksum_fm"],
                       writes=["P2"])
                vcopy(den_t[:, :],
                      P2[:, 0:32].rearrange("p (c two) -> p c two", two=4)[:, :, 0:2],
                      ["P2"], ["den_t"])
                vts(den_t[:, :], den_t[:, :], 1e-6, None, ALU.add, ALU.bypass,
                    ["den_t"], ["den_t"])
                vrecip(den_t[:, :], den_t[:, :], ["den_t"], ["den_t"])
                vtt(ALU.mult, att_t[:, :].rearrange("p (h e) -> p h e", h=16),
                    ps[:, :1024].rearrange("p (h e) -> p h e", h=16),
                    bcast(den_t[:, :], 64), [psk, "den_t"], ["att_t"])

            def epi_part(i):
                b = i % 2
                vred(st_sum[:, i:i + 1], att_t[:, :], ALU.add, ["att_t"],
                     [f"st_sum{i % 2}"])
                act(sq_scr[:, :D], att_t[:, :], ACT.Square, ["att_t"],
                    ["sq_scr"], accum=st_ssq[:, i:i + 1], accw=[f"st_ssq{i % 2}"])
                rstd_batch(D, slice(i, i + 1))
                act(a_t[:, :], att_t[:, :], ACT.Identity,
                    ["att_t", f"st_rstd{i % 2}", f"st_nmr{i % 2}"], ["a_t"],
                    bias=st_nmr[:, i:i + 1], scale=st_rstd[:, i:i + 1])
                vtt(ALU.mult, a_t[:, :], a_t[:, :], ga_rep[:, :],
                    ["a_t", "rep_ga"], ["a_t"])
                vtt(ALU.add, a_t[:, :], a_t[:, :], ba_rep[:, :],
                    ["a_t", "rep_ba"], ["a_t"])
                vtt(ALU.add, x2[b][:, :], x2[b][:, :], a_t[:, :],
                    [f"x2{b}", "a_t"], [f"x2{b}"])
                p.dma("sq", xres[i * 128:(i + 1) * 128, :], x2[b][:, :],
                      reads=[f"x2{b}"])
                vcopy(a_tb[:, :], a_t[:, :], ["a_t"], ["a_tb"])
                transpose_tm_tile(a_tb[:, :], "a_tb", afm, i, evT, "evT")

            for i in range(NT):
                mm_part(i)
                epi_part(i)
            p.fence()

        actx.close()
        actx = ctx.enter_context(ExitStack())
        asb = lambda n, sh, dt=F32: actx.enter_context(nc.sbuf_tensor(n, sh, dt))
        wstage = asb("wstage2", [128, 1024])
        fmin = [[asb(f"fmin2_{b}_{c}", [128, TC], BF16) for c in range(NF)]
                for b in range(2)]

        def load_fm_chunk(src_dram, j, b):
            for c in range(NF):
                p.dma("sq", fmin[b][c][:, :], src_dram[c, :, j * TC:(j + 1) * TC],
                      writes=[f"fmin{b}"], tag="fmld")

        with ExitStack() as wctx:
            w1 = load_w_bf16(wctx, "W1", W["W1"], D, MLP, wstage, "wstage")
            b1_sb = wctx.enter_context(nc.sbuf_tensor("b1_sb", [128, MLP // 128], F32))
            g1_b = wctx.enter_context(nc.sbuf_tensor("g1_b", [128, TC], BF16))
            for c in range(MLP // 128):
                p.dma("sq", b1_sb[:, c:c + 1],
                      W["b1"][c * 128:(c + 1) * 128].rearrange("(p o) -> p o", o=1),
                      writes=["b1_sb"], slow=True)

            def mm_part(k):
                j, m = k // (MLP // 128), k % (MLP // 128)
                if m == 0:
                    load_fm_chunk(afm, j, j % 2)
                bj = j % 2
                ps, psk = (P0, "P0") if k % 2 == 0 else (P1, "P1")
                for c in range(NF):
                    mm(ps[:, :TC], w1[c][:, m * 128:(m + 1) * 128],
                       fmin[bj][c][:, :], start=(c == 0), stop=(c == NF - 1),
                       reads=[f"fmin{bj}", f"wb_W1_{c}"], writes=[psk])

            def epi_part(k):
                j, m = k // (MLP // 128), k % (MLP // 128)
                ps, psk = (P0, "P0") if k % 2 == 0 else (P1, "P1")
                act(g1_b[:, :], ps[:, :TC], ACT.Gelu, [psk, "b1_sb"],
                    ["g1_b"], bias=b1_sb[:, m:m + 1])
                p.dma("gq", g1fm[m, :, j * TC:(j + 1) * TC], g1_b[:, :],
                      reads=["g1_b"], tag="g1_st")

            NK = NCH * (MLP // 128)
            for k in range(NK):
                mm_part(k)
                if k > 0:
                    epi_part(k - 1)
            epi_part(NK - 1)
            p.fence()

        with ExitStack() as wctx:
            w2 = load_w_bf16(wctx, "W2", W["W2"], MLP, 1024, wstage, "wstage")
            b2_rep = load_rep(wctx, "b2", W["b2"])
            wsb = lambda n, sh, dt=F32: wctx.enter_context(nc.sbuf_tensor(n, sh, dt))
            g1cat = [wsb(f"g1cat{b}", [128, MLP], BF16) for b in range(2)]
            g1in = [[g1cat[b][:, c * 128:(c + 1) * 128]
                     for c in range(MLP // 128)] for b in range(2)]
            x3 = [wsb(f"x3{b}", [128, D]) for b in range(2)]
            mo_t = [wsb(f"mo_t{b}", [128, 1024]) for b in range(2)]

            def mm_part(i):
                b = i % 2
                p.dma("sq", g1cat[b][:, :].rearrange("p (c t) -> p c t", c=MLP // 128),
                      g1fm[:, :, i * 128:(i + 1) * 128].rearrange("c p t -> p c t"),
                      writes=[f"g1in{b}"], tag="g1ld")
                p.dma("sq", x3[b][:, :], xres[i * 128:(i + 1) * 128, :],
                      writes=[f"x3{b}"])
                ps, psk = (P0, "P0") if i % 2 == 0 else (P1, "P1")
                for half in range(2):
                    h0 = half * 512
                    for c in range(MLP // 128):
                        mm(ps[:, h0:h0 + 512], g1in[b][c][:, :],
                           w2[c][:, h0:h0 + 512],
                           start=(c == 0), stop=(c == MLP // 128 - 1),
                           reads=[f"g1in{b}", f"wb_W2_{c}"], writes=[psk])

            def epi_part(i):
                b = i % 2
                ps, psk = (P0, "P0") if i % 2 == 0 else (P1, "P1")
                vtt(ALU.add, mo_t[b][:, :], ps[:, :1024], b2_rep[:, :],
                    [psk, "rep_b2"], [f"mo_t{b}"])
                vtt(ALU.add, x3[b][:, :], x3[b][:, :], mo_t[b][:, :],
                    [f"x3{b}", f"mo_t{b}"], [f"x3{b}"])
                dst = xres if len(phases) > 1 else out_d
                p.dma("sq", dst[i * 128:(i + 1) * 128, :], x3[b][:, :],
                      reads=[f"x3{b}"], tag="xout")

            for i in range(NT):
                mm_part(i)
                if i > 0:
                    epi_part(i - 1)
            epi_part(NT - 1)
            p.fence()
        actx.close()

    fctx = None
    alpha_sb = None
    if "B" in phases or "D" in phases:
        fctx = ctx.enter_context(ExitStack())
        alpha_sb = fctx.enter_context(nc.sbuf_tensor("alpha_sb", [1, 16], F32))
        fpctx = ExitStack()
        fsb = lambda n, sh, dt=F32: fpctx.enter_context(nc.sbuf_tensor(n, sh, dt))
        wqT = [fsb(f"wqTp_{c}", [128, 1024], BF16) for c in range(NF)]
        fstage = fsb("fstage", [128, 1024])
        fabs = fsb("fabs", [128, 1024])
        fmask = fsb("fmask", [128, 1024])
        fsgn = fsb("fsgn", [128, 1024])
        fsums = fsb("fsums", [128, 8])
        fred = fsb("fred", [128, 16])
        frow = fsb("frow", [1, 16])
        ones_col_f = fsb("ones_col_f", [128, 16])
        vmemset(ones_col_f[:, :], 1.0, ["ones_col_f"])

        fwsm = fsb("fwsm", [128, 1024], BF16)
        evF = fsb("evF", [128, 1024], BF16)
        dstag = fsb("dstag", [128, 16])
        vmemset(fred[:, :], 0.0, ["fred"])
        for t in range(NF):
            p.dma("sq", fstage[:, :], W["fw"][t * 128:(t + 1) * 128, :],
                  writes=["fstage"])
            act(fabs[:, :], fstage[:, :], ACT.Abs, ["fstage"], ["fabs"],
                accum=fsums[:, t:t + 1], accw=["fsums"])
        vred(fred[:, 0:1], fsums[:, :], ALU.add, ["fsums"], ["fred"])
        def mmf(out_ap, lhsT_ap, rhs_ap, reads, writes):
            mm(out_ap, lhsT_ap, rhs_ap, start=True, stop=True, reads=reads,
               writes=writes)
        mmf(PR[:1, 0:1], ones_col_f[:, 0:1], fred[:, 0:1],
            ["fred", "ones_col_f"], ["PR"])
        act(frow[:, 0:1], PR[:1, 0:1], ACT.Copy, ["PR"], ["frow"])
        vts(frow[:, 0:1], frow[:, 0:1], 0.7 / (1024.0 * 1024.0), None,
            ALU.mult, ALU.bypass, ["frow"], ["frow"])
        p.dma("gq", rowb[3:4, 0:1], frow[:, 0:1], reads=["frow"], writes=["rb3"],
              slow=True)
        p.dma("sq", dstag[:, 0:1], rowb[3:4, 0:1].partition_broadcast(128),
              reads=["rb3"], writes=["dstag"], slow=True)
        for t in range(NF):
            p.dma("sq", fstage[:, :], W["fw"][t * 128:(t + 1) * 128, :],
                  writes=["fstage"])
            act(fabs[:, :], fstage[:, :], ACT.Abs, ["fstage"], ["fabs"])
            def _mk(e, t=t):
                return e.tensor_single_scalar(fmask[:, :], fabs[:, :],
                                              dstag[:, 0:1], op=ALU.is_gt)
            p.emit("vector", _mk, reads=["fabs", "dstag"], writes=["fmask"])
            vtt(ALU.mult, sq_scr[:, :1024], fabs[:, :], fmask[:, :],
                ["fabs", "fmask"], ["sq_scr"])
            vred(fsums[:, t:t + 1], sq_scr[:, :1024], ALU.add, ["sq_scr"],
                 ["fsums"])
            vred(fred[:, t:t + 1], fmask[:, :], ALU.add, ["fmask"], ["fred"])
            act(fsgn[:, :], fstage[:, :], ACT.Sign, ["fstage"], ["fsgn"])
            vtt(ALU.mult, fwsm[:, :], fsgn[:, :], fmask[:, :],
                ["fsgn", "fmask"], ["fwsm"])
            p2b = P2[:, :512].bitcast(BF16)
            for c in range(NF):
                transp(p2b[:, c * 128:(c + 1) * 128],
                       fwsm[:, c * 128:(c + 1) * 128], eye_b[:, :],
                       ["fwsm", "eye_b"], ["P2"])
            act(evF[:, :1024], p2b[:, :1024], ACT.Copy, ["P2"], ["evF"])
            for c in range(NF):
                vcopy(wqT[c][:, t * 128:(t + 1) * 128],
                      evF[:, c * 128:(c + 1) * 128], ["evF"], [f"wqTp_{c}"])
        vred(fred[:, 8:9], fsums[:, :], ALU.add, ["fsums"], ["fred"])
        mmf(PR[:1, 0:16], ones_col_f[:, 0:1], fred[:, 0:16],
            ["fred", "ones_col_f"], ["PR"])
        act(frow[:, :], PR[:1, 0:16], ACT.Copy, ["PR"], ["frow"])
        vred(alpha_sb[:, 0:1], frow[:, 0:8], ALU.add, ["frow"], ["alpha_sb"])
        vts(alpha_sb[:, 0:1], alpha_sb[:, 0:1], 1.0, None, ALU.max, ALU.bypass,
            ["alpha_sb"], ["alpha_sb"])
        vrecip(alpha_sb[:, 1:2], alpha_sb[:, 0:1], ["alpha_sb"], ["alpha_sb"])
        def _amul(e):
            return e.tensor_tensor(alpha_sb[:, 2:3], alpha_sb[:, 1:2],
                                   frow[:, 8:9], op=ALU.mult)
        p.emit("vector", _amul, reads=["alpha_sb", "frow"], writes=["alpha_sb"])
        for c in range(NF):
            p.dma("gq", wqT_d[c, :, :], wqT[c][:, :], reads=[f"wqTp_{c}"],
                  tag="wqT_st")
        p.fence()
        fpctx.close()

    def ffn_phase(tag, dst_final):
        with ExitStack() as qctx:
            qsb = lambda n, sh, dt=F32: qctx.enter_context(
                nc.sbuf_tensor("f" + tag + n, sh, dt))
            gam_rep = load_rep(qctx, "fg" + tag, W["fgamma"])
            bet_rep = load_rep(qctx, "fb" + tag, W["fbeta"])
            xt = [qsb(f"x{b}", [128, D]) for b in range(2)]
            zt = [qsb(f"z{b}", [128, D]) for b in range(2)]
            amax_st = qsb("amax", [128, NT])
            gmax = qsb("gmax", [1, 8])
            scl = qsb("scl", [1, 4])
            inv_s = qsb("invs", [128, 1])
            osc = qsb("osc", [128, 1])
            cbuf = qsb("cbuf", [128, D])
            bbuf = qsb("bbuf", [128, D])
            qb = qsb("qb", [128, D], BF16)
            evQ = qsb("evQ", [128, 1024], BF16)
            qcat = [qctx.enter_context(
                nc.sbuf_tensor(f"f{tag}qcat{b}", [128, NF * TC], BF16))
                for b in range(2)]
            qin = [[qcat[b][:, c * TC:(c + 1) * TC] for c in range(NF)]
                   for b in range(2)]
            mo = qsb("mo", [128, D])
            wqTcat = qctx.enter_context(
                nc.sbuf_tensor(f"f{tag}wqTcat", [128, NF * 1024], BF16))
            wqT = [wqTcat[:, c * 1024:(c + 1) * 1024] for c in range(NF)]
            p.dma("sq", wqTcat[:, :].rearrange("p (c t) -> p c t", c=NF),
                  wqT_d[:, :, :].rearrange("c p t -> p c t"),
                  writes=[f"f{tag}wqT"])
            for i in range(NT):
                b = i % 2
                kx, kz = f"f{tag}x{b}", f"f{tag}z{b}"
                p.dma("sq", xt[b][:, :], xres[i * 128:(i + 1) * 128, :],
                      writes=[kx])
                act(sq_scr[:, :D], xt[b][:, :], ACT.Square, [kx], ["sq_scr"],
                    accum=st_ssq[:, i:i + 1], accw=[f"st_ssq{i % 2}"])
                act(zt[b][:, :], xt[b][:, :], ACT.Copy, [kx], [kz],
                    accum=st_sum[:, i:i + 1], accw=[f"st_sum{i % 2}"])
                rstd_batch(D, slice(i, i + 1))
                act(zt[b][:, :], xt[b][:, :], ACT.Identity,
                    [kx, f"st_rstd{i % 2}", f"st_nmr{i % 2}"], [kz],
                    bias=st_nmr[:, i:i + 1], scale=st_rstd[:, i:i + 1])
                vtt(ALU.mult, zt[b][:, :], zt[b][:, :], gam_rep[:, :],
                    [kz, "rep_fg" + tag], [kz])
                vtt(ALU.add, zt[b][:, :], zt[b][:, :], bet_rep[:, :],
                    [kz, "rep_fb" + tag], [kz])
                vred(amax_st[:, i:i + 1], zt[b][:, :], ALU.max, [kz],
                     ["f" + tag + "amax"], absval=True)
                p.dma("sq", xnb[i * 128:(i + 1) * 128, :], zt[b][:, :],
                      reads=[kz], tag="xnb_st")
            vred(amax_st[:, 0:1], amax_st[:, :], ALU.max, ["f" + tag + "amax"],
                 ["f" + tag + "amax"])
            transp(P3[0:1, 0:128], amax_st[:, 0:1], eye_f[:, :],
                   ["f" + tag + "amax", "eye_f"], ["P3"])
            vred(scl[:, 0:1], P3[:1, 0:128], ALU.max, ["P3"], ["fscl" + tag])
            def _brd(e):
                return e.tensor_copy(gmax[:, :], scl[:, 0:1].broadcast_to((1, 8)))
            p.emit("vector", _brd, reads=["fscl" + tag], writes=["fgmax" + tag])
            p.dma("gq", cc_in[:, :], gmax[:, :], reads=["fgmax" + tag],
                  writes=["fccin" + tag])
            def _cc(e):
                return e.collective_compute(
                    "AllReduce", ALU.max, replica_groups=[list(range(8))],
                    ins=[cc_in.opt()], outs=[cc_out.opt()])
            p.emit("gpsimd", _cc, reads=["fccin" + tag], writes=["fccout" + tag],
                   tag="cc")
            p.dma("sq", gmax[:, :], cc_out[:, :], reads=["fccout" + tag],
                  writes=["fgmax" + tag])
            vts(scl[:, 1:2], gmax[:, 0:1], 1e-8, None, ALU.max, ALU.bypass,
                ["fgmax" + tag], ["fscl" + tag])
            vrecip(scl[:, 2:3], scl[:, 1:2], ["fscl" + tag], ["fscl" + tag])
            vts(scl[:, 2:3], scl[:, 2:3], 127.0, None, ALU.mult, ALU.bypass,
                ["fscl" + tag], ["fscl" + tag])
            vts(scl[:, 1:2], scl[:, 1:2], 1.0 / 127.0, None, ALU.mult,
                ALU.bypass, ["fscl" + tag], ["fscl" + tag])
            def _osc(e):
                return e.tensor_tensor(scl[:, 3:4], scl[:, 1:2],
                                       alpha_sb[:, 2:3], op=ALU.mult)
            p.emit("vector", _osc, reads=["fscl" + tag, "alpha_sb"],
                   writes=["fscl" + tag])
            p.dma("gq", rowb[4:5, 0:4], scl[:, :], reads=["fscl" + tag],
                  writes=["rb4" + tag], slow=True)
            p.dma("sq", inv_s[:, :], rowb[4:5, 2:3].partition_broadcast(128),
                  reads=["rb4" + tag], writes=["finvs" + tag], slow=True)
            p.dma("sq", osc[:, :], rowb[4:5, 3:4].partition_broadcast(128),
                  reads=["rb4" + tag], writes=["fosc" + tag], slow=True)
            def quant_tile(i):
                b = i % 2
                kz = f"f{tag}z{b}"
                j, tb = i // 4, i % 4
                bj = j % 2
                p.dma("sq", zt[b][:, :], xnb[i * 128:(i + 1) * 128, :],
                      writes=[kz])
                act(cbuf[:, :], zt[b][:, :], ACT.Identity,
                    [kz, "finvs" + tag], ["fcb"], scale=inv_s[:, 0:1])
                vts(cbuf[:, :], cbuf[:, :], -127.0, 127.0, ALU.max, ALU.min,
                    ["fcb"], ["fcb"])
                vcopy(bbuf[:, :], cbuf[:, :], ["fcb"], ["fbb"])
                for k in [128.0, 64.0, 32.0, 16.0, 8.0, 4.0, 2.0, 1.0]:
                    vts(sq_scr[:, :D], bbuf[:, :], k - 128.5, None, ALU.is_ge,
                        ALU.bypass, ["fbb"], ["sq_scr"])
                    vstt(bbuf[:, :], sq_scr[:, :D], -k, bbuf[:, :], ALU.mult,
                         ALU.add, ["sq_scr", "fbb"], ["fbb"])
                vtt(ALU.subtract, cbuf[:, :], cbuf[:, :], bbuf[:, :],
                    ["fcb", "fbb"], ["fcb"])
                vts(qb[:, :], cbuf[:, :], -128.0, None, ALU.add, ALU.bypass,
                    ["fcb"], ["fqb"])
                p2b = P2[:, :512].bitcast(BF16)
                for c in range(NF):
                    transp(p2b[:, c * 128:(c + 1) * 128],
                           qb[:, c * 128:(c + 1) * 128], eye_b[:, :],
                           ["fqb", "eye_b"], ["P2"])
                act(qcat[bj][:, :].rearrange("p (c t) -> p c t", c=NF)
                    [:, :, tb * 128:tb * 128 + 128]
                    if False else
                    qcat[bj][:, tb * 128:tb * 128 + 128]
                    .rearrange("p (o t) -> p o t", o=1)
                    .broadcast_to((128, 1, 128)) if False else
                    qcat[bj][:, :]
                    .rearrange("p (c t) -> p c t", c=NF)[:, :, tb * 128:(tb + 1) * 128],
                    p2b[:, :1024].rearrange("p (c t) -> p c t", c=NF),
                    ACT.Copy, ["P2"], [f"f{tag}qin{bj}"])

            def mm_chunk(j):
                bj = j % 2
                for tb in range(4):
                    i = j * 4 + tb
                    b = i % 2
                    kx = f"f{tag}x{b}"
                    p.dma("sq", xt[b][:, :], xres[i * 128:(i + 1) * 128, :],
                          writes=[kx])
                    ps, psk = (P0, "P0") if i % 2 == 0 else (P1, "P1")
                    t0 = tb * 128
                    for half in range(2):
                        h0 = half * 512
                        for c in range(NF):
                            mm(ps[:, h0:h0 + 512], qin[bj][c][:, t0:t0 + 128],
                               wqT[c][:, h0:h0 + 512], start=(c == 0),
                               stop=(c == NF - 1),
                               reads=[f"f{tag}qin{bj}", f"f{tag}wqT"],
                               writes=[psk])
                    act(mo[:, :], ps[:, :1024], ACT.Identity,
                        [psk, "fosc" + tag], ["fmo"], scale=osc[:, 0:1])
                    vtt(ALU.add, xt[b][:, :], xt[b][:, :], mo[:, :],
                        [kx, "fmo"], [kx])
                    p.dma("sq", dst_final[i * 128:(i + 1) * 128, :], xt[b][:, :],
                          reads=[kx], tag="xout")

            for j in range(NCH):
                for tb in range(4):
                    quant_tile(j * 4 + tb)
                if j > 0:
                    mm_chunk(j - 1)
            mm_chunk(NCH - 1)
            p.fence()

    if "B" in phases:
        ffn_phase("B", xres if ("C" in phases or "D" in phases) else out_d)

    if "C" in phases:
        cctx = ctx.enter_context(ExitStack())
        csb = lambda n, sh, dt=F32: cctx.enter_context(nc.sbuf_tensor(n, sh, dt))
        dtT = csb("dtT", [64, DIN], BF16)
        xpwT = [csb(f"xpwT_{t}", [128, 96], BF16) for t in range(NCC)]
        A_sb = csb("A_sb", [128, NCC * DST])
        conv_sb = csb("conv_sb", [128, NCC * 4])
        cvb_sb = csb("cvb_sb", [128, NCC])
        dtb_sb = csb("dtb_sb", [128, NCC])
        Dv_sb = csb("Dv_sb", [128, NCC])
        cpctx = ExitStack()
        csb = lambda n, sh, dt=F32: cpctx.enter_context(nc.sbuf_tensor(n, sh, dt))
        cstage = csb("cstage", [128, 1024])
        cstb = csb("cstb", [128, 1024], BF16)
        evC = csb("evC", [128, 1024], BF16)
        for o in range(2 * DIN // 128):
            p.dma("sq", cstage[:, :], W["in_proj_w"][o * 128:(o + 1) * 128, :],
                  writes=["cstage"])
            vcopy(cstb[:, :], cstage[:, :], ["cstage"], ["cstb"])
            p2b = P2[:, :512].bitcast(BF16)
            for c in range(NF):
                transp(p2b[:, c * 128:(c + 1) * 128],
                       cstb[:, c * 128:(c + 1) * 128], eye_b[:, :],
                       ["cstb", "eye_b"], ["P2"])
            act(evC[:, :1024], p2b[:, :1024], ACT.Copy, ["P2"], ["evC"])
            for c in range(NF):
                p.dma("gq", ipT_d[c, :, o * 128:(o + 1) * 128],
                      evC[:, c * 128:(c + 1) * 128], reads=["evC"], tag="ipT_st")
        for o in range(D // 128):
            p.dma("sq", cstage[:, :1024],
                  W["out_proj_w"][o * 128:(o + 1) * 128, 0:1024],
                  writes=["cstage"])
            p.dma("sq", sq_scr[:, :1024],
                  W["out_proj_w"][o * 128:(o + 1) * 128, 1024:2048],
                  writes=["sq_scr"])
            vcopy(cstb[:, :], cstage[:, :1024], ["cstage"], ["cstb"])
            p2b = P2[:, :512].bitcast(BF16)
            for c in range(8):
                transp(p2b[:, c * 128:(c + 1) * 128],
                       cstb[:, c * 128:(c + 1) * 128], eye_b[:, :],
                       ["cstb", "eye_b"], ["P2"])
            act(evC[:, :1024], p2b[:, :1024], ACT.Copy, ["P2"], ["evC"])
            for c in range(8):
                p.dma("gq", opT_d[c, :, o * 128:(o + 1) * 128],
                      evC[:, c * 128:(c + 1) * 128], reads=["evC"], tag="opT_st")
            vcopy(cstb[:, :], sq_scr[:, :1024], ["sq_scr"], ["cstb"])
            for c in range(8):
                transp(p2b[:, c * 128:(c + 1) * 128],
                       cstb[:, c * 128:(c + 1) * 128], eye_b[:, :],
                       ["cstb", "eye_b"], ["P2"])
            act(evC[:, :1024], p2b[:, :1024], ACT.Copy, ["P2"], ["evC"])
            for c in range(8):
                p.dma("gq", opT_d[8 + c, :, o * 128:(o + 1) * 128],
                      evC[:, c * 128:(c + 1) * 128], reads=["evC"], tag="opT_st")
        for t in range(NCC):
            p.dma("sq", cstage[:, :64], W["dt_proj_w"][t * 128:(t + 1) * 128, :],
                  writes=["cstage"])
            vcopy(cstb[:, :64], cstage[:, :64], ["cstage"], ["cstb"])
            transp(P2[:, 0:64].bitcast(BF16)[0:64, :],
                   cstb[:, :64], eye_b[:, :], ["cstb", "eye_b"], ["P2"])
            vcopy(dtT[:, t * 128:(t + 1) * 128],
                  P2[:, 0:64].bitcast(BF16)[0:64, :], ["P2"], ["dtT"])
        xpw_sb = csb("xpw_sb", [96, DIN])
        xpw_sbb = csb("xpw_sbb", [96, DIN], BF16)
        p.dma("sq", xpw_sb[:, :], W["x_proj_w"][:, :], writes=["xpw_sb"])
        vcopy(xpw_sbb[:, :], xpw_sb[:, :], ["xpw_sb"], ["xpw_sbb"])
        for t in range(NCC):
            transp(P2[:, 0:48].bitcast(BF16)[:, 0:96],
                   xpw_sbb[:, t * 128:(t + 1) * 128], eye_b[:96, :96],
                   ["xpw_sbb", "eye_b"], ["P2"])
            vcopy(xpwT[t][:, :], P2[:, 0:48].bitcast(BF16)[:, 0:96], ["P2"],
                  [f"xpwT_{t}"])
        for t in range(NCC):
            p.dma("sq", cstage[:, :DST], W["A_log"][t * 128:(t + 1) * 128, :],
                  writes=["cstage"])
            act(A_sb[:, t * DST:(t + 1) * DST], cstage[:, :DST], ACT.Exp,
                ["cstage"], ["A_sb"])
        vts(A_sb[:, :], A_sb[:, :], -1.0, None, ALU.mult, ALU.bypass,
            ["A_sb"], ["A_sb"])
        for t in range(NCC):
            p.dma("sq", conv_sb[:, t * 4:(t + 1) * 4],
                  W["conv_w"].rearrange("c o k -> (c o) k")[t * 128:(t + 1) * 128, :],
                  writes=["conv_sb"])
        for t in range(NCC):
            p.dma("sq", cvb_sb[:, t:t + 1],
                  W["conv_b"][t * 128:(t + 1) * 128].rearrange("(p o) -> p o", o=1),
                  writes=["cvb_sb"], slow=True)
            p.dma("sq", dtb_sb[:, t:t + 1],
                  W["dt_proj_b"][t * 128:(t + 1) * 128].rearrange("(p o) -> p o", o=1),
                  writes=["dtb_sb"], slow=True)
            p.dma("sq", Dv_sb[:, t:t + 1],
                  W["Dv"][t * 128:(t + 1) * 128].rearrange("(p o) -> p o", o=1),
                  writes=["Dv_sb"], slow=True)
        p.fence()
        cpctx.close()
        csb = lambda n, sh, dt=F32: cctx.enter_context(nc.sbuf_tensor(n, sh, dt))

        with ExitStack() as wctx:
            nw_rep = load_rep(wctx, "nw", W["norm_w"])
            xt = [wctx.enter_context(nc.sbuf_tensor(f"c0x{b}", [128, D], F32))
                  for b in range(2)]
            zt = [wctx.enter_context(nc.sbuf_tensor(f"c0z{b}", [128, D], F32))
                  for b in range(2)]
            xrb = wctx.enter_context(nc.sbuf_tensor("c0xrb", [128, D], BF16))
            evC0 = wctx.enter_context(nc.sbuf_tensor("c0ev", [128, 1024], BF16))
            for i in range(NT):
                b = i % 2
                kx, kz = f"c0x{b}", f"c0z{b}"
                p.dma("sq", xt[b][:, :], xres[i * 128:(i + 1) * 128, :],
                      writes=[kx])
                act(sq_scr[:, :D], xt[b][:, :], ACT.Square, [kx], ["sq_scr"],
                    accum=st_ssq[:, i:i + 1], accw=[f"st_ssq{i % 2}"])
                act(zt[b][:, :], xt[b][:, :], ACT.Copy, [kx], [kz],
                    accum=st_sum[:, i:i + 1], accw=[f"st_sum{i % 2}"])
                rstd_batch(D, slice(i, i + 1))
                act(zt[b][:, :], xt[b][:, :], ACT.Identity,
                    [kx, f"st_rstd{i % 2}", f"st_nmr{i % 2}"], [kz],
                    bias=st_nmr[:, i:i + 1], scale=st_rstd[:, i:i + 1])
                vtt(ALU.add, xt[b][:, :], xt[b][:, :], zt[b][:, :], [kx, kz],
                    [kx])
                p.dma("sq", xres[i * 128:(i + 1) * 128, :], xt[b][:, :],
                      reads=[kx])
                act(sq_scr[:, :D], zt[b][:, :], ACT.Square, [kz], ["sq_scr"],
                    accum=st_ssq[:, i:i + 1], accw=[f"st_ssq{i % 2}"])
                vts(st_tmp[:, i:i + 1], st_ssq[:, i:i + 1], 1.0 / D, 1e-5,
                    ALU.mult, ALU.add, [f"st_ssq{i % 2}"], [f"st_tmp{i % 2}"])
                vrecip(st_rstd[:, i:i + 1], st_tmp[:, i:i + 1],
                       [f"st_tmp{i % 2}"], [f"st_rstd{i % 2}"])
                act(st_rstd[:, i:i + 1], st_rstd[:, i:i + 1], ACT.Sqrt,
                    [f"st_rstd{i % 2}"], [f"st_rstd{i % 2}"])
                act(st_nmr[:, i:i + 1], st_rstd[:, i:i + 1], ACT.Square,
                    [f"st_rstd{i % 2}"], [f"st_nmr{i % 2}"])
                vtt(ALU.mult, st_nmr[:, i:i + 1], st_nmr[:, i:i + 1],
                    st_tmp[:, i:i + 1], [f"st_nmr{i % 2}", f"st_tmp{i % 2}"],
                    [f"st_nmr{i % 2}"])
                vts(st_nmr[:, i:i + 1], st_nmr[:, i:i + 1], -0.5, 1.5,
                    ALU.mult, ALU.add, [f"st_nmr{i % 2}"], [f"st_nmr{i % 2}"])
                vtt(ALU.mult, st_rstd[:, i:i + 1], st_rstd[:, i:i + 1],
                    st_nmr[:, i:i + 1], [f"st_rstd{i % 2}", f"st_nmr{i % 2}"],
                    [f"st_rstd{i % 2}"])
                act(zt[b][:, :], zt[b][:, :], ACT.Identity,
                    [kz, f"st_rstd{i % 2}"], [kz], scale=st_rstd[:, i:i + 1])
                vtt(ALU.mult, xrb[:, :], zt[b][:, :], nw_rep[:, :],
                    [kz, "rep_nw"], ["c0xrb"])
                transpose_tm_tile(xrb[:, :], "c0xrb", qfm, i, evC0, "c0ev")
            p.fence()

        TCp = TC + 3
        xp = csb("xp", [128, NCC * TCp], BF16)
        u_cm = csb("u_cm", [128, NCC * TC])
        zgs = csb("zgs", [128, NCC * TC], BF16)
        bc_rep = csb("bc_rep", [128, 2 * DST * TC], BF16)
        b_half = csb("b_half", [128, 4 * TC], BF16)
        h_buf = csb("h_buf", [128, DST * TC], BF16)
        h_init = csb("h_init", [128, NCC * DST])
        ybf = csb("ybf", [128, NCC * TC], BF16)
        delta_t = [csb(f"delta_t{q}", [128, TC]) for q in range(2)]
        du3 = [csb(f"du3_{q}", [128, TC]) for q in range(3)]
        dA_q = csb("dA_q", [128, 5 * TC], BF16)
        g8 = csb("g8", [128, 8 * TC])
        xdbl_b = csb("xdbl_b", [64, TC], BF16)
        xincat = csb("xincat", [128, NF * TC], BF16)
        xin = [[xincat[:, c * TC:(c + 1) * TC] for c in range(NF)]]
        ipw = [csb(f"ipw{b}", [128, 8 * 128], BF16) for b in range(1)]
        opw = [csb(f"opw{b}", [128, 512], BF16) for b in range(2)]
        xt2 = [csb(f"c1x{b}", [128, D]) for b in range(4)]
        vmemset(h_init[:, :], 0.0, ["h_init"])
        vmemset(xp[:, :], 0.0, ["xp"])
        dstC = xres if "D" in phases else out_d

        def gtt(out_ap, a_ap, b_ap, reads, writes):
            def fn(e, o=out_ap, a=a_ap, b=b_ap):
                return e.tensor_tensor(o, a, b, op=ALU.add)
            p.emit("gpsimd", fn, reads=reads, writes=writes)

        def emit_xz_group(j, m):
            p.dma("sq", ipw[0][:, :].rearrange("p (c t) -> p c t", c=NF),
                  ipT_d[:, :, m * 128:(m + 1) * 128].rearrange("c p t -> p c t"),
                  writes=["ipw0"], tag="ipwld")
            for c in range(NF):
                mm(P2[:, :TC], ipw[0][:, c * 128:(c + 1) * 128],
                   xin[0][c][:, :], start=(c == 0), stop=(c == NF - 1),
                   reads=["ipw0", "cxin0"], writes=["P2"])
            if m < NCC:
                act(xp[:, m * TCp + 3:(m + 1) * TCp], P2[:, :TC],
                    ACT.Copy, ["P2"], ["xp"])
            else:
                cb = m - NCC
                act(zgs[:, cb * TC:(cb + 1) * TC], P2[:, :TC],
                    ACT.Silu, ["P2"], ["zgs"])

        def emit_xin(j):
            p.dma("sq", xincat[:, :].rearrange("p (c t) -> p c t", c=NF),
                  qfm[:, :, j * TC:(j + 1) * TC].rearrange("c p t -> p c t"),
                  writes=["cxin0"], tag="xnld")

        def emit_pre(j):
            for cb in range(NCC):
                x0 = cb * TCp
                q = cb % 3
                dq, kq = du3[q], f"du3_{q}"
                def _c0(e, x0=x0, cb=cb, dq=dq):
                    return e.tensor_scalar(dq[:, :], xp[:, x0:x0 + TC],
                                           conv_sb[:, 4 * cb:4 * cb + 1], None,
                                           op0=ALU.mult)
                p.emit("vector", _c0, reads=["xp", "conv_sb"], writes=[kq])
                for jj in range(1, 4):
                    vstt(dq[:, :], xp[:, x0 + jj:x0 + jj + TC],
                         conv_sb[:, 4 * cb + jj:4 * cb + jj + 1], dq[:, :],
                         ALU.mult, ALU.add, ["xp", "conv_sb", kq], [kq])
                act(u_cm[:, cb * TC:(cb + 1) * TC], dq[:, :], ACT.Silu,
                    [kq, "cvb_sb"], ["u_cm"], bias=cvb_sb[:, cb:cb + 1])
            for cb in range(NCC):
                def _tl(e, cb=cb):
                    return e.tensor_copy(xp[:, cb * TCp:cb * TCp + 3],
                                         xp[:, cb * TCp + TC:cb * TCp + TC + 3])
                p.emit("vector", _tl, reads=["xp"], writes=["xp"])
            for cb in range(NCC):
                vcopy(b_half[:, 0:TC], u_cm[:, cb * TC:(cb + 1) * TC], ["u_cm"],
                      ["b_half"])
                mm(P2[0:96, :TC], xpwT[cb][:, :], b_half[:, 0:TC],
                   start=(cb == 0), stop=(cb == NCC - 1),
                   reads=[f"xpwT_{cb}", "b_half"], writes=["P2"])
            xdbl = g8[0:96, 7 * TC:8 * TC]
            act(xdbl[:, :], P2[0:96, :TC], ACT.Copy, ["P2"], ["g8"])
            vcopy(xdbl_b[:, :], xdbl[0:64, :], ["g8"], ["xdbl_b"])
            p.dma("gq", bc_d[:, :], xdbl[64:96, :], reads=["g8"],
                  writes=["bcd"])
            for n in range(2 * DST):
                p.dma("sq", delta_t[n % 2][:, :],
                      bc_d[n:n + 1, :].partition_broadcast(128),
                      reads=["bcd"], writes=[f"delta_t{n % 2}"])
                vcopy(bc_rep[:, n * TC:(n + 1) * TC], delta_t[n % 2][:, :],
                      [f"delta_t{n % 2}"], ["bc_rep"])

        def delta_chain(cb):
            qd = cb % 2
            q3 = cb % 3
            dl, kdl = delta_t[qd], f"delta_t{qd}"
            du, kdu = du3[q3], f"du3_{q3}"
            mm(P3[:, :TC], dtT[:, cb * 128:(cb + 1) * 128], xdbl_b[:, :],
               start=True, stop=True, reads=["dtT", "xdbl_b"], writes=["P3"])
            act(dl[:, :], P3[:, :TC], ACT.Abs, ["P3", "dtb_sb"], [kdl],
                bias=dtb_sb[:, cb:cb + 1])
            act(du[:, :], dl[:, :], ACT.Exp, [kdl], [kdu], scale=-1.0)
            act(dl[:, :], du[:, :], ACT.Ln, [kdu], [kdl], bias=1.0)
            act(du[:, :], P3[:, :TC], ACT.Relu, ["P3", "dtb_sb"], [kdu],
                bias=dtb_sb[:, cb:cb + 1])
            vtt(ALU.add, dl[:, :], dl[:, :], du[:, :], [kdl, kdu], [kdl])
            vtt(ALU.mult, du[:, :], dl[:, :],
                u_cm[:, cb * TC:(cb + 1) * TC], [kdl, "u_cm"], [kdu])

        def emit_ypost(cb):
            q3 = cb % 3
            du, kdu = du3[q3], f"du3_{q3}"
            vstt(du[:, :], u_cm[:, cb * TC:(cb + 1) * TC],
                 Dv_sb[:, cb:cb + 1], du[:, :], ALU.mult, ALU.add,
                 ["u_cm", "Dv_sb", kdu], [kdu])
            vtt(ALU.mult, ybf[:, cb * TC:(cb + 1) * TC], du[:, :],
                zgs[:, cb * TC:(cb + 1) * TC], [kdu, "zgs"], ["ybf"])

        def emit_scan(j, interleave_next):
            delta_chain(0)
            for cb in range(NCC):
                qd = cb % 2
                q3 = cb % 3
                dl, kdl = delta_t[qd], f"delta_t{qd}"
                du, kdu = du3[q3], f"du3_{q3}"
                if cb + 1 < NCC:
                    delta_chain(cb + 1)
                if interleave_next:
                    if cb == 0:
                        emit_xin(j + 1)
                    emit_xz_group(j + 1, 2 * cb)
                    emit_xz_group(j + 1, 2 * cb + 1)
                for hf in range(4):
                    vtt(ALU.mult,
                        b_half[:, :].rearrange("p (n t) -> p n t", n=4),
                        bcast(du[:, :], 4).rearrange("p t n -> p n t"),
                        bc_rep[:, hf * 4 * TC:(hf + 1) * 4 * TC]
                        .rearrange("p (n t) -> p n t", n=4),
                        [kdu, "bc_rep"], ["b_half"])
                    for nn in range(4):
                        n = hf * 4 + nn
                        sl = n % 5
                        act(dA_q[:, sl * TC:(sl + 1) * TC], dl[:, :], ACT.Exp,
                            [kdl, "A_sb"], [f"dA{sl}"],
                            scale=A_sb[:, cb * DST + n:cb * DST + n + 1])
                        vscan(h_buf[:, n * TC:(n + 1) * TC],
                              dA_q[:, sl * TC:(sl + 1) * TC],
                              b_half[:, nn * TC:(nn + 1) * TC],
                              h_init[:, cb * DST + n:cb * DST + n + 1],
                              [f"dA{sl}", "b_half", "h_init"], ["h_buf"])
                if cb > 0:
                    emit_ypost(cb - 1)
                def _sv(e, cb=cb):
                    return e.tensor_copy(
                        h_init[:, cb * DST:(cb + 1) * DST],
                        h_buf[:, TC - 1::TC])
                p.emit("vector", _sv, reads=["h_buf"], writes=["h_init"])
                vtt(ALU.mult, h_buf[:, :], h_buf[:, :],
                    bc_rep[:, DST * TC:2 * DST * TC], ["h_buf", "bc_rep"],
                    ["h_buf"])
                vtt(ALU.add, g8[:, :], h_buf[:, 0:8 * TC],
                    h_buf[:, 8 * TC:16 * TC], ["h_buf"], ["g8"])
                for lvl in (4, 2):
                    gtt(g8[:, 0:lvl * TC], g8[:, 0:lvl * TC],
                        g8[:, lvl * TC:2 * lvl * TC], ["g8"], ["g8"])
                gtt(du[:, :], g8[:, 0:TC], g8[:, TC:2 * TC], ["g8"], [kdu])
            emit_ypost(NCC - 1)

        def emit_out(j):
            for tb in range(4):
                i = j * 4 + tb
                p.dma("sq", xt2[tb][:, :], xres[i * 128:(i + 1) * 128, :],
                      writes=[f"c1x{tb}"])
            for half in range(2):
                h0 = half * 512
                for c in range(NCC):
                    bm = c % 2
                    p.dma("sq", opw[bm][:, :], opT_d[c, :, h0:h0 + 512],
                          writes=[f"opw{bm}"], tag="opwld")
                    for tb in range(4):
                        ps, psk = (P0, "P0") if tb < 2 else (P1, "P1")
                        o0 = (tb % 2) * 512
                        mm(ps[:, o0:o0 + 512],
                           ybf[:, c * TC + tb * 128:c * TC + tb * 128 + 128],
                           opw[bm][:, :], start=(c == 0), stop=(c == NCC - 1),
                           reads=[f"opw{bm}", "ybf"], writes=[psk])
                for tb in range(4):
                    ps, psk = (P0, "P0") if tb < 2 else (P1, "P1")
                    o0 = (tb % 2) * 512
                    vtt(ALU.add, xt2[tb][:, h0:h0 + 512],
                        xt2[tb][:, h0:h0 + 512], ps[:, o0:o0 + 512],
                        [f"c1x{tb}", psk], [f"c1x{tb}"])
                    if half == 1:
                        i = j * 4 + tb
                        p.dma("sq", dstC[i * 128:(i + 1) * 128, :],
                              xt2[tb][:, :], reads=[f"c1x{tb}"], tag="xoutC")

        emit_xin(0)
        for m in range(2 * DIN // 128):
            emit_xz_group(0, m)
        emit_pre(0)
        for j in range(NCH):
            emit_scan(j, interleave_next=(j + 1 < NCH))
            if j + 1 < NCH:
                emit_pre(j + 1)
            emit_out(j)
        p.fence()
        cctx.close()

    if "D" in phases:
        ffn_phase("D", out_d)
    if fctx is not None:
        fctx.close()


    p.fence()
    if cut is not None:
        p.truncate_at_fence(cut)
    block = ctx.enter_context(nc.Block())
    p.finalize(block)
    ctx.close()
    return nc, taps


_BUILD_CACHE = {}


def _get_built(phases, debug):
    key = (phases, debug)
    if key not in _BUILD_CACHE:
        _BUILD_CACHE[key] = build(phases, debug)
    return _BUILD_CACHE[key]


def make_in_maps(inputs):
    pref = {
        "x": "L0_attn_a_x",
        "Wq": "L0_attn_a_Wq", "bq": "L0_attn_a_bq", "gq": "L0_attn_a_gq",
        "betaq": "L0_attn_a_betaq",
        "Wk": "L0_attn_a_Wk", "bk": "L0_attn_a_bk", "gk": "L0_attn_a_gk",
        "betak": "L0_attn_a_betak",
        "Wv": "L0_attn_a_Wv", "bv": "L0_attn_a_bv", "gv": "L0_attn_a_gv",
        "betav": "L0_attn_a_betav",
        "Wfq": "L0_attn_a_Wfq", "bfq": "L0_attn_a_bfq",
        "Wfk": "L0_attn_a_Wfk", "bfk": "L0_attn_a_bfk",
        "ga": "L0_attn_a_ga", "ba": "L0_attn_a_ba",
        "W1": "L0_attn_a_W1", "b1": "L0_attn_a_b1",
        "W2": "L0_attn_a_W2", "b2": "L0_attn_a_b2",
        "norm_w": "L0_attn_b_norm_w", "in_proj_w": "L0_attn_b_in_proj_w",
        "conv_w": "L0_attn_b_conv_w", "conv_b": "L0_attn_b_conv_b",
        "x_proj_w": "L0_attn_b_x_proj_w", "dt_proj_w": "L0_attn_b_dt_proj_w",
        "dt_proj_b": "L0_attn_b_dt_proj_b", "A_log": "L0_attn_b_A_log",
        "Dv": "L0_attn_b_D", "out_proj_w": "L0_attn_b_out_proj_w",
        "fw": "L0_ffn_weight", "fgamma": "L0_ffn_gamma", "fbeta": "L0_ffn_beta",
    }
    shared = {}
    for k, src in pref.items():
        if k == "x":
            continue
        shared[k] = np.ascontiguousarray(np.asarray(inputs[src], np.float32))
    shared["c_eye_f"] = np.eye(128, dtype=np.float32)
    shared["c_eye_b"] = np.eye(128, dtype=ml_dtypes.bfloat16)
    shared["c_ones_col_b"] = np.ones((128, 1), dtype=ml_dtypes.bfloat16)
    shared["c_ones_row_b"] = np.ones((1, 128), dtype=ml_dtypes.bfloat16)
    x = np.ascontiguousarray(np.asarray(inputs["L0_attn_a_x"], np.float32))
    return [dict(shared, x=np.ascontiguousarray(x[b])) for b in range(8)]


def kernel(**inputs):
    nc, _ = _get_built("ABCD", False)
    in_maps = make_in_maps(inputs)
    res = run_bass_kernel_spmd(nc, in_maps, core_ids=list(range(8)))
    return np.stack([np.asarray(res.results[b]["out"], np.float32)
                     for b in range(8)], axis=0)


# version 11 (final)
# speedup vs baseline: 1.0226x; 1.0200x over previous
import numpy as np
import ml_dtypes
from contextlib import ExitStack

import os as _os
import concourse.bass as bass
import concourse.mybir as mybir
from concourse.bass_utils import run_bass_kernel_spmd

F32 = mybir.dt.float32
BF16 = mybir.dt.bfloat16
ALU = mybir.AluOpType
ACT = mybir.ActivationFunctionType
AXX = mybir.AxisListType.X

L = 4096
D = 1024
NT = L // 128
NF = D // 128
H = 16
HD = 64
MLP = 4096
DIN = 2048
NCC = DIN // 128
DST = 16
DTR = 64
TC = 512
NCH = L // TC

PHASES = "A"
DEBUG = False


class Prog:

    ENG = ("sync", "tensor", "vector", "scalar", "gpsimd")
    NDMASEM = 20

    def __init__(self, nc, ctx):
        self.nc = nc
        self.sems = {e: ctx.enter_context(nc.semaphore("sem_" + e))
                     for e in self.ENG}
        for q in ("sq", "gq"):
            for i in range(self.NDMASEM):
                self.sems[f"{q}:{i}"] = ctx.enter_context(
                    nc.semaphore(f"sem_{q}_{i}"))
        self.plan = {e: [] for e in self.ENG}
        self.count = {k: 0 for k in self.sems}
        self.last_w = {}
        self.readers = {}
        self.watermark = {e: {} for e in self.ENG}
        self.region_sem = {"sq": {}, "gq": {}}
        self.sem_rr = {"sq": 0, "gq": 0}

    def _dma_sem(self, queue, region):
        m = self.region_sem[queue]
        if region not in m:
            m[region] = f"{queue}:{self.sem_rr[queue] % self.NDMASEM}"
            self.sem_rr[queue] += 1
        return m[region]

    def _waits_for(self, eng, reads, writes):
        waits = {}

        def need(src, val):
            if src == eng == "tensor":
                return
            if val <= 0:
                return
            wm = self.watermark[eng]
            if val > wm.get(src, 0) and val > waits.get(src, 0):
                waits[src] = val

        for r in reads:
            w = self.last_w.get(r)
            if w:
                need(*w)
        for wr in writes:
            w = self.last_w.get(wr)
            if w:
                need(*w)
            for e2, v in self.readers.get(wr, {}).items():
                need(e2, v)
        for sk, v in waits.items():
            self.watermark[eng][sk] = v
        return waits

    def emit(self, eng, fn, reads=(), writes=(), tag=""):
        waits = self._waits_for(eng, reads, writes)
        self.count[eng] += 1
        self.plan[eng].append((waits, fn, (eng, 1), tag))
        for r in reads:
            self.readers.setdefault(r, {})[eng] = self.count[eng]
        for wr in writes:
            self.last_w[wr] = (eng, self.count[eng])
            self.readers[wr] = {}

    def dma(self, queue, out_ap, in_ap, reads=(), writes=(), tag="",
            transpose=False, slow=False):
        eng = "sync" if queue == "sq" else "gpsimd"
        waits = self._waits_for(eng, reads, writes)
        primary = (list(writes) + list(reads) + [f"_anon_{queue}"])[0]
        sk = self._dma_sem(queue, primary)
        self.count[sk] += 16
        if transpose:
            def fn(e, o=out_ap, i=in_ap):
                return e.dma_start_transpose(o, i)
        elif slow:
            def fn(e, o=out_ap, i=in_ap):
                return e.dma_start(out=o, in_=i, allow_slow_non_contiguous=True)
        else:
            def fn(e, o=out_ap, i=in_ap):
                return e.dma_start(out=o, in_=i)
        self.plan[eng].append((waits, fn, (sk, 16), tag))
        for r in reads:
            self.readers.setdefault(r, {})[sk] = self.count[sk]
        for wr in writes:
            self.last_w[wr] = (sk, self.count[sk])
            self.readers[wr] = {}

    def fence(self):
        snap = dict(self.count)
        for eng in self.ENG:
            waits = {}
            for sk, v in snap.items():
                if sk == eng:
                    continue
                if v > self.watermark[eng].get(sk, 0):
                    waits[sk] = v
                    self.watermark[eng][sk] = v
            self.plan[eng].append((waits, None, None, "fence"))
        self.region_sem = {"sq": {}, "gq": {}}

    def truncate_at_fence(self, k):
        for eng in self.ENG:
            n = 0
            for idx, (w, fn, inc, tag) in enumerate(self.plan[eng]):
                if tag == "fence":
                    n += 1
                    if n == k:
                        self.plan[eng] = self.plan[eng][:idx + 1]
                        break

    def finalize(self, block):
        for eng in self.ENG:
            steps = self.plan[eng]

            def make_body(eng=eng, steps=steps):
                def body(e):
                    for waits, fn, inc, tag in steps:
                        for src, val in waits.items():
                            e.wait_ge(self.sems[src], val)
                        if fn is None:
                            continue
                        inst = fn(e)
                        inst.then_inc(self.sems[inc[0]], inc[1])
                return body
            getattr(block, eng)(make_body())


def bcast(ap, reps):
    return ap.rearrange("p n -> p n ()").broadcast_to(
        (ap.shape[0], ap.shape[1], reps))


def build(phases=None, debug=None, cut=None):
    phases = PHASES if phases is None else phases
    debug = DEBUG if debug is None else debug
    nc = bass.Bass("TRN2", target_bir_lowering=False)
    ctx = ExitStack()
    p = Prog(nc, ctx)
    skind = "ExternalOutput" if debug else "Internal"

    def din(name, shape, dt=F32):
        return nc.dram_tensor(name, shape, dt, kind="ExternalInput").ap()

    W = {}
    for n, sh in [
        ("x", [L, D]),
        ("Wq", [D, D]), ("bq", [D]), ("gq", [D]), ("betaq", [D]),
        ("Wk", [D, D]), ("bk", [D]), ("gk", [D]), ("betak", [D]),
        ("Wv", [D, D]), ("bv", [D]), ("gv", [D]), ("betav", [D]),
        ("Wfq", [D, D]), ("bfq", [D]), ("Wfk", [D, D]), ("bfk", [D]),
        ("ga", [D]), ("ba", [D]),
        ("W1", [D, MLP]), ("b1", [MLP]), ("W2", [MLP, D]), ("b2", [D]),
        ("norm_w", [D]), ("in_proj_w", [2 * DIN, D]),
        ("conv_w", [DIN, 1, 4]), ("conv_b", [DIN]),
        ("x_proj_w", [DTR + 2 * DST, DIN]), ("dt_proj_w", [DIN, DTR]),
        ("dt_proj_b", [DIN]), ("A_log", [DIN, DST]), ("Dv", [DIN]),
        ("out_proj_w", [D, DIN]),
        ("fw", [D, D]), ("fgamma", [D]), ("fbeta", [D]),
    ]:
        W[n] = din(n, sh)
    c_eye_f = din("c_eye_f", [128, 128])
    c_eye_b = din("c_eye_b", [128, 128], BF16)
    c_ones_col_b = din("c_ones_col_b", [128, 1], BF16)
    c_ones_row_b = din("c_ones_row_b", [1, 128], BF16)

    out_d = nc.dram_tensor("out", [L, D], F32, kind="ExternalOutput").ap()

    def dscr(name, shape, dt):
        return nc.dram_tensor(name, shape, dt, kind=skind).ap()

    xres = dscr("xres", [L, D], F32)
    zfm = dscr("zfm", [NF, 128, L], BF16)
    qfm = dscr("qfm", [NF, 128, L], BF16)
    kfm = dscr("kfm", [NF, 128, L], BF16)
    vtm = dscr("vtm", [NT, 128, D], BF16)
    pqfm = dscr("pqfm", [NF, 128, L], BF16)
    pkwtm = dscr("pkwtm", [NT, 128, D], BF16)
    afm = dscr("afm", [NF, 128, L], BF16)
    g1fm = dscr("g1fm", [MLP // 128, 128, L], BF16)
    rowb = nc.dram_tensor("rowb", [8, 4096], F32, kind=skind).ap()
    xnb = nc.dram_tensor("xnb", [L, D], F32, kind=skind).ap()
    ipT_d = nc.dram_tensor("ipT_d", [NF, 128, 2 * DIN], BF16, kind=skind).ap()
    opT_d = nc.dram_tensor("opT_d", [NCC, 128, D], BF16, kind=skind).ap()
    bc_d = nc.dram_tensor("bc_d", [2 * DST, TC], F32, kind=skind).ap()
    wqT_d = nc.dram_tensor("wqT_d", [NF, 128, 1024], BF16, kind=skind).ap()
    cc_in = nc.dram_tensor("cc_in", [1, 8], F32, kind="Internal").ap()
    cc_out = nc.dram_tensor("cc_out", [1, 8], F32, kind="Internal").ap()

    sb = lambda n, sh, dt=F32: ctx.enter_context(nc.sbuf_tensor(n, sh, dt))
    ones_col_b = sb("ones_col_b", [128, 1], BF16)
    ones_row_b = sb("ones_row_b", [1, 128], BF16)
    eye_f = sb("eye_f", [128, 128])
    eye_b = sb("eye_b", [128, 128], BF16)
    p.dma("sq", ones_col_b[:, :], c_ones_col_b[:, :], writes=["ones_col_b"])
    p.dma("sq", ones_row_b[:, :], c_ones_row_b[:, :], writes=["ones_row_b"])
    p.dma("sq", eye_f[:, :], c_eye_f[:, :], writes=["eye_f"])
    p.dma("sq", eye_b[:, :], c_eye_b[:, :], writes=["eye_b"])

    P0 = ctx.enter_context(nc.psum_tensor("P0", [128, 1024], F32))
    P1 = ctx.enter_context(nc.psum_tensor("P1", [128, 1024], F32))
    P2 = ctx.enter_context(nc.psum_tensor("P2", [128, 512], F32))
    P3 = ctx.enter_context(nc.psum_tensor("P3", [128, 512], F32))
    PR = ctx.enter_context(nc.psum_tensor("PR", [1, 1024], F32))

    st_sum = sb("st_sum", [128, NT])
    st_ssq = sb("st_ssq", [128, NT])
    st_rstd = sb("st_rstd", [128, NT])
    st_nmr = sb("st_nmr", [128, NT])
    st_tmp = sb("st_tmp", [128, NT])
    sq_scr = sb("sq_scr", [128, 1024])
    eps_col = sb("eps_col", [128, 1])

    def act(out_ap, in_ap, func, reads, writes, bias=0.0, scale=1.0, accum=None,
            accw=()):
        def fn(e, o=out_ap, i=in_ap, b=bias, s=scale, a=accum, f=func):
            return e.activation(o, i, f, bias=b, scale=s, accum_out=a)
        p.emit("scalar", fn, reads=reads, writes=list(writes) + list(accw))

    def vtt(op, out_ap, a_ap, b_ap, reads, writes):
        def fn(e, o=out_ap, a=a_ap, b=b_ap, op=op):
            return e.tensor_tensor(o, a, b, op=op)
        p.emit("vector", fn, reads=reads, writes=writes)

    def vts(out_ap, in_ap, s1, s2, op0, op1, reads, writes):
        def fn(e, o=out_ap, i=in_ap, s1=s1, s2=s2, op0=op0, op1=op1):
            return e.tensor_scalar(o, i, s1, s2, op0=op0, op1=op1)
        p.emit("vector", fn, reads=reads, writes=writes)

    def vstt(out_ap, in0_ap, scalar, in1_ap, op0, op1, reads, writes):
        def fn(e, o=out_ap, i0=in0_ap, s=scalar, i1=in1_ap, op0=op0, op1=op1):
            return e.scalar_tensor_tensor(o, i0, s, i1, op0=op0, op1=op1)
        p.emit("vector", fn, reads=reads, writes=writes)

    def vcopy(out_ap, in_ap, reads, writes):
        def fn(e, o=out_ap, i=in_ap):
            return e.tensor_copy(o, i)
        p.emit("vector", fn, reads=reads, writes=writes)

    def vred(out_ap, in_ap, op, reads, writes, absval=None):
        def fn(e, o=out_ap, i=in_ap, op=op, av=absval):
            return e.tensor_reduce(o, i, axis=AXX, op=op, apply_absolute_value=av)
        p.emit("vector", fn, reads=reads, writes=writes)

    def vrecip(out_ap, in_ap, reads, writes):
        def fn(e, o=out_ap, i=in_ap):
            return e.reciprocal(o, i)
        p.emit("vector", fn, reads=reads, writes=writes)

    def vmemset(ap, val, writes):
        def fn(e, a=ap, v=val):
            return e.memset(a, v)
        p.emit("vector", fn, writes=writes)

    def vscan(out_ap, a_ap, b_ap, init, reads, writes):
        def fn(e, o=out_ap, a=a_ap, b=b_ap, i=init):
            return e.tensor_tensor_scan(o, a, b, i, op0=ALU.mult, op1=ALU.add)
        p.emit("vector", fn, reads=reads, writes=writes)

    def mm(out_ap, lhsT_ap, rhs_ap, start, stop, reads, writes, tile_position=None):
        def fn(e, o=out_ap, l=lhsT_ap, r=rhs_ap, st=start, sp=stop,
               tp=tile_position):
            return e.matmul(o, l, r, start=st, stop=sp, tile_position=tp)
        p.emit("tensor", fn, reads=reads, writes=writes)

    def transp(out_ap, in_ap, ident, reads, writes):
        def fn(e, o=out_ap, i=in_ap, id_=ident):
            return e.transpose(o, i, id_)
        p.emit("tensor", fn, reads=reads, writes=writes)

    def rstd_batch(n, sl, eps=1e-5):
        par = (sl.start or 0) % 2
        kS, kQ = f"st_sum{par}", f"st_ssq{par}"
        kR, kN, kT = f"st_rstd{par}", f"st_nmr{par}", f"st_tmp{par}"
        vts(st_tmp[:, sl], st_sum[:, sl], -1.0 / n, None, ALU.mult, ALU.bypass,
            [kS], [kT])
        vtt(ALU.mult, st_nmr[:, sl], st_tmp[:, sl], st_tmp[:, sl], [kT], [kN])
        vstt(st_ssq[:, sl], st_ssq[:, sl], 1.0 / n, st_nmr[:, sl], ALU.mult,
             ALU.subtract, [kQ, kN], [kQ])
        act(st_rstd[:, sl], st_ssq[:, sl], ACT.Ln, [kQ, "eps_col"], [kR],
            bias=eps_col[:, 0:1])
        act(st_rstd[:, sl], st_rstd[:, sl], ACT.Exp, [kR], [kR], scale=-0.5)
        def _nm(e, sl=sl):
            return e.activation(st_nmr[:, sl], st_rstd[:, sl], ACT.Identity,
                                scale=st_tmp[:, sl], bias=0.0)
        p.emit("scalar", _nm, reads=[kR, kT], writes=[kN])

    def load_rep(actx, name, dram_vec):
        n = dram_vec.shape[0]
        t = actx.enter_context(nc.sbuf_tensor("rep_" + name, [128, n], F32))
        p.dma("sq", t[:, :],
              dram_vec.rearrange("(o n) -> o n", o=1).partition_broadcast(128),
              writes=["rep_" + name])
        return t

    def load_w_bf16(wctx, name, dram_w, rows, cols, stage, stage_key):
        tiles = []
        scw = stage.shape[1]
        for k in range(rows // 128):
            t = wctx.enter_context(
                nc.sbuf_tensor(f"wb_{name}_{k}", [128, cols], BF16))
            tiles.append(t)
            for c0 in range(0, cols, scw):
                c1 = min(cols, c0 + scw)
                p.dma("sq", stage[:, :c1 - c0],
                      dram_w[k * 128:(k + 1) * 128, c0:c1],
                      writes=[stage_key], tag="wload")
                vcopy(t[:, c0:c1], stage[:, :c1 - c0], [stage_key],
                      [f"wb_{name}_{k}"])
        return tiles

    def transpose_tm_tile(src_bf_ap, src_key, dst_dram, i, ev, ev_key):
        p2b = P2[:, :512].bitcast(BF16)
        for c in range(NF):
            transp(p2b[:, c * 128:(c + 1) * 128],
                   src_bf_ap[:, c * 128:(c + 1) * 128], eye_b[:, :],
                   [src_key, "eye_b"], ["P2"])
        act(ev[:, :1024], p2b[:, :1024], ACT.Copy, ["P2"], [ev_key])
        p.dma("gq", dst_dram[:, :, i * 128:(i + 1) * 128]
              .rearrange("c p t -> p c t"),
              ev[:, :1024].rearrange("p (c t) -> p c t", c=NF),
              reads=[ev_key], tag="fmst")

    taps = {}
    vmemset(eps_col[:, :], 1e-5, ["eps_col"])

    if "A" in phases:
        actx = ctx.enter_context(ExitStack())
        asb = lambda n, sh, dt=F32: actx.enter_context(nc.sbuf_tensor(n, sh, dt))

        xt = [asb(f"a0x{b}", [128, D]) for b in range(2)]
        zt = [asb(f"a0z{b}", [128, D]) for b in range(2)]
        ztb = asb("a0ztb", [128, D], BF16)
        ev = asb("a0ev", [128, 1024], BF16)
        def a0_s1(i):
            b = i % 2
            kx = f"a0x{b}"
            p.dma("sq", xt[b][:, :], W["x"][i * 128:(i + 1) * 128, :], writes=[kx])
            act(sq_scr[:, :D], xt[b][:, :], ACT.Square, [kx], ["sq_scr"],
                accum=st_ssq[:, i:i + 1], accw=[f"st_ssq{i % 2}"])
            act(zt[b][:, :], xt[b][:, :], ACT.Copy, [kx], [f"a0z{b}"],
                accum=st_sum[:, i:i + 1], accw=[f"st_sum{i % 2}"])

        def a0_s2(i):
            b = i % 2
            kx, kz = f"a0x{b}", f"a0z{b}"
            rstd_batch(D, slice(i, i + 1))
            act(zt[b][:, :], xt[b][:, :], ACT.Identity,
                [kx, f"st_rstd{i % 2}", f"st_nmr{i % 2}"],
                [kz], bias=st_nmr[:, i:i + 1], scale=st_rstd[:, i:i + 1])
            vtt(ALU.add, xt[b][:, :], xt[b][:, :], zt[b][:, :], [kx, kz], [kx])
            p.dma("sq", xres[i * 128:(i + 1) * 128, :], xt[b][:, :], reads=[kx])
            vcopy(ztb[:, :], zt[b][:, :], [kz], ["a0ztb"])
            transpose_tm_tile(ztb[:, :], "a0ztb", zfm, i, ev, "a0ev")

        a0_s1(0)
        for i in range(NT):
            if i + 1 < NT:
                a0_s1(i + 1)
            a0_s2(i)
        p.fence()

        fmincat = [asb(f"fmincat{b}", [128, NF * TC], BF16) for b in range(2)]
        fmin = [[fmincat[b][:, c * TC:(c + 1) * TC] for c in range(NF)]
                for b in range(2)]

        def load_fm_chunk(src_dram, j, b):
            p.dma("sq", fmincat[b][:, :].rearrange("p (c t) -> p c t", c=NF),
                  src_dram[:, :, j * TC:(j + 1) * TC].rearrange("c p t -> p c t"),
                  writes=[f"fmin{b}"], tag="fmld")

        wstage = asb("wstage", [128, 1024])
        brow_f = asb("brow_f", [1, 1024])
        brow_b = asb("brow_b", [1, 1024], BF16)
        ev_t = asb("ev_t", [128, 1024])
        ev_s = asb("ev_s", [128, 1024])
        ev_b = asb("ev_b", [128, 1024], BF16)
        evT = asb("evT", [128, 1024], BF16)
        qprobe = asb("qprobe", [1, 1024])
        qpb_f = asb("qpb_f", [128, 16])
        qpb = [asb(f"qpb{c}", [128, 16], BF16) for c in range(NF)]
        logit = asb("logit", [16, L])
        smax = asb("smax", [16, 1])
        srec = asb("srec", [16, 1])
        ssum = asb("ssum", [16, 1])
        sc0 = asb("sc0", [16, 1])
        negm = asb("negm", [16, 1])
        sc_tm = asb("sc_tm", [128, NT * 16])
        kin = asb("kin", [128, NF * TC], BF16)
        bfq_sb = asb("bfq_sb", [128, NF])
        pq_f = asb("pq_f", [128, TC])
        pq_b = asb("pq_b", [128, TC], BF16)
        pk_f = asb("pk_f", [128, 1024])
        pkw_t = [asb(f"pkw_t{b}", [128, 1024], BF16) for b in range(2)]
        ksum_row = asb("ksum_row", [1, 1024])
        ksum_fm_f = asb("ksum_fm_f", [128, NF])
        ksum_fm = asb("ksum_fm", [128, 64], BF16)
        sc0_fm = asb("sc0_fm", [128, NF])
        bfk_fm = asb("bfk_fm", [128, NF])
        pk0_fm = asb("pk0_fm", [128, NF])
        kv_sb = asb("kv_sb", [128, 1024], BF16)
        vvin = [asb(f"vvin{b}", [128, 1024], BF16) for b in range(2)]
        kv_acc = asb("kv_acc", [128, 512])
        pqin = [asb(f"pqin{b}", [128, 1024], BF16) for b in range(2)]
        den_t = asb("den_t", [128, 16])
        att_t = asb("att_t", [128, 1024])
        a_t = asb("a_t", [128, 1024])
        a_tb = asb("a_tb", [128, 1024], BF16)
        x2 = [asb(f"x2{b}", [128, D]) for b in range(2)]

        def proj_tm(dst_dram, Wname, bias_name, g_name, bt_name,
                    probe_psum=False, do_transpose=True):
            with ExitStack() as wctx:
                wt = load_w_bf16(wctx, Wname, W[Wname], D, 1024, wstage, "wstage")
                g_rep = load_rep(wctx, "g" + Wname, W[g_name])
                bt_rep = load_rep(wctx, "bt" + Wname, W[bt_name])
                p.dma("sq", brow_f[:, :],
                      W[bias_name].rearrange("(o n) -> o n", o=1), writes=["brow_f"])
                vcopy(brow_b[:, :], brow_f[:, :], ["brow_f"], ["brow_b"])

                def mm_part(i):
                    j, tb = i // 4, i % 4
                    if tb == 0:
                        load_fm_chunk(zfm, j, j % 2)
                    bj = j % 2
                    ps, psk = (P0, "P0") if i % 2 == 0 else (P1, "P1")
                    t0 = tb * 128
                    for half in range(2):
                        h0 = half * 512
                        for c in range(NF):
                            mm(ps[:, h0:h0 + 512], fmin[bj][c][:, t0:t0 + 128],
                               wt[c][:, h0:h0 + 512], start=(c == 0), stop=False,
                               reads=[f"fmin{bj}", f"wb_{Wname}_{c}"],
                               writes=[psk])
                        mm(ps[:, h0:h0 + 512], ones_row_b[:, :],
                           brow_b[:, h0:h0 + 512], start=False, stop=True,
                           reads=["ones_row_b", "brow_b"], writes=[psk])

                def epi_part(i):
                    ps, psk = (P0, "P0") if i % 2 == 0 else (P1, "P1")
                    act(ev_t[:, :], ps[:, :1024], ACT.Copy, [psk], ["ev_t"],
                        accum=st_sum[:, i:i + 1], accw=[f"st_sum{i % 2}"])
                    act(ev_s[:, :], ev_t[:, :], ACT.Square, ["ev_t"],
                        ["ev_s"], accum=st_ssq[:, i:i + 1], accw=[f"st_ssq{i % 2}"])
                    rstd_batch(D, slice(i, i + 1))
                    act(ev_t[:, :], ev_t[:, :], ACT.Identity,
                        ["ev_t", f"st_rstd{i % 2}", f"st_nmr{i % 2}"], ["ev_t"],
                        bias=st_nmr[:, i:i + 1], scale=st_rstd[:, i:i + 1])
                    vtt(ALU.mult, ev_s[:, :], ev_t[:, :], g_rep[:, :],
                        ["ev_t", "rep_g" + Wname], ["ev_s"])
                    vtt(ALU.add, ev_b[:, :], ev_s[:, :], bt_rep[:, :],
                        ["ev_s", "rep_bt" + Wname], ["ev_b"])
                    if probe_psum:
                        for half in range(2):
                            h0 = half * 512
                            mm(PR[:1, h0:h0 + 512], ones_col_b[:, :],
                               ev_b[:, h0:h0 + 512],
                               start=(i == 0), stop=(i == NT - 1),
                               reads=["ones_col_b", "ev_b"], writes=["PR"])
                    if do_transpose:
                        transpose_tm_tile(ev_b[:, :], "ev_b", dst_dram, i,
                                          evT, "evT")
                    else:
                        vcopy(evT[:, :], ev_b[:, :], ["ev_b"], ["evT"])
                        p.dma("gq", dst_dram[i, :, :], evT[:, :],
                              reads=["evT"], tag="vtm_st")

                for i in range(NT):
                    mm_part(i)
                    if i > 0:
                        epi_part(i - 1)
                epi_part(NT - 1)
                p.fence()

        proj_tm(qfm, "Wq", "bq", "gq", "betaq", probe_psum=True)
        act(qprobe[:, :], PR[:1, :1024], ACT.Copy, ["PR"], ["qprobe"],
            scale=1.0 / (L * 8.0))
        proj_tm(kfm, "Wk", "bk", "gk", "betak")
        proj_tm(vtm, "Wv", "bv", "gv", "betav", do_transpose=False)

        p.dma("gq", rowb[0:1, 0:1024], qprobe[:, :], reads=["qprobe"],
              writes=["rb0"])
        for c in range(NF):
            vmemset(qpb_f[:, :], 0.0, ["qpb_f"])
            for hh in range(2):
                h = 2 * c + hh
                p.dma("sq", qpb_f[hh * 64:(hh + 1) * 64, h:h + 1],
                      rowb[0:1, c * 128 + hh * 64:c * 128 + (hh + 1) * 64]
                      .rearrange("o n -> (o n) ()"),
                      reads=["rb0"], writes=["qpb_f"], slow=True)
            vcopy(qpb[c][:, :], qpb_f[:, :], ["qpb_f"], [f"qpb{c}"])

        def emit_logit_chunk(j):
            p.dma("sq", kin[:, :].rearrange("p (c t) -> p c t", c=NF),
                  kfm[:, :, j * TC:(j + 1) * TC].rearrange("c p t -> p c t"),
                  writes=["kin"], tag="kld")
            for c in range(NF):
                mm(P2[:16, :512], qpb[c][:, :], kin[:, c * TC:(c + 1) * TC],
                   start=(c == 0), stop=(c == NF - 1),
                   reads=["kin", f"qpb{c}"], writes=["P2"])
            vcopy(logit[:, j * TC:(j + 1) * TC], P2[:16, :512], ["P2"],
                  ["logit"])

        def emit_softmax():
            vred(smax[:, :], logit[:, :], ALU.max, ["logit"], ["smax"])
            vts(smax[:, :], smax[:, :], 0.0, None, ALU.max, ALU.bypass, ["smax"],
                ["smax"])
            vts(negm[:, :], smax[:, :], -1.0, None, ALU.mult, ALU.bypass, ["smax"],
                ["negm"])
            act(logit[:, :], logit[:, :], ACT.Exp, ["logit", "negm"], ["logit"],
                bias=negm[:, 0:1], accum=ssum[:, 0:1], accw=["ssum"])
            act(sc0[:, :], negm[:, :], ACT.Exp, ["negm"], ["sc0"])
            vtt(ALU.add, ssum[:, :], ssum[:, :], sc0[:, :], ["ssum", "sc0"], ["ssum"])
            vrecip(srec[:, :], ssum[:, :], ["ssum"], ["srec"])

            def _scmul(e):
                return e.tensor_scalar(logit[:, :], logit[:, :], srec[:, 0:1], None,
                                       op0=ALU.mult)
            p.emit("vector", _scmul, reads=["logit", "srec"], writes=["logit"])
            vtt(ALU.mult, sc0[:, :], sc0[:, :], srec[:, :], ["sc0", "srec"], ["sc0"])
            for i in range(NT):
                transp(P3[:, :16], logit[:, i * 128:(i + 1) * 128], eye_f[:16, :16],
                       ["logit", "eye_f"], ["P3"])
                vcopy(sc_tm[:, i * 16:(i + 1) * 16], P3[:128, :16], ["P3"], ["sc_tm"])

        with ExitStack() as wctx:
            wsb = lambda n, sh, dt=F32: wctx.enter_context(nc.sbuf_tensor(n, sh, dt))
            wfq = load_w_bf16(wctx, "Wfq", W["Wfq"], D, 1024, wstage, "wstage")
            for c in range(NF):
                p.dma("sq", bfq_sb[:, c:c + 1],
                      W["bfq"][c * 128:(c + 1) * 128].rearrange("(p o) -> p o", o=1),
                      writes=["bfq_sb"], slow=True)

            def mm_part(k):
                j, m = k // NF, k % NF
                if m == 0:
                    load_fm_chunk(qfm, j, j % 2)
                bj = j % 2
                ps, psk = (P0, "P0") if k % 2 == 0 else (P1, "P1")
                for c in range(NF):
                    mm(ps[:, :TC], wfq[c][:, m * 128:(m + 1) * 128],
                       fmin[bj][c][:, :], start=(c == 0), stop=(c == NF - 1),
                       reads=[f"fmin{bj}", f"wb_Wfq_{c}"], writes=[psk])

            def epi_part(k):
                j, m = k // NF, k % NF
                ps, psk = (P0, "P0") if k % 2 == 0 else (P1, "P1")
                act(pq_f[:, :], ps[:, :TC], ACT.Tanh, [psk, "bfq_sb"],
                    ["pq_f"], bias=bfq_sb[:, m:m + 1])
                vts(pq_b[:, :], pq_f[:, :], 1.0, None, ALU.add, ALU.bypass,
                    ["pq_f"], ["pq_b"])
                p.dma("gq", pqfm[m, :, j * TC:(j + 1) * TC], pq_b[:, :],
                      reads=["pq_b"], tag="pq_st")

            for k in range(NCH * NF):
                mm_part(k)
                if k % NF == 1 and k // NF < NCH:
                    emit_logit_chunk(k // NF)
                if k > 0:
                    epi_part(k - 1)
            epi_part(NCH * NF - 1)
            emit_softmax()
            p.fence()

        with ExitStack() as wctx:
            wsb = lambda n, sh, dt=F32: wctx.enter_context(nc.sbuf_tensor(n, sh, dt))
            wfk = load_w_bf16(wctx, "Wfk", W["Wfk"], D, 1024, wstage, "wstage")
            p.dma("sq", brow_f[:, :], W["bfk"].rearrange("(o n) -> o n", o=1),
                  writes=["brow_f"])
            vcopy(brow_b[:, :], brow_f[:, :], ["brow_f"], ["brow_b"])

            def mm_part(i):
                j, tb = i // 4, i % 4
                if tb == 0:
                    load_fm_chunk(kfm, j, j % 2)
                bj = j % 2
                ps, psk = (P0, "P0") if i % 2 == 0 else (P1, "P1")
                t0 = tb * 128
                for half in range(2):
                    h0 = half * 512
                    for c in range(NF):
                        mm(ps[:, h0:h0 + 512], fmin[bj][c][:, t0:t0 + 128],
                           wfk[c][:, h0:h0 + 512], start=(c == 0), stop=False,
                           reads=[f"fmin{bj}", f"wb_Wfk_{c}"], writes=[psk])
                    mm(ps[:, h0:h0 + 512], ones_row_b[:, :],
                       brow_b[:, h0:h0 + 512], start=False, stop=True,
                       reads=["ones_row_b", "brow_b"], writes=[psk])

            def epi_part(i):
                b = i % 2
                ps, psk = (P0, "P0") if i % 2 == 0 else (P1, "P1")
                pkw = pkw_t[b]
                kpk = f"pkw_t{b}"
                act(pk_f[:, :], ps[:, :1024], ACT.Tanh, [psk], ["pk_f"])
                vstt(pkw[:, :].rearrange("p (h e) -> p h e", h=16),
                     pk_f[:, :].rearrange("p (h e) -> p h e", h=16), 1.0,
                     bcast(sc_tm[:, i * 16:(i + 1) * 16], 64),
                     ALU.add, ALU.mult, ["pk_f", "sc_tm"], [kpk])
                for half in range(2):
                    h0 = half * 512
                    mm(PR[:1, h0:h0 + 512], ones_col_b[:, :],
                       pkw[:, h0:h0 + 512],
                       start=(i == 0), stop=(i == NT - 1),
                       reads=["ones_col_b", kpk], writes=["PR"])
                p.dma("sq", vvin[b][:, :], vtm[i, :, :], writes=[f"vvin{b}"])
                kps, kpsk = (P2, "P2") if i % 2 == 0 else (P3, "P3")
                for h in range(H):
                    c, hh = h // 2, h % 2
                    mm(kps[hh * 64:(hh + 1) * 64, c * 64:(c + 1) * 64],
                       pkw[:, h * 64:(h + 1) * 64],
                       vvin[b][:, h * 64:(h + 1) * 64],
                       start=True, stop=True,
                       reads=[kpk, f"vvin{b}"], writes=[kpsk],
                       tile_position=(0, hh * 64))
                vtt(ALU.add, kv_acc[:, :], kv_acc[:, :], kps[:, :512],
                    ["kv_acc", kpsk], ["kv_acc"])

            vmemset(kv_acc[:, :], 0.0, ["kv_acc"])
            for i in range(NT):
                mm_part(i)
                if i > 0:
                    epi_part(i - 1)
            epi_part(NT - 1)
            act(ksum_row[:, :], PR[:1, :1024], ACT.Copy, ["PR"], ["ksum_row"])
            p.dma("gq", rowb[2:3, 0:1024], ksum_row[:, :], reads=["ksum_row"],
                  writes=["rb2"])
            for c in range(NF):
                p.dma("sq", ksum_fm_f[:, c:c + 1],
                      rowb[2:3, c * 128:(c + 1) * 128].rearrange("o (p q) -> (o p) q", q=1),
                      reads=["rb2"], writes=["ksum_fm_f"], slow=True)
            p.dma("gq", rowb[1:2, 0:16], sc0[:, :], reads=["sc0"], writes=["rb1"],
                  slow=True)
            for c in range(NF):
                for hh in range(2):
                    h = 2 * c + hh
                    p.dma("sq", sc0_fm[hh * 64:(hh + 1) * 64, c:c + 1],
                          rowb[1:2, h:h + 1].partition_broadcast(64),
                          reads=["rb1"], writes=["sc0_fm"], slow=True)
            for c in range(NF):
                p.dma("sq", bfk_fm[:, c:c + 1],
                      W["bfk"][c * 128:(c + 1) * 128].rearrange("(p o) -> p o", o=1),
                      writes=["bfk_fm"], slow=True)
            act(pk0_fm[:, :], bfk_fm[:, :], ACT.Tanh, ["bfk_fm"], ["pk0_fm"])
            vts(pk0_fm[:, :], pk0_fm[:, :], 1.0, None, ALU.add, ALU.bypass,
                ["pk0_fm"], ["pk0_fm"])
            vtt(ALU.mult, pk0_fm[:, :], pk0_fm[:, :], sc0_fm[:, :],
                ["pk0_fm", "sc0_fm"], ["pk0_fm"])
            vtt(ALU.add, ksum_fm_f[:, :], ksum_fm_f[:, :], pk0_fm[:, :],
                ["ksum_fm_f", "pk0_fm"], ["ksum_fm_f"])
            vmemset(ksum_fm[:, :], 0.0, ["ksum_fm"])
            for h in range(H):
                c, hh = h // 2, h % 2
                vcopy(ksum_fm[hh * 64:(hh + 1) * 64, 8 * c + hh:8 * c + hh + 1],
                      ksum_fm_f[hh * 64:(hh + 1) * 64, c:c + 1],
                      ["ksum_fm_f"], ["ksum_fm"])
            p.fence()

        vmemset(kv_sb[:, :], 0.0, ["kv_sb"])
        for h in range(H):
            c, hh = h // 2, h % 2
            vcopy(kv_sb[hh * 64:(hh + 1) * 64,
                        c * 128 + hh * 64:c * 128 + hh * 64 + 64],
                  kv_acc[hh * 64:(hh + 1) * 64, c * 64:(c + 1) * 64],
                  ["kv_acc"], ["kv_sb"])
        p.fence()

        with ExitStack() as wctx:
            ga_rep = load_rep(wctx, "ga", W["ga"])
            ba_rep = load_rep(wctx, "ba", W["ba"])

            def mm_part(i):
                b = i % 2
                p.dma("sq", pqin[b][:, :].rearrange("p (c t) -> p c t", c=NF),
                      pqfm[:, :, i * 128:(i + 1) * 128].rearrange("c p t -> p c t"),
                      writes=[f"pqin{b}"])
                p.dma("sq", x2[b][:, :], xres[i * 128:(i + 1) * 128, :],
                      writes=[f"x2{b}"])
                ps, psk = (P0, "P0") if i % 2 == 0 else (P1, "P1")
                for c in range(NF):
                    lhsT = pqin[b][:, c * 128:(c + 1) * 128]
                    mm(ps[:, c * 128:(c + 1) * 128], lhsT,
                       kv_sb[:, c * 128:(c + 1) * 128],
                       start=True, stop=True, reads=[f"pqin{b}", "kv_sb"],
                       writes=[psk])
                    mm(P2[:, 4 * c:4 * c + 2], lhsT,
                       ksum_fm[:, 8 * c:8 * c + 2],
                       start=True, stop=True, reads=[f"pqin{b}", "ksum_fm"],
                       writes=["P2"])
                vcopy(den_t[:, :],
                      P2[:, 0:32].rearrange("p (c two) -> p c two", two=4)[:, :, 0:2],
                      ["P2"], ["den_t"])
                vts(den_t[:, :], den_t[:, :], 1e-6, None, ALU.add, ALU.bypass,
                    ["den_t"], ["den_t"])
                vrecip(den_t[:, :], den_t[:, :], ["den_t"], ["den_t"])
                vtt(ALU.mult, att_t[:, :].rearrange("p (h e) -> p h e", h=16),
                    ps[:, :1024].rearrange("p (h e) -> p h e", h=16),
                    bcast(den_t[:, :], 64), [psk, "den_t"], ["att_t"])

            def epi_part(i):
                b = i % 2
                vred(st_sum[:, i:i + 1], att_t[:, :], ALU.add, ["att_t"],
                     [f"st_sum{i % 2}"])
                act(sq_scr[:, :D], att_t[:, :], ACT.Square, ["att_t"],
                    ["sq_scr"], accum=st_ssq[:, i:i + 1], accw=[f"st_ssq{i % 2}"])
                rstd_batch(D, slice(i, i + 1))
                act(a_t[:, :], att_t[:, :], ACT.Identity,
                    ["att_t", f"st_rstd{i % 2}", f"st_nmr{i % 2}"], ["a_t"],
                    bias=st_nmr[:, i:i + 1], scale=st_rstd[:, i:i + 1])
                vtt(ALU.mult, a_t[:, :], a_t[:, :], ga_rep[:, :],
                    ["a_t", "rep_ga"], ["a_t"])
                vtt(ALU.add, a_t[:, :], a_t[:, :], ba_rep[:, :],
                    ["a_t", "rep_ba"], ["a_t"])
                vtt(ALU.add, x2[b][:, :], x2[b][:, :], a_t[:, :],
                    [f"x2{b}", "a_t"], [f"x2{b}"])
                p.dma("sq", xres[i * 128:(i + 1) * 128, :], x2[b][:, :],
                      reads=[f"x2{b}"])
                vcopy(a_tb[:, :], a_t[:, :], ["a_t"], ["a_tb"])
                transpose_tm_tile(a_tb[:, :], "a_tb", afm, i, evT, "evT")

            for i in range(NT):
                mm_part(i)
                epi_part(i)
            p.fence()

        actx.close()
        actx = ctx.enter_context(ExitStack())
        asb = lambda n, sh, dt=F32: actx.enter_context(nc.sbuf_tensor(n, sh, dt))
        wstage = asb("wstage2", [128, 1024])
        fmin = [[asb(f"fmin2_{b}_{c}", [128, TC], BF16) for c in range(NF)]
                for b in range(2)]

        def load_fm_chunk(src_dram, j, b):
            for c in range(NF):
                p.dma("sq", fmin[b][c][:, :], src_dram[c, :, j * TC:(j + 1) * TC],
                      writes=[f"fmin{b}"], tag="fmld")

        with ExitStack() as wctx:
            w1 = load_w_bf16(wctx, "W1", W["W1"], D, MLP, wstage, "wstage")
            b1_sb = wctx.enter_context(nc.sbuf_tensor("b1_sb", [128, MLP // 128], F32))
            g1_b = wctx.enter_context(nc.sbuf_tensor("g1_b", [128, TC], BF16))
            for c in range(MLP // 128):
                p.dma("sq", b1_sb[:, c:c + 1],
                      W["b1"][c * 128:(c + 1) * 128].rearrange("(p o) -> p o", o=1),
                      writes=["b1_sb"], slow=True)

            def mm_part(k):
                j, m = k // (MLP // 128), k % (MLP // 128)
                if m == 0:
                    load_fm_chunk(afm, j, j % 2)
                bj = j % 2
                ps, psk = (P0, "P0") if k % 2 == 0 else (P1, "P1")
                for c in range(NF):
                    mm(ps[:, :TC], w1[c][:, m * 128:(m + 1) * 128],
                       fmin[bj][c][:, :], start=(c == 0), stop=(c == NF - 1),
                       reads=[f"fmin{bj}", f"wb_W1_{c}"], writes=[psk])

            def epi_part(k):
                j, m = k // (MLP // 128), k % (MLP // 128)
                ps, psk = (P0, "P0") if k % 2 == 0 else (P1, "P1")
                act(g1_b[:, :], ps[:, :TC], ACT.Gelu, [psk, "b1_sb"],
                    ["g1_b"], bias=b1_sb[:, m:m + 1])
                p.dma("gq", g1fm[m, :, j * TC:(j + 1) * TC], g1_b[:, :],
                      reads=["g1_b"], tag="g1_st")

            NK = NCH * (MLP // 128)
            for k in range(NK):
                mm_part(k)
                if k > 0:
                    epi_part(k - 1)
            epi_part(NK - 1)
            p.fence()

        with ExitStack() as wctx:
            w2 = load_w_bf16(wctx, "W2", W["W2"], MLP, 1024, wstage, "wstage")
            b2_rep = load_rep(wctx, "b2", W["b2"])
            wsb = lambda n, sh, dt=F32: wctx.enter_context(nc.sbuf_tensor(n, sh, dt))
            g1cat = [wsb(f"g1cat{b}", [128, MLP], BF16) for b in range(2)]
            g1in = [[g1cat[b][:, c * 128:(c + 1) * 128]
                     for c in range(MLP // 128)] for b in range(2)]
            x3 = [wsb(f"x3{b}", [128, D]) for b in range(2)]
            mo_t = [wsb(f"mo_t{b}", [128, 1024]) for b in range(2)]

            def mm_part(i):
                b = i % 2
                p.dma("sq", g1cat[b][:, :].rearrange("p (c t) -> p c t", c=MLP // 128),
                      g1fm[:, :, i * 128:(i + 1) * 128].rearrange("c p t -> p c t"),
                      writes=[f"g1in{b}"], tag="g1ld")
                p.dma("sq", x3[b][:, :], xres[i * 128:(i + 1) * 128, :],
                      writes=[f"x3{b}"])
                ps, psk = (P0, "P0") if i % 2 == 0 else (P1, "P1")
                for half in range(2):
                    h0 = half * 512
                    for c in range(MLP // 128):
                        mm(ps[:, h0:h0 + 512], g1in[b][c][:, :],
                           w2[c][:, h0:h0 + 512],
                           start=(c == 0), stop=(c == MLP // 128 - 1),
                           reads=[f"g1in{b}", f"wb_W2_{c}"], writes=[psk])

            def epi_part(i):
                b = i % 2
                ps, psk = (P0, "P0") if i % 2 == 0 else (P1, "P1")
                vtt(ALU.add, mo_t[b][:, :], ps[:, :1024], b2_rep[:, :],
                    [psk, "rep_b2"], [f"mo_t{b}"])
                vtt(ALU.add, x3[b][:, :], x3[b][:, :], mo_t[b][:, :],
                    [f"x3{b}", f"mo_t{b}"], [f"x3{b}"])
                dst = xres if len(phases) > 1 else out_d
                p.dma("sq", dst[i * 128:(i + 1) * 128, :], x3[b][:, :],
                      reads=[f"x3{b}"], tag="xout")

            for i in range(NT):
                mm_part(i)
                if i > 0:
                    epi_part(i - 1)
            epi_part(NT - 1)
            p.fence()
        actx.close()

    fctx = None
    alpha_sb = None
    if "B" in phases or "D" in phases:
        fctx = ctx.enter_context(ExitStack())
        alpha_sb = fctx.enter_context(nc.sbuf_tensor("alpha_sb", [1, 16], F32))
        fpctx = ExitStack()
        fsb = lambda n, sh, dt=F32: fpctx.enter_context(nc.sbuf_tensor(n, sh, dt))
        wqT = [fsb(f"wqTp_{c}", [128, 1024], BF16) for c in range(NF)]
        fstage = fsb("fstage", [128, 1024])
        fabs = fsb("fabs", [128, 1024])
        fmask = fsb("fmask", [128, 1024])
        fsgn = fsb("fsgn", [128, 1024])
        fsums = fsb("fsums", [128, 8])
        fred = fsb("fred", [128, 16])
        frow = fsb("frow", [1, 16])
        ones_col_f = fsb("ones_col_f", [128, 16])
        vmemset(ones_col_f[:, :], 1.0, ["ones_col_f"])

        fwsm = fsb("fwsm", [128, 1024], BF16)
        evF = fsb("evF", [128, 1024], BF16)
        dstag = fsb("dstag", [128, 16])
        vmemset(fred[:, :], 0.0, ["fred"])
        for t in range(NF):
            p.dma("sq", fstage[:, :], W["fw"][t * 128:(t + 1) * 128, :],
                  writes=["fstage"])
            act(fabs[:, :], fstage[:, :], ACT.Abs, ["fstage"], ["fabs"],
                accum=fsums[:, t:t + 1], accw=["fsums"])
        vred(fred[:, 0:1], fsums[:, :], ALU.add, ["fsums"], ["fred"])
        def mmf(out_ap, lhsT_ap, rhs_ap, reads, writes):
            mm(out_ap, lhsT_ap, rhs_ap, start=True, stop=True, reads=reads,
               writes=writes)
        mmf(PR[:1, 0:1], ones_col_f[:, 0:1], fred[:, 0:1],
            ["fred", "ones_col_f"], ["PR"])
        act(frow[:, 0:1], PR[:1, 0:1], ACT.Copy, ["PR"], ["frow"])
        vts(frow[:, 0:1], frow[:, 0:1], 0.7 / (1024.0 * 1024.0), None,
            ALU.mult, ALU.bypass, ["frow"], ["frow"])
        p.dma("gq", rowb[3:4, 0:1], frow[:, 0:1], reads=["frow"], writes=["rb3"],
              slow=True)
        p.dma("sq", dstag[:, 0:1], rowb[3:4, 0:1].partition_broadcast(128),
              reads=["rb3"], writes=["dstag"], slow=True)
        for t in range(NF):
            p.dma("sq", fstage[:, :], W["fw"][t * 128:(t + 1) * 128, :],
                  writes=["fstage"])
            act(fabs[:, :], fstage[:, :], ACT.Abs, ["fstage"], ["fabs"])
            def _mk(e, t=t):
                return e.tensor_single_scalar(fmask[:, :], fabs[:, :],
                                              dstag[:, 0:1], op=ALU.is_gt)
            p.emit("vector", _mk, reads=["fabs", "dstag"], writes=["fmask"])
            vtt(ALU.mult, sq_scr[:, :1024], fabs[:, :], fmask[:, :],
                ["fabs", "fmask"], ["sq_scr"])
            vred(fsums[:, t:t + 1], sq_scr[:, :1024], ALU.add, ["sq_scr"],
                 ["fsums"])
            vred(fred[:, t:t + 1], fmask[:, :], ALU.add, ["fmask"], ["fred"])
            act(fsgn[:, :], fstage[:, :], ACT.Sign, ["fstage"], ["fsgn"])
            vtt(ALU.mult, fwsm[:, :], fsgn[:, :], fmask[:, :],
                ["fsgn", "fmask"], ["fwsm"])
            p2b = P2[:, :512].bitcast(BF16)
            for c in range(NF):
                transp(p2b[:, c * 128:(c + 1) * 128],
                       fwsm[:, c * 128:(c + 1) * 128], eye_b[:, :],
                       ["fwsm", "eye_b"], ["P2"])
            act(evF[:, :1024], p2b[:, :1024], ACT.Copy, ["P2"], ["evF"])
            for c in range(NF):
                vcopy(wqT[c][:, t * 128:(t + 1) * 128],
                      evF[:, c * 128:(c + 1) * 128], ["evF"], [f"wqTp_{c}"])
        vred(fred[:, 8:9], fsums[:, :], ALU.add, ["fsums"], ["fred"])
        mmf(PR[:1, 0:16], ones_col_f[:, 0:1], fred[:, 0:16],
            ["fred", "ones_col_f"], ["PR"])
        act(frow[:, :], PR[:1, 0:16], ACT.Copy, ["PR"], ["frow"])
        vred(alpha_sb[:, 0:1], frow[:, 0:8], ALU.add, ["frow"], ["alpha_sb"])
        vts(alpha_sb[:, 0:1], alpha_sb[:, 0:1], 1.0, None, ALU.max, ALU.bypass,
            ["alpha_sb"], ["alpha_sb"])
        vrecip(alpha_sb[:, 1:2], alpha_sb[:, 0:1], ["alpha_sb"], ["alpha_sb"])
        def _amul(e):
            return e.tensor_tensor(alpha_sb[:, 2:3], alpha_sb[:, 1:2],
                                   frow[:, 8:9], op=ALU.mult)
        p.emit("vector", _amul, reads=["alpha_sb", "frow"], writes=["alpha_sb"])
        for c in range(NF):
            p.dma("gq", wqT_d[c, :, :], wqT[c][:, :], reads=[f"wqTp_{c}"],
                  tag="wqT_st")
        p.fence()
        fpctx.close()

    def ffn_phase(tag, dst_final):
        with ExitStack() as qctx:
            qsb = lambda n, sh, dt=F32: qctx.enter_context(
                nc.sbuf_tensor("f" + tag + n, sh, dt))
            gam_rep = load_rep(qctx, "fg" + tag, W["fgamma"])
            bet_rep = load_rep(qctx, "fb" + tag, W["fbeta"])
            xt = [qsb(f"x{b}", [128, D]) for b in range(2)]
            zt = [qsb(f"z{b}", [128, D]) for b in range(2)]
            amax_st = qsb("amax", [128, NT])
            gmax = qsb("gmax", [1, 8])
            scl = qsb("scl", [1, 4])
            inv_s = qsb("invs", [128, 1])
            osc = qsb("osc", [128, 1])
            cbuf = qsb("cbuf", [128, D])
            bbuf = qsb("bbuf", [128, D])
            qb = qsb("qb", [128, D], BF16)
            evQ = qsb("evQ", [128, 1024], BF16)
            qcat = [qctx.enter_context(
                nc.sbuf_tensor(f"f{tag}qcat{b}", [128, NF * TC], BF16))
                for b in range(2)]
            qin = [[qcat[b][:, c * TC:(c + 1) * TC] for c in range(NF)]
                   for b in range(2)]
            mo = qsb("mo", [128, D])
            wqTcat = qctx.enter_context(
                nc.sbuf_tensor(f"f{tag}wqTcat", [128, NF * 1024], BF16))
            wqT = [wqTcat[:, c * 1024:(c + 1) * 1024] for c in range(NF)]
            p.dma("sq", wqTcat[:, :].rearrange("p (c t) -> p c t", c=NF),
                  wqT_d[:, :, :].rearrange("c p t -> p c t"),
                  writes=[f"f{tag}wqT"])
            for i in range(NT):
                b = i % 2
                kx, kz = f"f{tag}x{b}", f"f{tag}z{b}"
                p.dma("sq", xt[b][:, :], xres[i * 128:(i + 1) * 128, :],
                      writes=[kx])
                act(sq_scr[:, :D], xt[b][:, :], ACT.Square, [kx], ["sq_scr"],
                    accum=st_ssq[:, i:i + 1], accw=[f"st_ssq{i % 2}"])
                act(zt[b][:, :], xt[b][:, :], ACT.Copy, [kx], [kz],
                    accum=st_sum[:, i:i + 1], accw=[f"st_sum{i % 2}"])
                rstd_batch(D, slice(i, i + 1))
                act(zt[b][:, :], xt[b][:, :], ACT.Identity,
                    [kx, f"st_rstd{i % 2}", f"st_nmr{i % 2}"], [kz],
                    bias=st_nmr[:, i:i + 1], scale=st_rstd[:, i:i + 1])
                vtt(ALU.mult, zt[b][:, :], zt[b][:, :], gam_rep[:, :],
                    [kz, "rep_fg" + tag], [kz])
                vtt(ALU.add, zt[b][:, :], zt[b][:, :], bet_rep[:, :],
                    [kz, "rep_fb" + tag], [kz])
                vred(amax_st[:, i:i + 1], zt[b][:, :], ALU.max, [kz],
                     ["f" + tag + "amax"], absval=True)
                p.dma("sq", xnb[i * 128:(i + 1) * 128, :], zt[b][:, :],
                      reads=[kz], tag="xnb_st")
            vred(amax_st[:, 0:1], amax_st[:, :], ALU.max, ["f" + tag + "amax"],
                 ["f" + tag + "amax"])
            transp(P3[0:1, 0:128], amax_st[:, 0:1], eye_f[:, :],
                   ["f" + tag + "amax", "eye_f"], ["P3"])
            vred(scl[:, 0:1], P3[:1, 0:128], ALU.max, ["P3"], ["fscl" + tag])
            def _brd(e):
                return e.tensor_copy(gmax[:, :], scl[:, 0:1].broadcast_to((1, 8)))
            p.emit("vector", _brd, reads=["fscl" + tag], writes=["fgmax" + tag])
            p.dma("gq", cc_in[:, :], gmax[:, :], reads=["fgmax" + tag],
                  writes=["fccin" + tag])
            def _cc(e):
                return e.collective_compute(
                    "AllReduce", ALU.max, replica_groups=[list(range(8))],
                    ins=[cc_in.opt()], outs=[cc_out.opt()])
            p.emit("gpsimd", _cc, reads=["fccin" + tag], writes=["fccout" + tag],
                   tag="cc")
            p.dma("sq", gmax[:, :], cc_out[:, :], reads=["fccout" + tag],
                  writes=["fgmax" + tag])
            vts(scl[:, 1:2], gmax[:, 0:1], 1e-8, None, ALU.max, ALU.bypass,
                ["fgmax" + tag], ["fscl" + tag])
            vrecip(scl[:, 2:3], scl[:, 1:2], ["fscl" + tag], ["fscl" + tag])
            vts(scl[:, 2:3], scl[:, 2:3], 127.0, None, ALU.mult, ALU.bypass,
                ["fscl" + tag], ["fscl" + tag])
            vts(scl[:, 1:2], scl[:, 1:2], 1.0 / 127.0, None, ALU.mult,
                ALU.bypass, ["fscl" + tag], ["fscl" + tag])
            def _osc(e):
                return e.tensor_tensor(scl[:, 3:4], scl[:, 1:2],
                                       alpha_sb[:, 2:3], op=ALU.mult)
            p.emit("vector", _osc, reads=["fscl" + tag, "alpha_sb"],
                   writes=["fscl" + tag])
            p.dma("gq", rowb[4:5, 0:4], scl[:, :], reads=["fscl" + tag],
                  writes=["rb4" + tag], slow=True)
            p.dma("sq", inv_s[:, :], rowb[4:5, 2:3].partition_broadcast(128),
                  reads=["rb4" + tag], writes=["finvs" + tag], slow=True)
            p.dma("sq", osc[:, :], rowb[4:5, 3:4].partition_broadcast(128),
                  reads=["rb4" + tag], writes=["fosc" + tag], slow=True)
            def quant_tile(i):
                b = i % 2
                kz = f"f{tag}z{b}"
                j, tb = i // 4, i % 4
                bj = j % 2
                p.dma("sq", zt[b][:, :], xnb[i * 128:(i + 1) * 128, :],
                      writes=[kz])
                act(cbuf[:, :], zt[b][:, :], ACT.Identity,
                    [kz, "finvs" + tag], ["fcb"], scale=inv_s[:, 0:1])
                vts(cbuf[:, :], cbuf[:, :], -127.0, 127.0, ALU.max, ALU.min,
                    ["fcb"], ["fcb"])
                vcopy(bbuf[:, :], cbuf[:, :], ["fcb"], ["fbb"])
                for k in [128.0, 64.0, 32.0, 16.0, 8.0, 4.0, 2.0, 1.0]:
                    vts(sq_scr[:, :D], bbuf[:, :], k - 128.5, None, ALU.is_ge,
                        ALU.bypass, ["fbb"], ["sq_scr"])
                    vstt(bbuf[:, :], sq_scr[:, :D], -k, bbuf[:, :], ALU.mult,
                         ALU.add, ["sq_scr", "fbb"], ["fbb"])
                vtt(ALU.subtract, cbuf[:, :], cbuf[:, :], bbuf[:, :],
                    ["fcb", "fbb"], ["fcb"])
                vts(qb[:, :], cbuf[:, :], -128.0, None, ALU.add, ALU.bypass,
                    ["fcb"], ["fqb"])
                p2b = P2[:, :512].bitcast(BF16)
                for c in range(NF):
                    transp(p2b[:, c * 128:(c + 1) * 128],
                           qb[:, c * 128:(c + 1) * 128], eye_b[:, :],
                           ["fqb", "eye_b"], ["P2"])
                act(qcat[bj][:, :].rearrange("p (c t) -> p c t", c=NF)
                    [:, :, tb * 128:tb * 128 + 128]
                    if False else
                    qcat[bj][:, tb * 128:tb * 128 + 128]
                    .rearrange("p (o t) -> p o t", o=1)
                    .broadcast_to((128, 1, 128)) if False else
                    qcat[bj][:, :]
                    .rearrange("p (c t) -> p c t", c=NF)[:, :, tb * 128:(tb + 1) * 128],
                    p2b[:, :1024].rearrange("p (c t) -> p c t", c=NF),
                    ACT.Copy, ["P2"], [f"f{tag}qin{bj}"])

            def mm_chunk(j):
                bj = j % 2
                for tb in range(4):
                    i = j * 4 + tb
                    b = i % 2
                    kx = f"f{tag}x{b}"
                    p.dma("sq", xt[b][:, :], xres[i * 128:(i + 1) * 128, :],
                          writes=[kx])
                    ps, psk = (P0, "P0") if i % 2 == 0 else (P1, "P1")
                    t0 = tb * 128
                    for half in range(2):
                        h0 = half * 512
                        for c in range(NF):
                            mm(ps[:, h0:h0 + 512], qin[bj][c][:, t0:t0 + 128],
                               wqT[c][:, h0:h0 + 512], start=(c == 0),
                               stop=(c == NF - 1),
                               reads=[f"f{tag}qin{bj}", f"f{tag}wqT"],
                               writes=[psk])
                    act(mo[:, :], ps[:, :1024], ACT.Identity,
                        [psk, "fosc" + tag], ["fmo"], scale=osc[:, 0:1])
                    vtt(ALU.add, xt[b][:, :], xt[b][:, :], mo[:, :],
                        [kx, "fmo"], [kx])
                    p.dma("sq", dst_final[i * 128:(i + 1) * 128, :], xt[b][:, :],
                          reads=[kx], tag="xout")

            for j in range(NCH):
                for tb in range(4):
                    quant_tile(j * 4 + tb)
                if j > 0:
                    mm_chunk(j - 1)
            mm_chunk(NCH - 1)
            p.fence()

    if "B" in phases:
        ffn_phase("B", xres if ("C" in phases or "D" in phases) else out_d)

    if "C" in phases:
        cctx = ctx.enter_context(ExitStack())
        csb = lambda n, sh, dt=F32: cctx.enter_context(nc.sbuf_tensor(n, sh, dt))
        dtT = csb("dtT", [64, DIN], BF16)
        xpwT = [csb(f"xpwT_{t}", [128, 96], BF16) for t in range(NCC)]
        A_sb = csb("A_sb", [128, NCC * DST])
        conv_sb = csb("conv_sb", [128, NCC * 4])
        cvb_sb = csb("cvb_sb", [128, NCC])
        dtb_sb = csb("dtb_sb", [128, NCC])
        Dv_sb = csb("Dv_sb", [128, NCC])
        cpctx = ExitStack()
        csb = lambda n, sh, dt=F32: cpctx.enter_context(nc.sbuf_tensor(n, sh, dt))
        cstage = csb("cstage", [128, 1024])
        cstb = csb("cstb", [128, 1024], BF16)
        evC = csb("evC", [128, 1024], BF16)
        for o in range(2 * DIN // 128):
            p.dma("sq", cstage[:, :], W["in_proj_w"][o * 128:(o + 1) * 128, :],
                  writes=["cstage"])
            vcopy(cstb[:, :], cstage[:, :], ["cstage"], ["cstb"])
            p2b = P2[:, :512].bitcast(BF16)
            for c in range(NF):
                transp(p2b[:, c * 128:(c + 1) * 128],
                       cstb[:, c * 128:(c + 1) * 128], eye_b[:, :],
                       ["cstb", "eye_b"], ["P2"])
            act(evC[:, :1024], p2b[:, :1024], ACT.Copy, ["P2"], ["evC"])
            for c in range(NF):
                p.dma("gq", ipT_d[c, :, o * 128:(o + 1) * 128],
                      evC[:, c * 128:(c + 1) * 128], reads=["evC"], tag="ipT_st")
        for o in range(D // 128):
            p.dma("sq", cstage[:, :1024],
                  W["out_proj_w"][o * 128:(o + 1) * 128, 0:1024],
                  writes=["cstage"])
            p.dma("sq", sq_scr[:, :1024],
                  W["out_proj_w"][o * 128:(o + 1) * 128, 1024:2048],
                  writes=["sq_scr"])
            vcopy(cstb[:, :], cstage[:, :1024], ["cstage"], ["cstb"])
            p2b = P2[:, :512].bitcast(BF16)
            for c in range(8):
                transp(p2b[:, c * 128:(c + 1) * 128],
                       cstb[:, c * 128:(c + 1) * 128], eye_b[:, :],
                       ["cstb", "eye_b"], ["P2"])
            act(evC[:, :1024], p2b[:, :1024], ACT.Copy, ["P2"], ["evC"])
            for c in range(8):
                p.dma("gq", opT_d[c, :, o * 128:(o + 1) * 128],
                      evC[:, c * 128:(c + 1) * 128], reads=["evC"], tag="opT_st")
            vcopy(cstb[:, :], sq_scr[:, :1024], ["sq_scr"], ["cstb"])
            for c in range(8):
                transp(p2b[:, c * 128:(c + 1) * 128],
                       cstb[:, c * 128:(c + 1) * 128], eye_b[:, :],
                       ["cstb", "eye_b"], ["P2"])
            act(evC[:, :1024], p2b[:, :1024], ACT.Copy, ["P2"], ["evC"])
            for c in range(8):
                p.dma("gq", opT_d[8 + c, :, o * 128:(o + 1) * 128],
                      evC[:, c * 128:(c + 1) * 128], reads=["evC"], tag="opT_st")
        for t in range(NCC):
            p.dma("sq", cstage[:, :64], W["dt_proj_w"][t * 128:(t + 1) * 128, :],
                  writes=["cstage"])
            vcopy(cstb[:, :64], cstage[:, :64], ["cstage"], ["cstb"])
            transp(P2[:, 0:64].bitcast(BF16)[0:64, :],
                   cstb[:, :64], eye_b[:, :], ["cstb", "eye_b"], ["P2"])
            vcopy(dtT[:, t * 128:(t + 1) * 128],
                  P2[:, 0:64].bitcast(BF16)[0:64, :], ["P2"], ["dtT"])
        xpw_sb = csb("xpw_sb", [96, DIN])
        xpw_sbb = csb("xpw_sbb", [96, DIN], BF16)
        p.dma("sq", xpw_sb[:, :], W["x_proj_w"][:, :], writes=["xpw_sb"])
        vcopy(xpw_sbb[:, :], xpw_sb[:, :], ["xpw_sb"], ["xpw_sbb"])
        for t in range(NCC):
            transp(P2[:, 0:48].bitcast(BF16)[:, 0:96],
                   xpw_sbb[:, t * 128:(t + 1) * 128], eye_b[:96, :96],
                   ["xpw_sbb", "eye_b"], ["P2"])
            vcopy(xpwT[t][:, :], P2[:, 0:48].bitcast(BF16)[:, 0:96], ["P2"],
                  [f"xpwT_{t}"])
        for t in range(NCC):
            p.dma("sq", cstage[:, :DST], W["A_log"][t * 128:(t + 1) * 128, :],
                  writes=["cstage"])
            act(A_sb[:, t * DST:(t + 1) * DST], cstage[:, :DST], ACT.Exp,
                ["cstage"], ["A_sb"])
        vts(A_sb[:, :], A_sb[:, :], -1.0, None, ALU.mult, ALU.bypass,
            ["A_sb"], ["A_sb"])
        for t in range(NCC):
            p.dma("sq", conv_sb[:, t * 4:(t + 1) * 4],
                  W["conv_w"].rearrange("c o k -> (c o) k")[t * 128:(t + 1) * 128, :],
                  writes=["conv_sb"])
        for t in range(NCC):
            p.dma("sq", cvb_sb[:, t:t + 1],
                  W["conv_b"][t * 128:(t + 1) * 128].rearrange("(p o) -> p o", o=1),
                  writes=["cvb_sb"], slow=True)
            p.dma("sq", dtb_sb[:, t:t + 1],
                  W["dt_proj_b"][t * 128:(t + 1) * 128].rearrange("(p o) -> p o", o=1),
                  writes=["dtb_sb"], slow=True)
            p.dma("sq", Dv_sb[:, t:t + 1],
                  W["Dv"][t * 128:(t + 1) * 128].rearrange("(p o) -> p o", o=1),
                  writes=["Dv_sb"], slow=True)
        p.fence()
        cpctx.close()
        csb = lambda n, sh, dt=F32: cctx.enter_context(nc.sbuf_tensor(n, sh, dt))

        with ExitStack() as wctx:
            nw_rep = load_rep(wctx, "nw", W["norm_w"])
            xt = [wctx.enter_context(nc.sbuf_tensor(f"c0x{b}", [128, D], F32))
                  for b in range(2)]
            zt = [wctx.enter_context(nc.sbuf_tensor(f"c0z{b}", [128, D], F32))
                  for b in range(2)]
            xrb = wctx.enter_context(nc.sbuf_tensor("c0xrb", [128, D], BF16))
            evC0 = wctx.enter_context(nc.sbuf_tensor("c0ev", [128, 1024], BF16))
            for i in range(NT):
                b = i % 2
                kx, kz = f"c0x{b}", f"c0z{b}"
                p.dma("sq", xt[b][:, :], xres[i * 128:(i + 1) * 128, :],
                      writes=[kx])
                act(sq_scr[:, :D], xt[b][:, :], ACT.Square, [kx], ["sq_scr"],
                    accum=st_ssq[:, i:i + 1], accw=[f"st_ssq{i % 2}"])
                act(zt[b][:, :], xt[b][:, :], ACT.Copy, [kx], [kz],
                    accum=st_sum[:, i:i + 1], accw=[f"st_sum{i % 2}"])
                rstd_batch(D, slice(i, i + 1))
                act(zt[b][:, :], xt[b][:, :], ACT.Identity,
                    [kx, f"st_rstd{i % 2}", f"st_nmr{i % 2}"], [kz],
                    bias=st_nmr[:, i:i + 1], scale=st_rstd[:, i:i + 1])
                vtt(ALU.add, xt[b][:, :], xt[b][:, :], zt[b][:, :], [kx, kz],
                    [kx])
                p.dma("sq", xres[i * 128:(i + 1) * 128, :], xt[b][:, :],
                      reads=[kx])
                act(sq_scr[:, :D], zt[b][:, :], ACT.Square, [kz], ["sq_scr"],
                    accum=st_ssq[:, i:i + 1], accw=[f"st_ssq{i % 2}"])
                vts(st_tmp[:, i:i + 1], st_ssq[:, i:i + 1], 1.0 / D, None,
                    ALU.mult, ALU.bypass, [f"st_ssq{i % 2}"], [f"st_tmp{i % 2}"])
                act(st_rstd[:, i:i + 1], st_tmp[:, i:i + 1], ACT.Ln,
                    [f"st_tmp{i % 2}", "eps_col"], [f"st_rstd{i % 2}"],
                    bias=eps_col[:, 0:1])
                act(st_rstd[:, i:i + 1], st_rstd[:, i:i + 1], ACT.Exp,
                    [f"st_rstd{i % 2}"], [f"st_rstd{i % 2}"], scale=-0.5)
                act(zt[b][:, :], zt[b][:, :], ACT.Identity,
                    [kz, f"st_rstd{i % 2}"], [kz], scale=st_rstd[:, i:i + 1])
                vtt(ALU.mult, xrb[:, :], zt[b][:, :], nw_rep[:, :],
                    [kz, "rep_nw"], ["c0xrb"])
                transpose_tm_tile(xrb[:, :], "c0xrb", qfm, i, evC0, "c0ev")
            p.fence()

        TCp = TC + 3
        xp = csb("xp", [128, NCC * TCp], BF16)
        u_cm = csb("u_cm", [128, NCC * TC])
        zgs = csb("zgs", [128, NCC * TC], BF16)
        bc_rep = csb("bc_rep", [128, 2 * DST * TC], BF16)
        b_half = csb("b_half", [128, 4 * TC], BF16)
        h_buf = csb("h_buf", [128, DST * TC], BF16)
        h_init = csb("h_init", [128, NCC * DST])
        ybf = csb("ybf", [128, NCC * TC], BF16)
        delta_t = [csb(f"delta_t{q}", [128, TC]) for q in range(2)]
        dub = [csb(f"dub{q}", [128, TC], BF16) for q in range(2)]
        du3 = [csb(f"du3_{q}", [128, TC]) for q in range(3)]
        dA_q = csb("dA_q", [128, 4 * TC], BF16)
        g8 = csb("g8", [128, 8 * TC])
        xdbl_b = csb("xdbl_b", [64, TC], BF16)
        xincat = csb("xincat", [128, NF * TC], BF16)
        xin = [[xincat[:, c * TC:(c + 1) * TC] for c in range(NF)]]
        ipw = [csb(f"ipw{b}", [128, 8 * 128], BF16) for b in range(1)]
        opw = [csb(f"opw{b}", [128, 512], BF16) for b in range(1)]
        xt2 = [csb(f"c1x{b}", [128, D]) for b in range(4)]
        vmemset(h_init[:, :], 0.0, ["h_init"])
        vmemset(xp[:, :], 0.0, ["xp"])
        dstC = xres if "D" in phases else out_d

        def gtt(out_ap, a_ap, b_ap, reads, writes):
            def fn(e, o=out_ap, a=a_ap, b=b_ap):
                return e.tensor_tensor(o, a, b, op=ALU.add)
            p.emit("gpsimd", fn, reads=reads, writes=writes)

        def emit_xz_group(j, m):
            p.dma("sq", ipw[0][:, :].rearrange("p (c t) -> p c t", c=NF),
                  ipT_d[:, :, m * 128:(m + 1) * 128].rearrange("c p t -> p c t"),
                  writes=["ipw0"], tag="ipwld")
            for c in range(NF):
                mm(P2[:, :TC], ipw[0][:, c * 128:(c + 1) * 128],
                   xin[0][c][:, :], start=(c == 0), stop=(c == NF - 1),
                   reads=["ipw0", "cxin0"], writes=["P2"])
            if m < NCC:
                act(xp[:, m * TCp + 3:(m + 1) * TCp], P2[:, :TC],
                    ACT.Copy, ["P2"], ["xp"])
            else:
                cb = m - NCC
                act(zgs[:, cb * TC:(cb + 1) * TC], P2[:, :TC],
                    ACT.Silu, ["P2"], ["zgs"])

        def emit_xin(j):
            p.dma("sq", xincat[:, :].rearrange("p (c t) -> p c t", c=NF),
                  qfm[:, :, j * TC:(j + 1) * TC].rearrange("c p t -> p c t"),
                  writes=["cxin0"], tag="xnld")

        def emit_pre(j):
            for cb in range(NCC):
                x0 = cb * TCp
                q = cb % 3
                dq, kq = du3[q], f"du3_{q}"
                def _c0(e, x0=x0, cb=cb, dq=dq):
                    return e.tensor_scalar(dq[:, :], xp[:, x0:x0 + TC],
                                           conv_sb[:, 4 * cb:4 * cb + 1], None,
                                           op0=ALU.mult)
                p.emit("vector", _c0, reads=["xp", "conv_sb"], writes=[kq])
                for jj in range(1, 4):
                    vstt(dq[:, :], xp[:, x0 + jj:x0 + jj + TC],
                         conv_sb[:, 4 * cb + jj:4 * cb + jj + 1], dq[:, :],
                         ALU.mult, ALU.add, ["xp", "conv_sb", kq], [kq])
                act(u_cm[:, cb * TC:(cb + 1) * TC], dq[:, :], ACT.Silu,
                    [kq, "cvb_sb"], ["u_cm"], bias=cvb_sb[:, cb:cb + 1])
            for cb in range(NCC):
                def _tl(e, cb=cb):
                    return e.tensor_copy(xp[:, cb * TCp:cb * TCp + 3],
                                         xp[:, cb * TCp + TC:cb * TCp + TC + 3])
                p.emit("vector", _tl, reads=["xp"], writes=["xp"])
            for cb in range(NCC):
                vcopy(b_half[:, 0:TC], u_cm[:, cb * TC:(cb + 1) * TC], ["u_cm"],
                      ["b_half"])
                mm(P2[0:96, :TC], xpwT[cb][:, :], b_half[:, 0:TC],
                   start=(cb == 0), stop=(cb == NCC - 1),
                   reads=[f"xpwT_{cb}", "b_half"], writes=["P2"])
            xdbl = g8[0:96, 7 * TC:8 * TC]
            act(xdbl[:, :], P2[0:96, :TC], ACT.Copy, ["P2"], ["g8"])
            vcopy(xdbl_b[:, :], xdbl[0:64, :], ["g8"], ["xdbl_b"])
            p.dma("gq", bc_d[:, :], xdbl[64:96, :], reads=["g8"],
                  writes=["bcd"])
            for n in range(2 * DST):
                p.dma("sq", delta_t[n % 2][:, :],
                      bc_d[n:n + 1, :].partition_broadcast(128),
                      reads=["bcd"], writes=[f"delta_t{n % 2}"])
                vcopy(bc_rep[:, n * TC:(n + 1) * TC], delta_t[n % 2][:, :],
                      [f"delta_t{n % 2}"], ["bc_rep"])

        def delta_chain(cb):
            qd = cb % 2
            q3 = cb % 3
            dl, kdl = delta_t[qd], f"delta_t{qd}"
            du, kdu = du3[q3], f"du3_{q3}"
            mm(P3[:, :TC], dtT[:, cb * 128:(cb + 1) * 128], xdbl_b[:, :],
               start=True, stop=True, reads=["dtT", "xdbl_b"], writes=["P3"])
            act(dl[:, :], P3[:, :TC], ACT.Abs, ["P3", "dtb_sb"], [kdl],
                bias=dtb_sb[:, cb:cb + 1])
            act(du[:, :], dl[:, :], ACT.Exp, [kdl], [kdu], scale=-1.0)
            act(dl[:, :], du[:, :], ACT.Ln, [kdu], [kdl], bias=1.0)
            act(du[:, :], P3[:, :TC], ACT.Relu, ["P3", "dtb_sb"], [kdu],
                bias=dtb_sb[:, cb:cb + 1])
            vtt(ALU.add, dl[:, :], dl[:, :], du[:, :], [kdl, kdu], [kdl])
            vtt(ALU.mult, du[:, :], dl[:, :],
                u_cm[:, cb * TC:(cb + 1) * TC], [kdl, "u_cm"], [kdu])
            vcopy(dub[cb % 2][:, :], du[:, :], [kdu], [f"dub{cb % 2}"])

        def emit_ypost(cb):
            q3 = cb % 3
            du, kdu = du3[q3], f"du3_{q3}"
            vstt(du[:, :], u_cm[:, cb * TC:(cb + 1) * TC],
                 Dv_sb[:, cb:cb + 1], du[:, :], ALU.mult, ALU.add,
                 ["u_cm", "Dv_sb", kdu], [kdu])
            vtt(ALU.mult, ybf[:, cb * TC:(cb + 1) * TC], du[:, :],
                zgs[:, cb * TC:(cb + 1) * TC], [kdu, "zgs"], ["ybf"])

        def emit_scan(j, interleave_next):
            delta_chain(0)
            for cb in range(NCC):
                qd = cb % 2
                q3 = cb % 3
                dl, kdl = delta_t[qd], f"delta_t{qd}"
                du, kdu = du3[q3], f"du3_{q3}"
                if cb + 1 < NCC:
                    delta_chain(cb + 1)
                if interleave_next:
                    if cb == 0:
                        emit_xin(j + 1)
                    emit_xz_group(j + 1, 2 * cb)
                    emit_xz_group(j + 1, 2 * cb + 1)
                for n in range(DST):
                    bsl = n % 4
                    vtt(ALU.mult, b_half[:, bsl * TC:(bsl + 1) * TC],
                        dub[cb % 2][:, :],
                        bc_rep[:, n * TC:(n + 1) * TC],
                        [f"dub{cb % 2}", "bc_rep"], [f"bh{bsl}"])
                    sl = n % 4
                    act(dA_q[:, sl * TC:(sl + 1) * TC], dl[:, :], ACT.Exp,
                        [kdl, "A_sb"], [f"dA{sl}"],
                        scale=A_sb[:, cb * DST + n:cb * DST + n + 1])
                    vscan(h_buf[:, n * TC:(n + 1) * TC],
                          dA_q[:, sl * TC:(sl + 1) * TC],
                          b_half[:, bsl * TC:(bsl + 1) * TC],
                          h_init[:, cb * DST + n:cb * DST + n + 1],
                          [f"dA{sl}", f"bh{bsl}", "h_init"], ["h_buf"])
                if cb > 0:
                    emit_ypost(cb - 1)
                def _sv(e, cb=cb):
                    return e.tensor_copy(
                        h_init[:, cb * DST:(cb + 1) * DST],
                        h_buf[:, TC - 1::TC])
                p.emit("vector", _sv, reads=["h_buf"], writes=["h_init"])
                vtt(ALU.mult, h_buf[:, :], h_buf[:, :],
                    bc_rep[:, DST * TC:2 * DST * TC], ["h_buf", "bc_rep"],
                    ["h_buf"])
                vtt(ALU.add, g8[:, :], h_buf[:, 0:8 * TC],
                    h_buf[:, 8 * TC:16 * TC], ["h_buf"], ["g8"])
                for lvl in (4, 2):
                    gtt(g8[:, 0:lvl * TC], g8[:, 0:lvl * TC],
                        g8[:, lvl * TC:2 * lvl * TC], ["g8"], ["g8"])
                gtt(du[:, :], g8[:, 0:TC], g8[:, TC:2 * TC], ["g8"], [kdu])
            emit_ypost(NCC - 1)

        def emit_out(j):
            for tb in range(4):
                i = j * 4 + tb
                p.dma("sq", xt2[tb][:, :], xres[i * 128:(i + 1) * 128, :],
                      writes=[f"c1x{tb}"])
            for half in range(2):
                h0 = half * 512
                for c in range(NCC):
                    bm = 0
                    p.dma("sq", opw[bm][:, :], opT_d[c, :, h0:h0 + 512],
                          writes=[f"opw{bm}"], tag="opwld")
                    for tb in range(4):
                        ps, psk = (P0, "P0") if tb < 2 else (P1, "P1")
                        o0 = (tb % 2) * 512
                        mm(ps[:, o0:o0 + 512],
                           ybf[:, c * TC + tb * 128:c * TC + tb * 128 + 128],
                           opw[bm][:, :], start=(c == 0), stop=(c == NCC - 1),
                           reads=[f"opw{bm}", "ybf"], writes=[psk])
                for tb in range(4):
                    ps, psk = (P0, "P0") if tb < 2 else (P1, "P1")
                    o0 = (tb % 2) * 512
                    vtt(ALU.add, xt2[tb][:, h0:h0 + 512],
                        xt2[tb][:, h0:h0 + 512], ps[:, o0:o0 + 512],
                        [f"c1x{tb}", psk], [f"c1x{tb}"])
                    if half == 1:
                        i = j * 4 + tb
                        p.dma("sq", dstC[i * 128:(i + 1) * 128, :],
                              xt2[tb][:, :], reads=[f"c1x{tb}"], tag="xoutC")

        emit_xin(0)
        for m in range(2 * DIN // 128):
            emit_xz_group(0, m)
        emit_pre(0)
        for j in range(NCH):
            emit_scan(j, interleave_next=(j + 1 < NCH))
            if j + 1 < NCH:
                emit_pre(j + 1)
            emit_out(j)
        p.fence()
        cctx.close()

    if "D" in phases:
        ffn_phase("D", out_d)
    if fctx is not None:
        fctx.close()


    p.fence()
    if cut is not None:
        p.truncate_at_fence(cut)
    block = ctx.enter_context(nc.Block())
    p.finalize(block)
    ctx.close()
    return nc, taps


_BUILD_CACHE = {}


def _get_built(phases, debug):
    key = (phases, debug)
    if key not in _BUILD_CACHE:
        _BUILD_CACHE[key] = build(phases, debug)
    return _BUILD_CACHE[key]


def make_in_maps(inputs):
    pref = {
        "x": "L0_attn_a_x",
        "Wq": "L0_attn_a_Wq", "bq": "L0_attn_a_bq", "gq": "L0_attn_a_gq",
        "betaq": "L0_attn_a_betaq",
        "Wk": "L0_attn_a_Wk", "bk": "L0_attn_a_bk", "gk": "L0_attn_a_gk",
        "betak": "L0_attn_a_betak",
        "Wv": "L0_attn_a_Wv", "bv": "L0_attn_a_bv", "gv": "L0_attn_a_gv",
        "betav": "L0_attn_a_betav",
        "Wfq": "L0_attn_a_Wfq", "bfq": "L0_attn_a_bfq",
        "Wfk": "L0_attn_a_Wfk", "bfk": "L0_attn_a_bfk",
        "ga": "L0_attn_a_ga", "ba": "L0_attn_a_ba",
        "W1": "L0_attn_a_W1", "b1": "L0_attn_a_b1",
        "W2": "L0_attn_a_W2", "b2": "L0_attn_a_b2",
        "norm_w": "L0_attn_b_norm_w", "in_proj_w": "L0_attn_b_in_proj_w",
        "conv_w": "L0_attn_b_conv_w", "conv_b": "L0_attn_b_conv_b",
        "x_proj_w": "L0_attn_b_x_proj_w", "dt_proj_w": "L0_attn_b_dt_proj_w",
        "dt_proj_b": "L0_attn_b_dt_proj_b", "A_log": "L0_attn_b_A_log",
        "Dv": "L0_attn_b_D", "out_proj_w": "L0_attn_b_out_proj_w",
        "fw": "L0_ffn_weight", "fgamma": "L0_ffn_gamma", "fbeta": "L0_ffn_beta",
    }
    shared = {}
    for k, src in pref.items():
        if k == "x":
            continue
        shared[k] = np.ascontiguousarray(np.asarray(inputs[src], np.float32))
    shared["c_eye_f"] = np.eye(128, dtype=np.float32)
    shared["c_eye_b"] = np.eye(128, dtype=ml_dtypes.bfloat16)
    shared["c_ones_col_b"] = np.ones((128, 1), dtype=ml_dtypes.bfloat16)
    shared["c_ones_row_b"] = np.ones((1, 128), dtype=ml_dtypes.bfloat16)
    x = np.ascontiguousarray(np.asarray(inputs["L0_attn_a_x"], np.float32))
    return [dict(shared, x=np.ascontiguousarray(x[b])) for b in range(8)]


def kernel(**inputs):
    nc, _ = _get_built("ABCD", False)
    in_maps = make_in_maps(inputs)
    res = run_bass_kernel_spmd(nc, in_maps, core_ids=list(range(8)))
    return np.stack([np.asarray(res.results[b]["out"], np.float32)
                     for b in range(8)], axis=0)
